# version 25
# baseline (speedup 1.0000x reference)
import contextlib
import numpy as np
import ml_dtypes
import concourse.bass as bass
import concourse.mybir as mybir
from concourse.bass_utils import run_bass_kernel_spmd

F32 = mybir.dt.float32
BF16 = mybir.dt.bfloat16
AF = mybir.ActivationFunctionType
ALU = mybir.AluOpType
AX = mybir.AxisListType
bf16 = ml_dtypes.bfloat16

T = 2048
TS = 512
NS = T // TS
NT = TS // 128
D = 1024
H = 4096
R = 1024
LN32 = float(np.log(np.sqrt(1024.0)))
EPS = 1e-5


class Ctr:
    def __init__(self, nc, stack, name, eng):
        self.sem = stack.enter_context(nc.semaphore(name))
        self.n = 0
        self.eng = eng

    def inc(self, instr, k=1):
        instr.then_inc(self.sem, k)
        self.n += k
        return self.n


def build_nc():
    nc = bass.Bass("TRN2", target_bir_lowering=False)
    g = nc.gpsimd
    sy = nc.sync
    t = nc.tensor
    v = nc.vector
    sc = nc.scalar

    xT = nc.dram_tensor("xT", [D, T], F32, kind="ExternalInput").ap()
    projT = nc.dram_tensor("projT", [D, R], F32, kind="ExternalInput").ap()
    iwT = nc.dram_tensor("iwT", [D, R], F32, kind="ExternalInput").ap()
    xis = nc.dram_tensor("xis", [R], F32, kind="ExternalInput").ap()
    w1p = nc.dram_tensor("w1p", [8, 128, 8, 512], F32, kind="ExternalInput").ap()
    w2p = nc.dram_tensor("w2p", [8, 128, 8, 512], F32, kind="ExternalInput").ap()
    h1p = nc.dram_tensor("h1p", [8, 128, 32, 512], F32, kind="ExternalInput").ap()
    h2p = nc.dram_tensor("h2p", [8, 128, 32, 512], F32, kind="ExternalInput").ap()
    w3p = nc.dram_tensor("w3p", [2, 128, 32, 512], F32, kind="ExternalInput").ap()
    w1s = nc.dram_tensor("w1s", [H], F32, kind="ExternalInput").ap()
    w2s = nc.dram_tensor("w2s", [H], F32, kind="ExternalInput").ap()
    h1s = nc.dram_tensor("h1s", [H], F32, kind="ExternalInput").ap()
    h2s = nc.dram_tensor("h2s", [H], F32, kind="ExternalInput").ap()
    w3s = nc.dram_tensor("w3s", [D], F32, kind="ExternalInput").ap()
    y = nc.dram_tensor("y", [D, T], F32, kind="ExternalOutput").ap()

    identf_h = nc.inline_tensor(np.eye(128).astype(np.float32), "ident_f")

    with contextlib.ExitStack() as stack:
        E = stack.enter_context

        def sb(name, shape, dtype):
            return E(nc.sbuf_tensor(name, shape, dtype))

        projT_sb = sb("projT_sb", [128, 9, R], BF16)
        Wt_sb = sb("Wt_sb", [128, 8, R], BF16)
        xt_sweep = sb("xt_sweep", [128, 8, TS], BF16)
        xt_r = sb("xt_r", [128, 2, 512], F32)
        rows = sb("rows", [128, 4, TS], F32)
        ln_tmp = sb("ln_tmp", [128, TS], F32)
        xn_fm = sb("xn_fm", [128, 9, TS], BF16)
        feats = sb("feats", [128, 8, TS], BF16)
        s_sb = sb("s_sb", [128, 8, TS], F32)
        h_sb = sb("h_sb", [128, 32, TS], BF16)
        g1_sb = sb("g1_sb", [128, 32, TS], BF16)
        wring = sb("wring", [128, 5, 8, TS], BF16)
        t1 = sb("t1", [128, 4, TS], BF16)
        t2 = sb("t2", [128, 4, TS], BF16)
        identf = sb("identf", [128, 128], F32)
        vec_sb = sb("vec_sb", [32, 6, 128], F32)
        ab_col = sb("ab_col", [128, 8], F32)
        bias_f = sb("bias_f", [128, 8], F32)
        bias_w = sb("bias_w", [128, 8], F32)
        w1s_c = sb("w1s_c", [128, 32], F32)
        w2s_c = sb("w2s_c", [128, 32], F32)
        h1s_c = sb("h1s_c", [128, 32], F32)
        h2s_c = sb("h2s_c", [128, 32], F32)
        w3s_c = sb("w3s_c", [128, 8], F32)
        stat = sb("stat", [128, 16], F32)
        ones_nh = sb("ones_nh", [128, 1], BF16)
        ones_b = sb("ones_b", [128, 1], BF16)
        ones_c = sb("ones_c", [128, 1], F32)
        ones_row = sb("ones_row", [1, 128], F32)
        c_eps = sb("c_eps", [128, 1], F32)
        c_nhalf = sb("c_nhalf", [128, 1], F32)

        iw_view = g1_sb.ap().rearrange("p a b -> p (a b)")
        iw_c = [iw_view[:, j * R:(j + 1) * R] for j in range(9)]

        p = [E(nc.psum_tensor(f"pb{i}", [128, 512], F32)) for i in range(8)]

        cT = Ctr(nc, stack, "cT", t)
        cV = Ctr(nc, stack, "cV", v)
        cS = Ctr(nc, stack, "cS", sc)
        d_xts = Ctr(nc, stack, "d_xts", g)
        d_xt = [Ctr(nc, stack, f"d_xt{i}", sy) for i in range(2)]
        d_y = [Ctr(nc, stack, f"d_y{i}", sy) for i in range(8)]
        d_wr = [Ctr(nc, stack, f"d_wr{i}", g) for i in range(5)]
        d_ident = Ctr(nc, stack, "d_ident", sy)
        d_misc = Ctr(nc, stack, "d_misc", sy)
        d_vecj = [Ctr(nc, stack, f"d_vecj{i}", sy) for i in range(6)]
        cast_sem = {}

        waited = {}

        def need(eng, ctr, val, allow_self=False):
            if ctr is None or val <= 0:
                return
            if ctr.eng is eng and not allow_self:
                return
            key = (id(eng), id(ctr.sem))
            if waited.get(key, 0) >= val:
                return
            eng.wait_ge(ctr.sem, val)
            waited[key] = val

        bank_guard = {}

        def guard_bank(i):
            gd = bank_guard.get(i)
            if gd is not None:
                need(t, gd[0], gd[1])

        def V(instr_fn):
            need(v, cV, cV.n, allow_self=True)
            return cV.inc(instr_fn())

        def S(instr_fn):
            need(sc, cS, cS.n, allow_self=True)
            return cS.inc(instr_fn())

        cast_sem["projT"] = Ctr(nc, stack, "cs_projT", g)
        g.dma_start(projT_sb[:, 0:8, :], projT.rearrange("(j p) r -> p j r", p=128))\
            .then_inc(cast_sem["projT"].sem, 16)
        cast_sem["projT"].n = 16
        cast_sem["iwT"] = Ctr(nc, stack, "cs_iwT", g)
        g.dma_start(iw_view[:, 0:8 * R].rearrange("p (j r) -> p j r", j=8),
                    iwT.rearrange("(j p) r -> p j r", p=128)).then_inc(cast_sem["iwT"].sem, 16)
        cast_sem["iwT"].n = 16

        xts_state = {"guard": 0}

        def emit_xts_loads(s_):
            need(g, cV, xts_state["guard"])
            for j in range(8):
                g.dma_start(xt_sweep[:, j, :],
                            xT[j * 128:(j + 1) * 128, s_ * TS:(s_ + 1) * TS])\
                    .then_inc(d_xts.sem, 16)
                d_xts.n += 16

        emit_xts_loads(0)

        srcs = {"w1": w1p, "w2": w2p, "h1": h1p, "h2": h2p, "w3": w3p}

        sy.dma_start(identf[:, :], identf_h.ap()).then_inc(d_ident.sem, 16)
        d_ident.n += 16

        ms_p8 = V(lambda: v.memset(projT_sb[:, 8, :], 0.0))
        V(lambda: v.memset(iw_c[8], 0.0))
        V(lambda: v.memset(xn_fm[:, 8, :], 0.0))
        V(lambda: v.memset(ones_nh[:, :], -0.5))
        V(lambda: v.memset(c_eps[:, :], EPS))
        V(lambda: v.memset(c_nhalf[:, :], -0.5))
        V(lambda: v.memset(ones_b[:, :], 1.0))
        V(lambda: v.memset(ones_c[:, :], 1.0))
        V(lambda: v.memset(ones_row[:, :], 1.0))
        cast_sem["xisrow"] = Ctr(nc, stack, "cs_xisrow", g)
        g.wait_ge(cV.sem, ms_p8)
        g.dma_start(projT_sb[0:1, 8, :], xis[None, :]).then_inc(cast_sem["xisrow"].sem, 16)
        cast_sem["xisrow"].n = 16

        vec_jobs = [
            (xis, 8, ab_col, 8), (w1s, 32, w1s_c, 32), (w2s, 32, w2s_c, 32),
            (h1s, 32, h1s_c, 32), (h2s, 32, h2s_c, 32), (w3s, 8, w3s_c, 8),
        ]
        for vi, (vsrc, nparts, dst, ncols) in enumerate(vec_jobs):
            sy.dma_start(vec_sb[0:nparts, vi, :], vsrc.rearrange("(c p) -> c p", p=128))\
                .then_inc(d_vecj[vi].sem, 16)
            d_vecj[vi].n += 16
        for vi, (vsrc, nparts, dst, ncols) in enumerate(vec_jobs):
            need(t, d_vecj[vi], 16)
            need(t, d_ident, 16)
            bk = 4 + (vi % 4)
            guard_bank(bk)
            vt = cT.inc(t.transpose(p[bk][0:128, 0:ncols], vec_sb[0:nparts, vi, :],
                                    identf[0:nparts, 0:nparts]))
            need(v, cT, vt)
            ev = V(lambda dst=dst, ncols=ncols, bk=bk: v.tensor_copy(dst[:, 0:ncols],
                                                                    p[bk][:, 0:ncols]))
            bank_guard[bk] = (cV, ev)

        V(lambda: v.tensor_scalar_add(bias_w[:, :], ab_col[:, :], -LN32))
        V(lambda: v.tensor_mul(stat[:, 0:8], ab_col[:, :], ab_col[:, :]))
        V(lambda: v.scalar_tensor_tensor(bias_f[:, :], stat[:, 0:8], -512.0,
                                         bias_w[:, :], op0=ALU.mult, op1=ALU.add))
        cols_ready = cV.n

        ring_seq = 0
        slot_last = [0, 0, 0, 0, 0]

        def ring_load(nm, strip, kk8, first):
            nonlocal ring_seq
            slot = ring_seq % 5
            fsrc = srcs[nm][strip] if nm in ("w1", "w2") \
                else srcs[nm][strip][:, kk8 * 8:(kk8 + 1) * 8, :]
            need(g, cT, slot_last[slot])
            g.dma_start(wring[:, slot, :, :], fsrc).then_inc(d_wr[slot].sem, 16)
            d_wr[slot].n += 16
            ring_seq += 1
            return slot, d_wr[slot], d_wr[slot].n

        xsq = feats
        ln_state = {}

        def ln2_squares(s_, smm_last):
            need(v, cT, smm_last)
            need(v, d_xts, d_xts.n)
            for j in range(8):
                V(lambda j=j: v.tensor_mul(xsq[:, j, :], xt_sweep[:, j, :],
                                           xt_sweep[:, j, :]))
            ln_state["sq"] = cV.n

        def ln2_stats(s_):
            need(t, cV, ln_state["sq"])
            guard_bank(6)
            guard_bank(7)
            for j in range(8):
                t.matmul(p[6][0:1, :], ones_b[:, 0:1], xt_sweep[:, j, :],
                         start=(j == 0), stop=(j == 7))
                ii = t.matmul(p[7][0:1, :], ones_b[:, 0:1], xsq[:, j, :],
                              start=(j == 0), stop=(j == 7))
            st = cT.inc(ii)
            need(v, cT, st)
            V(lambda: v.tensor_copy(rows[0:1, 0, :], p[6][0:1, :]))
            ev = V(lambda: v.tensor_copy(rows[0:1, 1, :], p[7][0:1, :]))
            bank_guard[6] = (cV, ev)
            bank_guard[7] = (cV, ev)
            V(lambda: v.tensor_scalar_mul(rows[0:1, 0, :], rows[0:1, 0, :], 1.0 / D))
            V(lambda: v.tensor_scalar_mul(rows[0:1, 1, :], rows[0:1, 1, :], 1.0 / D))
            V(lambda: v.tensor_mul(rows[0:1, 2, :], rows[0:1, 0, :], rows[0:1, 0, :]))
            sv = V(lambda: v.tensor_sub(rows[0:1, 1, :], rows[0:1, 1, :], rows[0:1, 2, :]))
            need(sc, cV, sv)
            S(lambda: sc.activation(rows[0:1, 2, :], rows[0:1, 1, :], AF.Ln,
                                    bias=c_eps[0:1, 0:1]))
            rs = S(lambda: sc.activation(rows[0:1, 2, :], rows[0:1, 2, :], AF.Exp,
                                         scale=c_nhalf[0:1, 0:1]))
            need(v, cS, rs)
            V(lambda: v.tensor_mul(rows[0:1, 3, :], rows[0:1, 0, :], rows[0:1, 2, :]))
            rd = V(lambda: v.scalar_tensor_tensor(rows[0:1, 1, :], rows[0:1, 2, :],
                                                  512.0 * EPS, rows[0:1, 2, :],
                                                  op0=ALU.mult, op1=ALU.mult))
            ln_state["rows"] = rd

        def ln2_bcast_affine(s_, ffn1_last):
            need(t, cV, ln_state["rows"])
            guard_bank(4)
            guard_bank(5)
            bc1 = cT.inc(t.matmul(p[4][:, :], ones_row[0:1, :], rows[0:1, 2, :],
                                  start=True, stop=True))
            bc2 = cT.inc(t.matmul(p[5][:, :], ones_row[0:1, :], rows[0:1, 3, :],
                                  start=True, stop=True))
            need(v, cT, bc2)
            need(v, cT, ffn1_last)
            for j in range(8):
                V(lambda j=j: v.tensor_mul(ln_tmp[:, :], xt_sweep[:, j, :], p[4][:, :]))
                V(lambda j=j: v.tensor_sub(xn_fm[:, j, :], ln_tmp[:, :], p[5][:, :]))
            xr = V(lambda: v.tensor_copy(xn_fm[0:1, 8, :], rows[0:1, 1, :]))
            bank_guard[4] = (cV, xr)
            bank_guard[5] = (cV, xr)
            xts_state["guard"] = xr
            return xr

        ffn4_done_T = 0
        t1_guard = {}
        t2_guard = {}
        y_val = {}
        xt_guard = {}

        need(v, cast_sem["iwT"], 16)
        sq_done = 0
        for j in range(8):
            sq_done = V(lambda j=j: v.tensor_mul(
                h_sb[:, 2 * j:2 * j + 2, :].rearrange("p a b -> p (a b)"),
                iw_c[j], iw_c[j]))

        ln2_squares(0, 0)

        wstate = {}

        def emit_wphase():
            need(t, cV, sq_done)
            need(t, cast_sem["iwT"], 16)
            guard_bank(4)
            guard_bank(5)
            for j in range(8):
                sqv = h_sb[:, 2 * j:2 * j + 2, :].rearrange("p a b -> p (a b)")
                t.matmul(p[4][0:1, :], ones_nh[:, 0:1], sqv[:, 0:512],
                         start=(j == 0), stop=(j == 7))
                ii = t.matmul(p[5][0:1, :], ones_nh[:, 0:1], sqv[:, 512:1024],
                              start=(j == 0), stop=(j == 7))
            so_done = cT.inc(ii)
            wstate["so_mm"] = so_done
            need(sc, cT, so_done)
            e1 = S(lambda: sc.copy(iw_c[8][0:1, 0:512], p[4][0:1, :]))
            e2 = S(lambda: sc.copy(iw_c[8][0:1, 512:1024], p[5][0:1, :]))
            bank_guard[4] = (cS, e1)
            bank_guard[5] = (cS, e2)
            need(t, cast_sem["projT"], 16)
            need(t, cast_sem["xisrow"], 16)
            need(t, cS, e2)
            need(t, cV, cols_ready)
            ev = 0
            for rc in range(8):
                for oh in range(2):
                    bk = 4 + oh
                    guard_bank(bk)
                    for j in range(9):
                        ii = t.matmul(p[bk][:, :], projT_sb[:, j, rc * 128:(rc + 1) * 128],
                                      iw_c[j][:, oh * 512:(oh + 1) * 512],
                                      start=(j == 0), stop=(j == 8))
                    stop = cT.inc(ii)
                    need(sc, cT, stop)
                    ev = S(lambda rc=rc, oh=oh, bk=bk: sc.activation(
                        Wt_sb[:, rc, oh * 512:(oh + 1) * 512], p[bk][:, :], AF.Exp,
                        bias=bias_w[:, rc:rc + 1], scale=ab_col[:, rc:rc + 1]))
                    bank_guard[bk] = (cS, ev)
            wstate["wt_done"] = ev

        ln2_stats(0)
        emit_wphase()
        xn_ready_sweep = ln2_bcast_affine(0, 0)

        def emit_dot0(xn_rdy):
            need(t, cast_sem["projT"], 16)
            need(t, cast_sem["xisrow"], 16)
            need(t, cV, xn_rdy)
            need(t, cV, cols_ready)
            for rc in range(8):
                bk = (6, 7, 4, 5)[rc % 4]
                guard_bank(bk)
                for j in range(9):
                    ii = t.matmul(p[bk][:, :], projT_sb[:, j, rc * 128:(rc + 1) * 128],
                                  xn_fm[:, j, :], start=(j == 0), stop=(j == 8))
                stop = cT.inc(ii)
                need(sc, cT, stop)
                ev = S(lambda rc=rc, bk=bk: sc.activation(
                    feats[:, rc, :], p[bk][:, :], AF.Exp,
                    bias=bias_f[:, rc:rc + 1], scale=ab_col[:, rc:rc + 1]))
                bank_guard[bk] = (cS, ev)
            return cS.n

        def emit_smm(feats_done):
            need(t, cS, feats_done)
            need(t, cS, wstate["wt_done"])
            for oc in range(8):
                bk = oc % 4
                guard_bank(bk)
                for rc in range(8):
                    ii = t.matmul(p[bk][:, :], Wt_sb[:, rc, oc * 128:(oc + 1) * 128],
                                  feats[:, rc, :], start=(rc == 0), stop=(rc == 7))
                stop = cT.inc(ii)
                need(v, cT, stop)
                need(v, d_y[oc], y_val.get(oc, 0))
                ev = V(lambda oc=oc, bk=bk: v.tensor_copy(s_sb[:, oc, :], p[bk][:, :]))
                bank_guard[bk] = (cV, ev)
            return cT.n

        smm_next = 0
        feats_next = 0
        for s_ in range(NS):
            tok0 = s_ * TS

            h_by_strip = {}

            def ffn1_strip(ss):
                slot1, wc1, wv1 = ring_load("w1", ss, 0, s_ == 0)
                slot2, wc2, wv2 = ring_load("w2", ss, 0, s_ == 0)
                for hh in range(2):
                    ba1 = 0 if hh == 0 else 4
                    ba2 = 2 if hh == 0 else 6
                    need(t, wc1, wv1)
                    for og in range(2):
                        guard_bank(ba1 + og)
                    for k in range(8):
                        for og in range(2):
                            ii = t.matmul(p[ba1 + og][:, :],
                                          wring[:, slot1, k, (hh * 2 + og) * 128:(hh * 2 + og + 1) * 128],
                                          xn_fm[:, k, :], start=(k == 0), stop=(k == 7))
                    a1_stop = cT.inc(ii)
                    if hh == 1:
                        slot_last[slot1] = max(slot_last[slot1], a1_stop)
                    need(sc, cT, a1_stop)
                    last_silu = 0
                    for og in range(2):
                        oc = ss * 4 + hh * 2 + og
                        sl = hh * 2 + og
                        tg = t1_guard.get(sl)
                        if tg:
                            need(sc, cV, tg)
                        ev = S(lambda sl=sl, og=og, oc=oc, ba1=ba1: sc.activation(
                            t1[:, sl, :], p[ba1 + og][:, :], AF.Silu, scale=w1s_c[:, oc:oc + 1]))
                        bank_guard[ba1 + og] = (cS, ev)
                        last_silu = ev
                    need(t, wc2, wv2)
                    for og in range(2):
                        guard_bank(ba2 + og)
                    for k in range(8):
                        for og in range(2):
                            ii = t.matmul(p[ba2 + og][:, :],
                                          wring[:, slot2, k, (hh * 2 + og) * 128:(hh * 2 + og + 1) * 128],
                                          xn_fm[:, k, :], start=(k == 0), stop=(k == 7))
                    a2_stop = cT.inc(ii)
                    if hh == 1:
                        slot_last[slot2] = max(slot_last[slot2], a2_stop)
                    need(v, cT, a2_stop)
                    need(v, cS, last_silu)
                    need(v, cT, ffn4_done_T)
                    if s_ == 0:
                        need(v, cT, wstate["so_mm"])
                    for og in range(2):
                        oc = ss * 4 + hh * 2 + og
                        sl = hh * 2 + og
                        ev = V(lambda oc=oc, og=og, sl=sl, ba2=ba2: v.scalar_tensor_tensor(
                            h_sb[:, oc, :], p[ba2 + og][:, :], w2s_c[:, oc:oc + 1],
                            t1[:, sl, :], op0=ALU.mult, op1=ALU.mult))
                        bank_guard[ba2 + og] = (cV, ev)
                        t1_guard[sl] = ev
                h_by_strip[ss] = cV.n

            if s_ == 0:
                feats_done = emit_dot0(xn_ready_sweep)
                ffn1_strip(0)
                ffn1_strip(1)
                smm_last = emit_smm(feats_done)
                first_strip = 2
            else:
                smm_last = smm_next
                first_strip = 0

            for ss in range(first_strip, 8):
                ffn1_strip(ss)
                if ss == 2 and s_ + 1 < NS:
                    emit_xts_loads(s_ + 1)
                if ss == 3 and s_ + 1 < NS:
                    ln2_squares(s_ + 1, smm_last)
                if ss == 5 and s_ + 1 < NS:
                    ln2_stats(s_ + 1)
            ffn1_last = cT.n

            if s_ + 1 < NS:
                xn_ready_sweep = ln2_bcast_affine(s_ + 1, ffn1_last)

            g1_by_strip = {}
            g2_by_strip = {}
            for nm, src_sb, dst_sb, s_col, src_rdy, dst_rec in (
                    ("h1", h_sb, g1_sb, h1s_c, (cV, h_by_strip), g1_by_strip),
                    ("h2", g1_sb, h_sb, h2s_c, (cS, g1_by_strip), g2_by_strip)):
                rdy_ctr, rdy_map = src_rdy
                for ss in range(8):
                    quad = [0, 1, 2, 3] if ss % 2 == 0 else [4, 5, 6, 7]
                    for bi in quad:
                        guard_bank(bi)
                    for kk in range(4):
                        need(t, rdy_ctr, rdy_map[2 * kk + 1])
                        slot, wc, wv = ring_load(nm, ss, kk, s_ == 0)
                        need(t, wc, wv)
                        for k8 in range(8):
                            for og in range(4):
                                ii = t.matmul(p[quad[og]][:, :],
                                              wring[:, slot, k8, og * 128:(og + 1) * 128],
                                              src_sb[:, kk * 8 + k8, :],
                                              start=(kk == 0 and k8 == 0),
                                              stop=(kk == 3 and k8 == 7))
                        slot_last[slot] = cT.inc(ii)
                    stop = slot_last[slot]
                    need(sc, cT, stop)
                    for og in range(4):
                        oc = ss * 4 + og
                        ev = S(lambda dst_sb=dst_sb, oc=oc, og=og, quad=quad, s_col=s_col:
                               sc.activation(dst_sb[:, oc, :], p[quad[og]][:, :], AF.Gelu,
                                             scale=s_col[:, oc:oc + 1]))
                        bank_guard[quad[og]] = (cS, ev)
                    dst_rec[ss] = cS.n

            if s_ + 1 < NS:
                feats_next = emit_dot0(xn_ready_sweep)

            sg_done = [0] * 8
            for ss in range(2):
                quad = [0, 1, 2, 3] if ss == 0 else [4, 5, 2, 3]
                for bi in quad:
                    guard_bank(bi)
                for kk in range(4):
                    need(t, cS, g2_by_strip[2 * kk + 1])
                    slot, wc, wv = ring_load("w3", ss, kk, s_ == 0)
                    need(t, wc, wv)
                    for k8 in range(8):
                        for og in range(4):
                            ii = t.matmul(p[quad[og]][:, :],
                                          wring[:, slot, k8, og * 128:(og + 1) * 128],
                                          h_sb[:, kk * 8 + k8, :],
                                          start=(kk == 0 and k8 == 0),
                                          stop=(kk == 3 and k8 == 7))
                    slot_last[slot] = cT.inc(ii)
                ffn4_done_T = slot_last[slot]
                need(sc, cT, ffn4_done_T)
                for og in range(4):
                    dc = ss * 4 + og
                    tg = t2_guard.get(og)
                    if tg:
                        need(sc, cV, tg)
                    ev = S(lambda og=og, dc=dc, quad=quad: sc.activation(
                        t2[:, og, :], p[quad[og]][:, :], AF.Silu,
                        scale=w3s_c[:, dc:dc + 1]))
                    bank_guard[quad[og]] = (cS, ev)
                for og in range(4):
                    dc = ss * 4 + og
                    need(v, cS, bank_guard[quad[og]][1] if bank_guard[quad[og]][0] is cS else 0)
                    mv = V(lambda og=og, dc=dc: v.tensor_mul(
                        s_sb[:, dc, :], t2[:, og, :], s_sb[:, dc, :]))
                    t2_guard[og] = mv
                    sg_done[dc] = mv

            xt_val = {}

            def xt_load(dc):
                slot = dc % 2
                need(sy, cV, xt_guard.get(slot, 0))
                sy.dma_start(xt_r[:, slot, :],
                             xT[dc * 128:(dc + 1) * 128, tok0:tok0 + TS])\
                    .then_inc(d_xt[slot].sem, 16)
                d_xt[slot].n += 16
                xt_val[dc] = d_xt[slot].n

            xt_load(0)
            xt_load(1)
            for dc in range(8):
                slot = dc % 2
                need(v, cV, sg_done[dc])
                need(v, d_xt[slot], xt_val[dc])
                av = V(lambda dc=dc, slot=slot: v.tensor_add(
                    s_sb[:, dc, :], xt_r[:, slot, :], s_sb[:, dc, :]))
                xt_guard[slot] = av
                if dc + 2 < 8:
                    xt_load(dc + 2)
                need(sy, cV, av)
                sy.dma_start(y[dc * 128:(dc + 1) * 128, tok0:tok0 + TS],
                             s_sb[:, dc, :]).then_inc(d_y[dc].sem, 16)
                d_y[dc].n += 16
                y_val[dc] = d_y[dc].n

            if s_ + 1 < NS:
                smm_next = emit_smm(feats_next)

        for slot in range(8):
            g.wait_ge(d_y[slot].sem, d_y[slot].n)
    return nc


def _pack(w):
    O, I = w.shape
    wt = np.ascontiguousarray(w.T)
    arr = wt.reshape(I // 128, 128, O // 512, 512)
    return np.ascontiguousarray(arr.transpose(2, 1, 0, 3))


_NC_CACHE = {}


def kernel(L0_attn_x, L0_attn_input_weights, L0_attn_xis, L0_attn_proj_matrix,
           L0_ffn_w1_w, L0_ffn_w1_s, L0_ffn_w2_w, L0_ffn_w2_s,
           L0_ffn_h1_w, L0_ffn_h1_s, L0_ffn_h2_w, L0_ffn_h2_s,
           L0_ffn_w3_w, L0_ffn_w3_s):
    x = np.ascontiguousarray(np.asarray(L0_attn_x, dtype=np.float32)).reshape(-1, D)
    shared = {
        "projT": np.ascontiguousarray(np.asarray(L0_attn_proj_matrix, np.float32).T),
        "iwT": np.ascontiguousarray(np.asarray(L0_attn_input_weights, np.float32).T),
        "xis": np.ascontiguousarray(np.asarray(L0_attn_xis, np.float32)),
        "w1p": _pack(np.asarray(L0_ffn_w1_w, np.float32)),
        "w2p": _pack(np.asarray(L0_ffn_w2_w, np.float32)),
        "h1p": _pack(np.asarray(L0_ffn_h1_w, np.float32)),
        "h2p": _pack(np.asarray(L0_ffn_h2_w, np.float32)),
        "w3p": _pack(np.asarray(L0_ffn_w3_w, np.float32)),
        "w1s": np.ascontiguousarray(np.asarray(L0_ffn_w1_s, np.float32)),
        "w2s": np.ascontiguousarray(np.asarray(L0_ffn_w2_s, np.float32)),
        "h1s": np.ascontiguousarray(np.asarray(L0_ffn_h1_s, np.float32)),
        "h2s": np.ascontiguousarray(np.asarray(L0_ffn_h2_s, np.float32)),
        "w3s": np.ascontiguousarray(np.asarray(L0_ffn_w3_s, np.float32)),
    }
    if "nc" not in _NC_CACHE:
        _NC_CACHE["nc"] = build_nc()
    nc = _NC_CACHE["nc"]
    in_maps = []
    for c in range(8):
        xc = np.ascontiguousarray(x[c * T:(c + 1) * T])
        m = {"xT": np.ascontiguousarray(xc.T)}
        m.update(shared)
        in_maps.append(m)
    res = run_bass_kernel_spmd(nc, in_maps, core_ids=list(range(8)))
    out = np.concatenate([np.ascontiguousarray(res.results[c]["y"].T) for c in range(8)], axis=0)
    return out.reshape(4, 4096, D)


if __name__ == "__main__":
    nc = build_nc()
    print("built ok, instructions:", nc.n_instructions())


# version 26
# speedup vs baseline: 1.0136x; 1.0136x over previous
import contextlib
import numpy as np
import ml_dtypes
import concourse.bass as bass
import concourse.mybir as mybir
from concourse.bass_utils import run_bass_kernel_spmd

F32 = mybir.dt.float32
BF16 = mybir.dt.bfloat16
AF = mybir.ActivationFunctionType
ALU = mybir.AluOpType
AX = mybir.AxisListType
bf16 = ml_dtypes.bfloat16

T = 2048
TS = 512
NS = T // TS
NT = TS // 128
D = 1024
H = 4096
R = 1024
LN32 = float(np.log(np.sqrt(1024.0)))
EPS = 1e-5


class Ctr:
    def __init__(self, nc, stack, name, eng):
        self.sem = stack.enter_context(nc.semaphore(name))
        self.n = 0
        self.eng = eng

    def inc(self, instr, k=1):
        instr.then_inc(self.sem, k)
        self.n += k
        return self.n


def build_nc():
    nc = bass.Bass("TRN2", target_bir_lowering=False)
    g = nc.gpsimd
    sy = nc.sync
    t = nc.tensor
    v = nc.vector
    sc = nc.scalar

    xT = nc.dram_tensor("xT", [D, T], F32, kind="ExternalInput").ap()
    projT = nc.dram_tensor("projT", [D, R], F32, kind="ExternalInput").ap()
    iwT = nc.dram_tensor("iwT", [D, R], F32, kind="ExternalInput").ap()
    xis = nc.dram_tensor("xis", [R], F32, kind="ExternalInput").ap()
    w1p = nc.dram_tensor("w1p", [8, 128, 8, 512], F32, kind="ExternalInput").ap()
    w2p = nc.dram_tensor("w2p", [8, 128, 8, 512], F32, kind="ExternalInput").ap()
    h1p = nc.dram_tensor("h1p", [8, 128, 32, 512], F32, kind="ExternalInput").ap()
    h2p = nc.dram_tensor("h2p", [8, 128, 32, 512], F32, kind="ExternalInput").ap()
    w3p = nc.dram_tensor("w3p", [2, 128, 32, 512], F32, kind="ExternalInput").ap()
    w1s = nc.dram_tensor("w1s", [H], F32, kind="ExternalInput").ap()
    w2s = nc.dram_tensor("w2s", [H], F32, kind="ExternalInput").ap()
    h1s = nc.dram_tensor("h1s", [H], F32, kind="ExternalInput").ap()
    h2s = nc.dram_tensor("h2s", [H], F32, kind="ExternalInput").ap()
    w3s = nc.dram_tensor("w3s", [D], F32, kind="ExternalInput").ap()
    y = nc.dram_tensor("y", [D, T], F32, kind="ExternalOutput").ap()

    identf_h = nc.inline_tensor(np.eye(128).astype(np.float32), "ident_f")

    with contextlib.ExitStack() as stack:
        E = stack.enter_context

        def sb(name, shape, dtype):
            return E(nc.sbuf_tensor(name, shape, dtype))

        projT_sb = sb("projT_sb", [128, 9, R], BF16)
        Wt_sb = sb("Wt_sb", [128, 8, R], BF16)
        xt_sweep = sb("xt_sweep", [128, 8, TS], BF16)
        xt_r = sb("xt_r", [128, 4, 512], F32)
        rows = sb("rows", [128, 4, TS], F32)
        xn_fm = sb("xn_fm", [128, 9, TS], BF16)
        feats = sb("feats", [128, 8, TS], BF16)
        s_sb = sb("s_sb", [128, 8, TS], F32)
        h_sb = sb("h_sb", [128, 32, TS], BF16)
        g1_sb = sb("g1_sb", [128, 32, TS], BF16)
        wring = sb("wring", [128, 5, 8, TS], BF16)
        t1 = sb("t1", [128, 4, TS], BF16)
        t2 = sb("t2", [128, 4, TS], BF16)
        identf = sb("identf", [128, 128], F32)
        vec_sb = sb("vec_sb", [32, 6, 128], F32)
        ab_col = sb("ab_col", [128, 8], F32)
        bias_f = sb("bias_f", [128, 8], F32)
        bias_w = sb("bias_w", [128, 8], F32)
        w1s_c = sb("w1s_c", [128, 32], F32)
        w2s_c = sb("w2s_c", [128, 32], F32)
        h1s_c = sb("h1s_c", [128, 32], F32)
        h2s_c = sb("h2s_c", [128, 32], F32)
        w3s_c = sb("w3s_c", [128, 8], F32)
        stat = sb("stat", [128, 16], F32)
        ones_nh = sb("ones_nh", [128, 1], BF16)
        ones_b = sb("ones_b", [128, 1], BF16)
        ones_c = sb("ones_c", [128, 1], F32)
        ones_row = sb("ones_row", [1, 128], F32)
        c_eps = sb("c_eps", [128, 1], F32)
        c_nhalf = sb("c_nhalf", [128, 1], F32)

        iw_view = g1_sb.ap().rearrange("p a b -> p (a b)")
        iw_c = [iw_view[:, j * R:(j + 1) * R] for j in range(9)]

        p = [E(nc.psum_tensor(f"pb{i}", [128, 512], F32)) for i in range(8)]

        cT = Ctr(nc, stack, "cT", t)
        cV = Ctr(nc, stack, "cV", v)
        cS = Ctr(nc, stack, "cS", sc)
        d_xts = Ctr(nc, stack, "d_xts", g)
        d_xt = [Ctr(nc, stack, f"d_xt{i}", sy) for i in range(4)]
        d_y = [Ctr(nc, stack, f"d_y{i}", sy) for i in range(8)]
        d_wr = [Ctr(nc, stack, f"d_wr{i}", g) for i in range(5)]
        d_ident = Ctr(nc, stack, "d_ident", sy)
        d_misc = Ctr(nc, stack, "d_misc", sy)
        d_vecj = [Ctr(nc, stack, f"d_vecj{i}", sy) for i in range(6)]
        cast_sem = {}

        waited = {}

        def need(eng, ctr, val, allow_self=False):
            if ctr is None or val <= 0:
                return
            if ctr.eng is eng and not allow_self:
                return
            key = (id(eng), id(ctr.sem))
            if waited.get(key, 0) >= val:
                return
            eng.wait_ge(ctr.sem, val)
            waited[key] = val

        bank_guard = {}

        def guard_bank(i):
            gd = bank_guard.get(i)
            if gd is not None:
                need(t, gd[0], gd[1])

        def V(instr_fn):
            need(v, cV, cV.n, allow_self=True)
            return cV.inc(instr_fn())

        def S(instr_fn):
            need(sc, cS, cS.n, allow_self=True)
            return cS.inc(instr_fn())

        cast_sem["projT"] = Ctr(nc, stack, "cs_projT", g)
        g.dma_start(projT_sb[:, 0:8, :], projT.rearrange("(j p) r -> p j r", p=128))\
            .then_inc(cast_sem["projT"].sem, 16)
        cast_sem["projT"].n = 16
        cast_sem["iwT"] = Ctr(nc, stack, "cs_iwT", g)
        g.dma_start(iw_view[:, 0:8 * R].rearrange("p (j r) -> p j r", j=8),
                    iwT.rearrange("(j p) r -> p j r", p=128)).then_inc(cast_sem["iwT"].sem, 16)
        cast_sem["iwT"].n = 16

        xts_state = {"guard": 0}

        def emit_xts_loads(s_):
            need(g, cV, xts_state["guard"])
            for j in range(8):
                g.dma_start(xt_sweep[:, j, :],
                            xT[j * 128:(j + 1) * 128, s_ * TS:(s_ + 1) * TS])\
                    .then_inc(d_xts.sem, 16)
                d_xts.n += 16

        emit_xts_loads(0)

        srcs = {"w1": w1p, "w2": w2p, "h1": h1p, "h2": h2p, "w3": w3p}

        sy.dma_start(identf[:, :], identf_h.ap()).then_inc(d_ident.sem, 16)
        d_ident.n += 16

        ms_p8 = V(lambda: v.memset(projT_sb[:, 8, :], 0.0))
        V(lambda: v.memset(iw_c[8], 0.0))
        V(lambda: v.memset(xn_fm[:, 8, :], 0.0))
        V(lambda: v.memset(ones_nh[:, :], -0.5))
        V(lambda: v.memset(c_eps[:, :], EPS))
        V(lambda: v.memset(c_nhalf[:, :], -0.5))
        V(lambda: v.memset(ones_b[:, :], 1.0))
        V(lambda: v.memset(ones_c[:, :], 1.0))
        V(lambda: v.memset(ones_row[:, :], 1.0))
        cast_sem["xisrow"] = Ctr(nc, stack, "cs_xisrow", g)
        g.wait_ge(cV.sem, ms_p8)
        g.dma_start(projT_sb[0:1, 8, :], xis[None, :]).then_inc(cast_sem["xisrow"].sem, 16)
        cast_sem["xisrow"].n = 16

        vec_jobs = [
            (xis, 8, ab_col, 8), (w1s, 32, w1s_c, 32), (w2s, 32, w2s_c, 32),
            (h1s, 32, h1s_c, 32), (h2s, 32, h2s_c, 32), (w3s, 8, w3s_c, 8),
        ]
        for vi, (vsrc, nparts, dst, ncols) in enumerate(vec_jobs):
            sy.dma_start(vec_sb[0:nparts, vi, :], vsrc.rearrange("(c p) -> c p", p=128))\
                .then_inc(d_vecj[vi].sem, 16)
            d_vecj[vi].n += 16
        for vi, (vsrc, nparts, dst, ncols) in enumerate(vec_jobs):
            need(t, d_vecj[vi], 16)
            need(t, d_ident, 16)
            bk = 4 + (vi % 4)
            guard_bank(bk)
            vt = cT.inc(t.transpose(p[bk][0:128, 0:ncols], vec_sb[0:nparts, vi, :],
                                    identf[0:nparts, 0:nparts]))
            need(v, cT, vt)
            ev = V(lambda dst=dst, ncols=ncols, bk=bk: v.tensor_copy(dst[:, 0:ncols],
                                                                    p[bk][:, 0:ncols]))
            bank_guard[bk] = (cV, ev)

        V(lambda: v.tensor_scalar_add(bias_w[:, :], ab_col[:, :], -LN32))
        V(lambda: v.tensor_mul(stat[:, 0:8], ab_col[:, :], ab_col[:, :]))
        V(lambda: v.scalar_tensor_tensor(bias_f[:, :], stat[:, 0:8], -512.0,
                                         bias_w[:, :], op0=ALU.mult, op1=ALU.add))
        cols_ready = cV.n

        ring_seq = 0
        slot_last = [0, 0, 0, 0, 0]

        def ring_load(nm, strip, kk8, first):
            nonlocal ring_seq
            slot = ring_seq % 5
            fsrc = srcs[nm][strip] if nm in ("w1", "w2") \
                else srcs[nm][strip][:, kk8 * 8:(kk8 + 1) * 8, :]
            need(g, cT, slot_last[slot])
            g.dma_start(wring[:, slot, :, :], fsrc).then_inc(d_wr[slot].sem, 16)
            d_wr[slot].n += 16
            ring_seq += 1
            return slot, d_wr[slot], d_wr[slot].n

        xsq = feats
        ln_state = {}

        def ln2_squares(s_, smm_last):
            need(v, cT, smm_last)
            need(v, d_xts, d_xts.n)
            for j in range(8):
                V(lambda j=j: v.tensor_mul(xsq[:, j, :], xt_sweep[:, j, :],
                                           xt_sweep[:, j, :]))
            ln_state["sq"] = cV.n

        def ln2_stats(s_):
            need(t, cV, ln_state["sq"])
            guard_bank(6)
            guard_bank(7)
            for j in range(8):
                t.matmul(p[6][0:1, :], ones_b[:, 0:1], xt_sweep[:, j, :],
                         start=(j == 0), stop=(j == 7))
                ii = t.matmul(p[7][0:1, :], ones_b[:, 0:1], xsq[:, j, :],
                              start=(j == 0), stop=(j == 7))
            st = cT.inc(ii)
            need(v, cT, st)
            V(lambda: v.tensor_copy(rows[0:1, 0, :], p[6][0:1, :]))
            ev = V(lambda: v.tensor_copy(rows[0:1, 1, :], p[7][0:1, :]))
            bank_guard[6] = (cV, ev)
            bank_guard[7] = (cV, ev)
            V(lambda: v.tensor_scalar_mul(rows[0:1, 0, :], rows[0:1, 0, :], 1.0 / D))
            V(lambda: v.tensor_scalar_mul(rows[0:1, 1, :], rows[0:1, 1, :], 1.0 / D))
            V(lambda: v.tensor_mul(rows[0:1, 2, :], rows[0:1, 0, :], rows[0:1, 0, :]))
            sv = V(lambda: v.tensor_sub(rows[0:1, 1, :], rows[0:1, 1, :], rows[0:1, 2, :]))
            need(sc, cV, sv)
            S(lambda: sc.activation(rows[0:1, 2, :], rows[0:1, 1, :], AF.Ln,
                                    bias=c_eps[0:1, 0:1]))
            rs = S(lambda: sc.activation(rows[0:1, 2, :], rows[0:1, 2, :], AF.Exp,
                                         scale=c_nhalf[0:1, 0:1]))
            need(v, cS, rs)
            V(lambda: v.tensor_mul(rows[0:1, 3, :], rows[0:1, 0, :], rows[0:1, 2, :]))
            rd = V(lambda: v.scalar_tensor_tensor(rows[0:1, 1, :], rows[0:1, 2, :],
                                                  512.0 * EPS, rows[0:1, 2, :],
                                                  op0=ALU.mult, op1=ALU.mult))
            ln_state["rows"] = rd

        def ln2_bcast_affine(s_, ffn1_last):
            need(t, cV, ln_state["rows"])
            guard_bank(4)
            guard_bank(5)
            bc1 = cT.inc(t.matmul(p[4][:, :], ones_row[0:1, :], rows[0:1, 2, :],
                                  start=True, stop=True))
            bc2 = cT.inc(t.matmul(p[5][:, :], ones_row[0:1, :], rows[0:1, 3, :],
                                  start=True, stop=True))
            need(v, cT, bc2)
            need(v, cT, ffn1_last)
            for j in range(8):
                V(lambda j=j: v.tensor_mul(xn_fm[:, j, :], xt_sweep[:, j, :], p[4][:, :]))
                V(lambda j=j: v.tensor_sub(xn_fm[:, j, :], xn_fm[:, j, :], p[5][:, :]))
            xr = V(lambda: v.tensor_copy(xn_fm[0:1, 8, :], rows[0:1, 1, :]))
            bank_guard[4] = (cV, xr)
            bank_guard[5] = (cV, xr)
            xts_state["guard"] = xr
            return xr

        ffn4_done_T = 0
        t1_guard = {}
        t2_guard = {}
        y_val = {}
        xt_guard = {}

        need(v, cast_sem["iwT"], 16)
        sq_done = 0
        for j in range(8):
            sq_done = V(lambda j=j: v.tensor_mul(
                h_sb[:, 2 * j:2 * j + 2, :].rearrange("p a b -> p (a b)"),
                iw_c[j], iw_c[j]))

        ln2_squares(0, 0)

        wstate = {}

        def emit_wphase():
            need(t, cV, sq_done)
            need(t, cast_sem["iwT"], 16)
            guard_bank(4)
            guard_bank(5)
            for j in range(8):
                sqv = h_sb[:, 2 * j:2 * j + 2, :].rearrange("p a b -> p (a b)")
                t.matmul(p[4][0:1, :], ones_nh[:, 0:1], sqv[:, 0:512],
                         start=(j == 0), stop=(j == 7))
                ii = t.matmul(p[5][0:1, :], ones_nh[:, 0:1], sqv[:, 512:1024],
                              start=(j == 0), stop=(j == 7))
            so_done = cT.inc(ii)
            wstate["so_mm"] = so_done
            need(sc, cT, so_done)
            e1 = S(lambda: sc.copy(iw_c[8][0:1, 0:512], p[4][0:1, :]))
            e2 = S(lambda: sc.copy(iw_c[8][0:1, 512:1024], p[5][0:1, :]))
            bank_guard[4] = (cS, e1)
            bank_guard[5] = (cS, e2)
            need(t, cast_sem["projT"], 16)
            need(t, cast_sem["xisrow"], 16)
            need(t, cS, e2)
            need(t, cV, cols_ready)
            ev = 0
            for rc in range(8):
                for oh in range(2):
                    bk = 4 + oh
                    guard_bank(bk)
                    for j in range(9):
                        ii = t.matmul(p[bk][:, :], projT_sb[:, j, rc * 128:(rc + 1) * 128],
                                      iw_c[j][:, oh * 512:(oh + 1) * 512],
                                      start=(j == 0), stop=(j == 8))
                    stop = cT.inc(ii)
                    need(sc, cT, stop)
                    ev = S(lambda rc=rc, oh=oh, bk=bk: sc.activation(
                        Wt_sb[:, rc, oh * 512:(oh + 1) * 512], p[bk][:, :], AF.Exp,
                        bias=bias_w[:, rc:rc + 1], scale=ab_col[:, rc:rc + 1]))
                    bank_guard[bk] = (cS, ev)
            wstate["wt_done"] = ev

        ln2_stats(0)
        emit_wphase()
        xn_ready_sweep = ln2_bcast_affine(0, 0)

        def emit_dot0(xn_rdy):
            need(t, cast_sem["projT"], 16)
            need(t, cast_sem["xisrow"], 16)
            need(t, cV, xn_rdy)
            need(t, cV, cols_ready)
            for rc in range(8):
                bk = (6, 7, 4, 5)[rc % 4]
                guard_bank(bk)
                for j in range(9):
                    ii = t.matmul(p[bk][:, :], projT_sb[:, j, rc * 128:(rc + 1) * 128],
                                  xn_fm[:, j, :], start=(j == 0), stop=(j == 8))
                stop = cT.inc(ii)
                need(sc, cT, stop)
                ev = S(lambda rc=rc, bk=bk: sc.activation(
                    feats[:, rc, :], p[bk][:, :], AF.Exp,
                    bias=bias_f[:, rc:rc + 1], scale=ab_col[:, rc:rc + 1]))
                bank_guard[bk] = (cS, ev)
            return cS.n

        def emit_smm(feats_done):
            need(t, cS, feats_done)
            need(t, cS, wstate["wt_done"])
            for oc in range(8):
                bk = oc % 4
                guard_bank(bk)
                for rc in range(8):
                    ii = t.matmul(p[bk][:, :], Wt_sb[:, rc, oc * 128:(oc + 1) * 128],
                                  feats[:, rc, :], start=(rc == 0), stop=(rc == 7))
                stop = cT.inc(ii)
                need(v, cT, stop)
                need(v, d_y[oc], y_val.get(oc, 0))
                ev = V(lambda oc=oc, bk=bk: v.tensor_copy(s_sb[:, oc, :], p[bk][:, :]))
                bank_guard[bk] = (cV, ev)
            return cT.n

        smm_next = 0
        feats_next = 0
        for s_ in range(NS):
            tok0 = s_ * TS

            h_by_strip = {}

            def ffn1_strip(ss):
                slot1, wc1, wv1 = ring_load("w1", ss, 0, s_ == 0)
                slot2, wc2, wv2 = ring_load("w2", ss, 0, s_ == 0)
                for hh in range(2):
                    ba1 = 0 if hh == 0 else 4
                    ba2 = 2 if hh == 0 else 6
                    need(t, wc1, wv1)
                    for og in range(2):
                        guard_bank(ba1 + og)
                    for k in range(8):
                        for og in range(2):
                            ii = t.matmul(p[ba1 + og][:, :],
                                          wring[:, slot1, k, (hh * 2 + og) * 128:(hh * 2 + og + 1) * 128],
                                          xn_fm[:, k, :], start=(k == 0), stop=(k == 7))
                    a1_stop = cT.inc(ii)
                    if hh == 1:
                        slot_last[slot1] = max(slot_last[slot1], a1_stop)
                    need(sc, cT, a1_stop)
                    last_silu = 0
                    for og in range(2):
                        oc = ss * 4 + hh * 2 + og
                        sl = hh * 2 + og
                        tg = t1_guard.get(sl)
                        if tg:
                            need(sc, cV, tg)
                        ev = S(lambda sl=sl, og=og, oc=oc, ba1=ba1: sc.activation(
                            t1[:, sl, :], p[ba1 + og][:, :], AF.Silu, scale=w1s_c[:, oc:oc + 1]))
                        bank_guard[ba1 + og] = (cS, ev)
                        last_silu = ev
                    need(t, wc2, wv2)
                    for og in range(2):
                        guard_bank(ba2 + og)
                    for k in range(8):
                        for og in range(2):
                            ii = t.matmul(p[ba2 + og][:, :],
                                          wring[:, slot2, k, (hh * 2 + og) * 128:(hh * 2 + og + 1) * 128],
                                          xn_fm[:, k, :], start=(k == 0), stop=(k == 7))
                    a2_stop = cT.inc(ii)
                    if hh == 1:
                        slot_last[slot2] = max(slot_last[slot2], a2_stop)
                    need(v, cT, a2_stop)
                    need(v, cS, last_silu)
                    need(v, cT, ffn4_done_T)
                    if s_ == 0:
                        need(v, cT, wstate["so_mm"])
                    for og in range(2):
                        oc = ss * 4 + hh * 2 + og
                        sl = hh * 2 + og
                        ev = V(lambda oc=oc, og=og, sl=sl, ba2=ba2: v.scalar_tensor_tensor(
                            h_sb[:, oc, :], p[ba2 + og][:, :], w2s_c[:, oc:oc + 1],
                            t1[:, sl, :], op0=ALU.mult, op1=ALU.mult))
                        bank_guard[ba2 + og] = (cV, ev)
                        t1_guard[sl] = ev
                h_by_strip[ss] = cV.n

            if s_ == 0:
                feats_done = emit_dot0(xn_ready_sweep)
                ffn1_strip(0)
                ffn1_strip(1)
                smm_last = emit_smm(feats_done)
                first_strip = 2
            else:
                smm_last = smm_next
                first_strip = 0

            for ss in range(first_strip, 8):
                ffn1_strip(ss)
                if ss == 2 and s_ + 1 < NS:
                    emit_xts_loads(s_ + 1)
                if ss == 3 and s_ + 1 < NS:
                    ln2_squares(s_ + 1, smm_last)
                if ss == 5 and s_ + 1 < NS:
                    ln2_stats(s_ + 1)
            ffn1_last = cT.n

            if s_ + 1 < NS:
                xn_ready_sweep = ln2_bcast_affine(s_ + 1, ffn1_last)

            g1_by_strip = {}
            g2_by_strip = {}
            for nm, src_sb, dst_sb, s_col, src_rdy, dst_rec in (
                    ("h1", h_sb, g1_sb, h1s_c, (cV, h_by_strip), g1_by_strip),
                    ("h2", g1_sb, h_sb, h2s_c, (cS, g1_by_strip), g2_by_strip)):
                rdy_ctr, rdy_map = src_rdy
                for ss in range(8):
                    quad = [0, 1, 2, 3] if ss % 2 == 0 else [4, 5, 6, 7]
                    for bi in quad:
                        guard_bank(bi)
                    for kk in range(4):
                        need(t, rdy_ctr, rdy_map[2 * kk + 1])
                        slot, wc, wv = ring_load(nm, ss, kk, s_ == 0)
                        need(t, wc, wv)
                        for k8 in range(8):
                            for og in range(4):
                                ii = t.matmul(p[quad[og]][:, :],
                                              wring[:, slot, k8, og * 128:(og + 1) * 128],
                                              src_sb[:, kk * 8 + k8, :],
                                              start=(kk == 0 and k8 == 0),
                                              stop=(kk == 3 and k8 == 7))
                        slot_last[slot] = cT.inc(ii)
                    stop = slot_last[slot]
                    need(sc, cT, stop)
                    for og in range(4):
                        oc = ss * 4 + og
                        ev = S(lambda dst_sb=dst_sb, oc=oc, og=og, quad=quad, s_col=s_col:
                               sc.activation(dst_sb[:, oc, :], p[quad[og]][:, :], AF.Gelu,
                                             scale=s_col[:, oc:oc + 1]))
                        bank_guard[quad[og]] = (cS, ev)
                    dst_rec[ss] = cS.n

            if s_ + 1 < NS:
                feats_next = emit_dot0(xn_ready_sweep)

            xt_val = {}

            def xt_load(dc):
                slot = dc % 4
                need(sy, cV, xt_guard.get(slot, 0))
                sy.dma_start(xt_r[:, slot, :],
                             xT[dc * 128:(dc + 1) * 128, tok0:tok0 + TS])\
                    .then_inc(d_xt[slot].sem, 16)
                d_xt[slot].n += 16
                xt_val[dc] = d_xt[slot].n

            for dc in range(4):
                xt_load(dc)

            sg_done = [0] * 8
            for ss in range(2):
                quad = [0, 1, 2, 3] if ss == 0 else [4, 5, 2, 3]
                for bi in quad:
                    guard_bank(bi)
                for kk in range(4):
                    need(t, cS, g2_by_strip[2 * kk + 1])
                    slot, wc, wv = ring_load("w3", ss, kk, s_ == 0)
                    need(t, wc, wv)
                    for k8 in range(8):
                        for og in range(4):
                            ii = t.matmul(p[quad[og]][:, :],
                                          wring[:, slot, k8, og * 128:(og + 1) * 128],
                                          h_sb[:, kk * 8 + k8, :],
                                          start=(kk == 0 and k8 == 0),
                                          stop=(kk == 3 and k8 == 7))
                    slot_last[slot] = cT.inc(ii)
                ffn4_done_T = slot_last[slot]
                need(sc, cT, ffn4_done_T)
                for og in range(4):
                    dc = ss * 4 + og
                    tg = t2_guard.get(og)
                    if tg:
                        need(sc, cV, tg)
                    ev = S(lambda og=og, dc=dc, quad=quad: sc.activation(
                        t2[:, og, :], p[quad[og]][:, :], AF.Silu,
                        scale=w3s_c[:, dc:dc + 1]))
                    bank_guard[quad[og]] = (cS, ev)
                for og in range(4):
                    dc = ss * 4 + og
                    need(v, cS, bank_guard[quad[og]][1] if bank_guard[quad[og]][0] is cS else 0)
                    mv = V(lambda og=og, dc=dc: v.tensor_mul(
                        s_sb[:, dc, :], t2[:, og, :], s_sb[:, dc, :]))
                    t2_guard[og] = mv
                    sg_done[dc] = mv

            for dc in range(8):
                slot = dc % 4
                need(v, cV, sg_done[dc])
                need(v, d_xt[slot], xt_val[dc])
                av = V(lambda dc=dc, slot=slot: v.tensor_add(
                    s_sb[:, dc, :], xt_r[:, slot, :], s_sb[:, dc, :]))
                xt_guard[slot] = av
                if dc + 4 < 8:
                    xt_load(dc + 4)
                need(sy, cV, av)
                sy.dma_start(y[dc * 128:(dc + 1) * 128, tok0:tok0 + TS],
                             s_sb[:, dc, :]).then_inc(d_y[dc].sem, 16)
                d_y[dc].n += 16
                y_val[dc] = d_y[dc].n

            if s_ + 1 < NS:
                smm_next = emit_smm(feats_next)

        for slot in range(8):
            g.wait_ge(d_y[slot].sem, d_y[slot].n)
    return nc


def _pack(w):
    O, I = w.shape
    wt = np.ascontiguousarray(w.T)
    arr = wt.reshape(I // 128, 128, O // 512, 512)
    return np.ascontiguousarray(arr.transpose(2, 1, 0, 3))


_NC_CACHE = {}


def kernel(L0_attn_x, L0_attn_input_weights, L0_attn_xis, L0_attn_proj_matrix,
           L0_ffn_w1_w, L0_ffn_w1_s, L0_ffn_w2_w, L0_ffn_w2_s,
           L0_ffn_h1_w, L0_ffn_h1_s, L0_ffn_h2_w, L0_ffn_h2_s,
           L0_ffn_w3_w, L0_ffn_w3_s):
    x = np.ascontiguousarray(np.asarray(L0_attn_x, dtype=np.float32)).reshape(-1, D)
    shared = {
        "projT": np.ascontiguousarray(np.asarray(L0_attn_proj_matrix, np.float32).T),
        "iwT": np.ascontiguousarray(np.asarray(L0_attn_input_weights, np.float32).T),
        "xis": np.ascontiguousarray(np.asarray(L0_attn_xis, np.float32)),
        "w1p": _pack(np.asarray(L0_ffn_w1_w, np.float32)),
        "w2p": _pack(np.asarray(L0_ffn_w2_w, np.float32)),
        "h1p": _pack(np.asarray(L0_ffn_h1_w, np.float32)),
        "h2p": _pack(np.asarray(L0_ffn_h2_w, np.float32)),
        "w3p": _pack(np.asarray(L0_ffn_w3_w, np.float32)),
        "w1s": np.ascontiguousarray(np.asarray(L0_ffn_w1_s, np.float32)),
        "w2s": np.ascontiguousarray(np.asarray(L0_ffn_w2_s, np.float32)),
        "h1s": np.ascontiguousarray(np.asarray(L0_ffn_h1_s, np.float32)),
        "h2s": np.ascontiguousarray(np.asarray(L0_ffn_h2_s, np.float32)),
        "w3s": np.ascontiguousarray(np.asarray(L0_ffn_w3_s, np.float32)),
    }
    if "nc" not in _NC_CACHE:
        _NC_CACHE["nc"] = build_nc()
    nc = _NC_CACHE["nc"]
    in_maps = []
    for c in range(8):
        xc = np.ascontiguousarray(x[c * T:(c + 1) * T])
        m = {"xT": np.ascontiguousarray(xc.T)}
        m.update(shared)
        in_maps.append(m)
    res = run_bass_kernel_spmd(nc, in_maps, core_ids=list(range(8)))
    out = np.concatenate([np.ascontiguousarray(res.results[c]["y"].T) for c in range(8)], axis=0)
    return out.reshape(4, 4096, D)


if __name__ == "__main__":
    nc = build_nc()
    print("built ok, instructions:", nc.n_instructions())


# version 27
# speedup vs baseline: 1.0188x; 1.0052x over previous
import contextlib
import numpy as np
import ml_dtypes
import concourse.bass as bass
import concourse.mybir as mybir
from concourse.bass_utils import run_bass_kernel_spmd

F32 = mybir.dt.float32
BF16 = mybir.dt.bfloat16
AF = mybir.ActivationFunctionType
ALU = mybir.AluOpType
AX = mybir.AxisListType
bf16 = ml_dtypes.bfloat16

T = 2048
TS = 512
NS = T // TS
NT = TS // 128
D = 1024
H = 4096
R = 1024
LN32 = float(np.log(np.sqrt(1024.0)))
EPS = 1e-5


class Ctr:
    def __init__(self, nc, stack, name, eng):
        self.sem = stack.enter_context(nc.semaphore(name))
        self.n = 0
        self.eng = eng

    def inc(self, instr, k=1):
        instr.then_inc(self.sem, k)
        self.n += k
        return self.n


def build_nc():
    nc = bass.Bass("TRN2", target_bir_lowering=False)
    g = nc.gpsimd
    sy = nc.sync
    t = nc.tensor
    v = nc.vector
    sc = nc.scalar

    xT = nc.dram_tensor("xT", [D, T], F32, kind="ExternalInput").ap()
    projT = nc.dram_tensor("projT", [D, R], F32, kind="ExternalInput").ap()
    iwT = nc.dram_tensor("iwT", [D, R], F32, kind="ExternalInput").ap()
    xis = nc.dram_tensor("xis", [R], F32, kind="ExternalInput").ap()
    w1p = nc.dram_tensor("w1p", [8, 128, 8, 512], F32, kind="ExternalInput").ap()
    w2p = nc.dram_tensor("w2p", [8, 128, 8, 512], F32, kind="ExternalInput").ap()
    h1p = nc.dram_tensor("h1p", [8, 128, 32, 512], F32, kind="ExternalInput").ap()
    h2p = nc.dram_tensor("h2p", [8, 128, 32, 512], F32, kind="ExternalInput").ap()
    w3p = nc.dram_tensor("w3p", [2, 128, 32, 512], F32, kind="ExternalInput").ap()
    w1s = nc.dram_tensor("w1s", [H], F32, kind="ExternalInput").ap()
    w2s = nc.dram_tensor("w2s", [H], F32, kind="ExternalInput").ap()
    h1s = nc.dram_tensor("h1s", [H], F32, kind="ExternalInput").ap()
    h2s = nc.dram_tensor("h2s", [H], F32, kind="ExternalInput").ap()
    w3s = nc.dram_tensor("w3s", [D], F32, kind="ExternalInput").ap()
    y = nc.dram_tensor("y", [D, T], F32, kind="ExternalOutput").ap()

    identf_h = nc.inline_tensor(np.eye(128).astype(np.float32), "ident_f")

    with contextlib.ExitStack() as stack:
        E = stack.enter_context

        def sb(name, shape, dtype):
            return E(nc.sbuf_tensor(name, shape, dtype))

        projT_sb = sb("projT_sb", [128, 9, R], BF16)
        Wt_sb = sb("Wt_sb", [128, 8, R], BF16)
        xt_sweep = sb("xt_sweep", [128, 8, TS], BF16)
        xt_r = sb("xt_r", [128, 4, 512], F32)
        rows = sb("rows", [128, 4, TS], F32)
        xn_fm = sb("xn_fm", [128, 9, TS], BF16)
        feats = sb("feats", [128, 8, TS], BF16)
        s_sb = sb("s_sb", [128, 8, TS], F32)
        h_sb = sb("h_sb", [128, 32, TS], BF16)
        g1_sb = sb("g1_sb", [128, 32, TS], BF16)
        wring = sb("wring", [128, 5, 8, TS], BF16)
        t1 = sb("t1", [128, 4, TS], BF16)
        t2 = sb("t2", [128, 4, TS], BF16)
        identf = sb("identf", [128, 128], F32)
        vec_sb = sb("vec_sb", [32, 6, 128], F32)
        ab_col = sb("ab_col", [128, 8], F32)
        bias_f = sb("bias_f", [128, 8], F32)
        bias_w = sb("bias_w", [128, 8], F32)
        w1s_c = sb("w1s_c", [128, 32], F32)
        w2s_c = sb("w2s_c", [128, 32], F32)
        h1s_c = sb("h1s_c", [128, 32], F32)
        h2s_c = sb("h2s_c", [128, 32], F32)
        w3s_c = sb("w3s_c", [128, 8], F32)
        stat = sb("stat", [128, 16], F32)
        ones_nh = sb("ones_nh", [128, 1], BF16)
        ones_b = sb("ones_b", [128, 1], BF16)
        ones_c = sb("ones_c", [128, 1], F32)
        ones_row = sb("ones_row", [1, 128], F32)
        c_eps = sb("c_eps", [128, 1], F32)
        c_nhalf = sb("c_nhalf", [128, 1], F32)

        iw_view = g1_sb.ap().rearrange("p a b -> p (a b)")
        iw_c = [iw_view[:, j * R:(j + 1) * R] for j in range(9)]

        p = [E(nc.psum_tensor(f"pb{i}", [128, 512], F32)) for i in range(8)]

        cT = Ctr(nc, stack, "cT", t)
        cV = Ctr(nc, stack, "cV", v)
        cS = Ctr(nc, stack, "cS", sc)
        d_xts = Ctr(nc, stack, "d_xts", g)
        d_xt = [Ctr(nc, stack, f"d_xt{i}", sy) for i in range(4)]
        d_y = [Ctr(nc, stack, f"d_y{i}", sy) for i in range(8)]
        d_wr = [Ctr(nc, stack, f"d_wr{i}", g) for i in range(5)]
        d_ident = Ctr(nc, stack, "d_ident", sy)
        d_misc = Ctr(nc, stack, "d_misc", sy)
        d_vecj = [Ctr(nc, stack, f"d_vecj{i}", sy) for i in range(6)]
        cast_sem = {}

        waited = {}

        def need(eng, ctr, val, allow_self=False):
            if ctr is None or val <= 0:
                return
            if ctr.eng is eng and not allow_self:
                return
            key = (id(eng), id(ctr.sem))
            if waited.get(key, 0) >= val:
                return
            eng.wait_ge(ctr.sem, val)
            waited[key] = val

        bank_guard = {}

        def guard_bank(i):
            gd = bank_guard.get(i)
            if gd is not None:
                need(t, gd[0], gd[1])

        def V(instr_fn):
            need(v, cV, cV.n, allow_self=True)
            return cV.inc(instr_fn())

        def S(instr_fn):
            need(sc, cS, cS.n, allow_self=True)
            return cS.inc(instr_fn())

        cast_sem["projT"] = Ctr(nc, stack, "cs_projT", g)
        g.dma_start(projT_sb[:, 0:8, :], projT.rearrange("(j p) r -> p j r", p=128))\
            .then_inc(cast_sem["projT"].sem, 16)
        cast_sem["projT"].n = 16
        cast_sem["iwT"] = Ctr(nc, stack, "cs_iwT", g)
        g.dma_start(iw_view[:, 0:8 * R].rearrange("p (j r) -> p j r", j=8),
                    iwT.rearrange("(j p) r -> p j r", p=128)).then_inc(cast_sem["iwT"].sem, 16)
        cast_sem["iwT"].n = 16

        xts_state = {"guard": 0}

        def emit_xts_loads(s_):
            need(g, cV, xts_state["guard"])
            for j in range(8):
                g.dma_start(xt_sweep[:, j, :],
                            xT[j * 128:(j + 1) * 128, s_ * TS:(s_ + 1) * TS])\
                    .then_inc(d_xts.sem, 16)
                d_xts.n += 16

        emit_xts_loads(0)

        srcs = {"w1": w1p, "w2": w2p, "h1": h1p, "h2": h2p, "w3": w3p}

        sy.dma_start(identf[:, :], identf_h.ap()).then_inc(d_ident.sem, 16)
        d_ident.n += 16

        ms_p8 = V(lambda: v.memset(projT_sb[:, 8, :], 0.0))
        V(lambda: v.memset(iw_c[8], 0.0))
        V(lambda: v.memset(xn_fm[:, 8, :], 0.0))
        V(lambda: v.memset(ones_nh[:, :], -0.5))
        V(lambda: v.memset(c_eps[:, :], EPS))
        V(lambda: v.memset(c_nhalf[:, :], -0.5))
        V(lambda: v.memset(ones_b[:, :], 1.0))
        V(lambda: v.memset(ones_c[:, :], 1.0))
        V(lambda: v.memset(ones_row[:, :], 1.0))
        cast_sem["xisrow"] = Ctr(nc, stack, "cs_xisrow", g)
        g.wait_ge(cV.sem, ms_p8)
        g.dma_start(projT_sb[0:1, 8, :], xis[None, :]).then_inc(cast_sem["xisrow"].sem, 16)
        cast_sem["xisrow"].n = 16

        vec_jobs = [
            (xis, 8, ab_col, 8), (w1s, 32, w1s_c, 32), (w2s, 32, w2s_c, 32),
            (h1s, 32, h1s_c, 32), (h2s, 32, h2s_c, 32), (w3s, 8, w3s_c, 8),
        ]
        for vi, (vsrc, nparts, dst, ncols) in enumerate(vec_jobs):
            sy.dma_start(vec_sb[0:nparts, vi, :], vsrc.rearrange("(c p) -> c p", p=128))\
                .then_inc(d_vecj[vi].sem, 16)
            d_vecj[vi].n += 16
        for vi, (vsrc, nparts, dst, ncols) in enumerate(vec_jobs):
            need(t, d_vecj[vi], 16)
            need(t, d_ident, 16)
            bk = 4 + (vi % 4)
            guard_bank(bk)
            vt = cT.inc(t.transpose(p[bk][0:128, 0:ncols], vec_sb[0:nparts, vi, :],
                                    identf[0:nparts, 0:nparts]))
            need(v, cT, vt)
            ev = V(lambda dst=dst, ncols=ncols, bk=bk: v.tensor_copy(dst[:, 0:ncols],
                                                                    p[bk][:, 0:ncols]))
            bank_guard[bk] = (cV, ev)

        V(lambda: v.tensor_scalar_add(bias_w[:, :], ab_col[:, :], -LN32))
        V(lambda: v.tensor_mul(stat[:, 0:8], ab_col[:, :], ab_col[:, :]))
        V(lambda: v.scalar_tensor_tensor(bias_f[:, :], stat[:, 0:8], -512.0,
                                         bias_w[:, :], op0=ALU.mult, op1=ALU.add))
        cols_ready = cV.n

        ring_seq = 0
        slot_last = [0, 0, 0, 0, 0]

        def ring_load(nm, strip, kk8, first):
            nonlocal ring_seq
            slot = ring_seq % 5
            fsrc = srcs[nm][strip] if nm in ("w1", "w2") \
                else srcs[nm][strip][:, kk8 * 8:(kk8 + 1) * 8, :]
            need(g, cT, slot_last[slot])
            g.dma_start(wring[:, slot, :, :], fsrc).then_inc(d_wr[slot].sem, 16)
            d_wr[slot].n += 16
            ring_seq += 1
            return slot, d_wr[slot], d_wr[slot].n

        xsq = feats
        ln_state = {}

        def ln2_squares(s_, smm_last):
            need(v, cT, smm_last)
            need(v, d_xts, d_xts.n)
            for j in range(8):
                V(lambda j=j: v.tensor_mul(xsq[:, j, :], xt_sweep[:, j, :],
                                           xt_sweep[:, j, :]))
            ln_state["sq"] = cV.n

        def ln2_stats(s_):
            need(t, cV, ln_state["sq"])
            guard_bank(6)
            guard_bank(7)
            for j in range(8):
                t.matmul(p[6][0:1, :], ones_b[:, 0:1], xt_sweep[:, j, :],
                         start=(j == 0), stop=(j == 7))
                ii = t.matmul(p[7][0:1, :], ones_b[:, 0:1], xsq[:, j, :],
                              start=(j == 0), stop=(j == 7))
            st = cT.inc(ii)
            need(v, cT, st)
            V(lambda: v.tensor_copy(rows[0:1, 0, :], p[6][0:1, :]))
            ev = V(lambda: v.tensor_copy(rows[0:1, 1, :], p[7][0:1, :]))
            bank_guard[6] = (cV, ev)
            bank_guard[7] = (cV, ev)
            V(lambda: v.tensor_scalar_mul(rows[0:1, 0, :], rows[0:1, 0, :], 1.0 / D))
            V(lambda: v.tensor_scalar_mul(rows[0:1, 1, :], rows[0:1, 1, :], 1.0 / D))
            V(lambda: v.tensor_mul(rows[0:1, 2, :], rows[0:1, 0, :], rows[0:1, 0, :]))
            sv = V(lambda: v.tensor_sub(rows[0:1, 1, :], rows[0:1, 1, :], rows[0:1, 2, :]))
            need(sc, cV, sv)
            S(lambda: sc.activation(rows[0:1, 2, :], rows[0:1, 1, :], AF.Ln,
                                    bias=c_eps[0:1, 0:1]))
            rs = S(lambda: sc.activation(rows[0:1, 2, :], rows[0:1, 2, :], AF.Exp,
                                         scale=c_nhalf[0:1, 0:1]))
            need(v, cS, rs)
            V(lambda: v.tensor_mul(rows[0:1, 3, :], rows[0:1, 0, :], rows[0:1, 2, :]))
            rd = V(lambda: v.scalar_tensor_tensor(rows[0:1, 1, :], rows[0:1, 2, :],
                                                  512.0 * EPS, rows[0:1, 2, :],
                                                  op0=ALU.mult, op1=ALU.mult))
            ln_state["rows"] = rd

        def ln2_bcast_affine(s_, ffn1_last):
            need(t, cV, ln_state["rows"])
            guard_bank(4)
            guard_bank(5)
            bc1 = cT.inc(t.matmul(p[4][:, :], ones_row[0:1, :], rows[0:1, 2, :],
                                  start=True, stop=True))
            bc2 = cT.inc(t.matmul(p[5][:, :], ones_row[0:1, :], rows[0:1, 3, :],
                                  start=True, stop=True))
            need(v, cT, bc2)
            need(v, cT, ffn1_last)
            for j in range(8):
                V(lambda j=j: v.tensor_mul(xn_fm[:, j, :], xt_sweep[:, j, :], p[4][:, :]))
                V(lambda j=j: v.tensor_sub(xn_fm[:, j, :], xn_fm[:, j, :], p[5][:, :]))
            xr = V(lambda: v.tensor_copy(xn_fm[0:1, 8, :], rows[0:1, 1, :]))
            bank_guard[4] = (cV, xr)
            bank_guard[5] = (cV, xr)
            xts_state["guard"] = xr
            return xr

        ffn4_done_T = 0
        t1_guard = {}
        t2_guard = {}
        y_val = {}
        xt_guard = {}

        need(v, cast_sem["iwT"], 16)
        sq_done = 0
        for j in range(8):
            sq_done = V(lambda j=j: v.tensor_mul(
                h_sb[:, 2 * j:2 * j + 2, :].rearrange("p a b -> p (a b)"),
                iw_c[j], iw_c[j]))

        ln2_squares(0, 0)

        wstate = {}

        def emit_wphase():
            need(t, cV, sq_done)
            need(t, cast_sem["iwT"], 16)
            guard_bank(4)
            guard_bank(5)
            for j in range(8):
                sqv = h_sb[:, 2 * j:2 * j + 2, :].rearrange("p a b -> p (a b)")
                t.matmul(p[4][0:1, :], ones_nh[:, 0:1], sqv[:, 0:512],
                         start=(j == 0), stop=(j == 7))
                ii = t.matmul(p[5][0:1, :], ones_nh[:, 0:1], sqv[:, 512:1024],
                              start=(j == 0), stop=(j == 7))
            so_done = cT.inc(ii)
            wstate["so_mm"] = so_done
            need(sc, cT, so_done)
            e1 = S(lambda: sc.copy(iw_c[8][0:1, 0:512], p[4][0:1, :]))
            e2 = S(lambda: sc.copy(iw_c[8][0:1, 512:1024], p[5][0:1, :]))
            bank_guard[4] = (cS, e1)
            bank_guard[5] = (cS, e2)
            need(t, cast_sem["projT"], 16)
            need(t, cast_sem["xisrow"], 16)
            need(t, cS, e2)
            need(t, cV, cols_ready)
            ev = 0
            for rc in range(8):
                for oh in range(2):
                    bk = 4 + oh
                    guard_bank(bk)
                    for j in range(9):
                        ii = t.matmul(p[bk][:, :], projT_sb[:, j, rc * 128:(rc + 1) * 128],
                                      iw_c[j][:, oh * 512:(oh + 1) * 512],
                                      start=(j == 0), stop=(j == 8))
                    stop = cT.inc(ii)
                    need(sc, cT, stop)
                    ev = S(lambda rc=rc, oh=oh, bk=bk: sc.activation(
                        Wt_sb[:, rc, oh * 512:(oh + 1) * 512], p[bk][:, :], AF.Exp,
                        bias=bias_w[:, rc:rc + 1], scale=ab_col[:, rc:rc + 1]))
                    bank_guard[bk] = (cS, ev)
            wstate["wt_done"] = ev

        ln2_stats(0)
        emit_wphase()
        xn_ready_sweep = ln2_bcast_affine(0, 0)

        def emit_dot0(xn_rdy):
            need(t, cast_sem["projT"], 16)
            need(t, cast_sem["xisrow"], 16)
            need(t, cV, xn_rdy)
            need(t, cV, cols_ready)
            for rc in range(8):
                bk = (6, 7, 4, 5)[rc % 4]
                guard_bank(bk)
                for j in range(9):
                    ii = t.matmul(p[bk][:, :], projT_sb[:, j, rc * 128:(rc + 1) * 128],
                                  xn_fm[:, j, :], start=(j == 0), stop=(j == 8))
                stop = cT.inc(ii)
                need(sc, cT, stop)
                ev = S(lambda rc=rc, bk=bk: sc.activation(
                    feats[:, rc, :], p[bk][:, :], AF.Exp,
                    bias=bias_f[:, rc:rc + 1], scale=ab_col[:, rc:rc + 1]))
                bank_guard[bk] = (cS, ev)
            return cS.n

        def emit_smm(feats_done):
            need(t, cS, feats_done)
            need(t, cS, wstate["wt_done"])
            for oc in range(8):
                bk = oc
                guard_bank(bk)
                for rc in range(8):
                    ii = t.matmul(p[bk][:, :], Wt_sb[:, rc, oc * 128:(oc + 1) * 128],
                                  feats[:, rc, :], start=(rc == 0), stop=(rc == 7))
                stop = cT.inc(ii)
                need(v, cT, stop)
                need(v, d_y[oc], y_val.get(oc, 0))
                ev = V(lambda oc=oc, bk=bk: v.tensor_copy(s_sb[:, oc, :], p[bk][:, :]))
                bank_guard[bk] = (cV, ev)
            return cT.n

        smm_next = 0
        feats_next = 0
        for s_ in range(NS):
            tok0 = s_ * TS

            h_by_strip = {}

            def ffn1_strip(ss):
                slot1, wc1, wv1 = ring_load("w1", ss, 0, s_ == 0)
                slot2, wc2, wv2 = ring_load("w2", ss, 0, s_ == 0)
                for hh in range(2):
                    ba1 = 0 if hh == 0 else 4
                    ba2 = 2 if hh == 0 else 6
                    need(t, wc1, wv1)
                    for og in range(2):
                        guard_bank(ba1 + og)
                    for k in range(8):
                        for og in range(2):
                            ii = t.matmul(p[ba1 + og][:, :],
                                          wring[:, slot1, k, (hh * 2 + og) * 128:(hh * 2 + og + 1) * 128],
                                          xn_fm[:, k, :], start=(k == 0), stop=(k == 7))
                    a1_stop = cT.inc(ii)
                    if hh == 1:
                        slot_last[slot1] = max(slot_last[slot1], a1_stop)
                    need(sc, cT, a1_stop)
                    last_silu = 0
                    for og in range(2):
                        oc = ss * 4 + hh * 2 + og
                        sl = hh * 2 + og
                        tg = t1_guard.get(sl)
                        if tg:
                            need(sc, cV, tg)
                        ev = S(lambda sl=sl, og=og, oc=oc, ba1=ba1: sc.activation(
                            t1[:, sl, :], p[ba1 + og][:, :], AF.Silu, scale=w1s_c[:, oc:oc + 1]))
                        bank_guard[ba1 + og] = (cS, ev)
                        last_silu = ev
                    need(t, wc2, wv2)
                    for og in range(2):
                        guard_bank(ba2 + og)
                    for k in range(8):
                        for og in range(2):
                            ii = t.matmul(p[ba2 + og][:, :],
                                          wring[:, slot2, k, (hh * 2 + og) * 128:(hh * 2 + og + 1) * 128],
                                          xn_fm[:, k, :], start=(k == 0), stop=(k == 7))
                    a2_stop = cT.inc(ii)
                    if hh == 1:
                        slot_last[slot2] = max(slot_last[slot2], a2_stop)
                    need(v, cT, a2_stop)
                    need(v, cS, last_silu)
                    need(v, cT, ffn4_done_T)
                    if s_ == 0:
                        need(v, cT, wstate["so_mm"])
                    for og in range(2):
                        oc = ss * 4 + hh * 2 + og
                        sl = hh * 2 + og
                        ev = V(lambda oc=oc, og=og, sl=sl, ba2=ba2: v.scalar_tensor_tensor(
                            h_sb[:, oc, :], p[ba2 + og][:, :], w2s_c[:, oc:oc + 1],
                            t1[:, sl, :], op0=ALU.mult, op1=ALU.mult))
                        bank_guard[ba2 + og] = (cV, ev)
                        t1_guard[sl] = ev
                h_by_strip[ss] = cV.n

            if s_ == 0:
                feats_done = emit_dot0(xn_ready_sweep)
                ffn1_strip(0)
                ffn1_strip(1)
                smm_last = emit_smm(feats_done)
                first_strip = 2
            else:
                smm_last = smm_next
                first_strip = 0

            for ss in range(first_strip, 8):
                ffn1_strip(ss)
                if ss == 2 and s_ + 1 < NS:
                    emit_xts_loads(s_ + 1)
                if ss == 3 and s_ + 1 < NS:
                    ln2_squares(s_ + 1, smm_last)
                if ss == 5 and s_ + 1 < NS:
                    ln2_stats(s_ + 1)
            ffn1_last = cT.n

            if s_ + 1 < NS:
                xn_ready_sweep = ln2_bcast_affine(s_ + 1, ffn1_last)

            g1_by_strip = {}
            g2_by_strip = {}
            for nm, src_sb, dst_sb, s_col, src_rdy, dst_rec in (
                    ("h1", h_sb, g1_sb, h1s_c, (cV, h_by_strip), g1_by_strip),
                    ("h2", g1_sb, h_sb, h2s_c, (cS, g1_by_strip), g2_by_strip)):
                rdy_ctr, rdy_map = src_rdy
                for ss in range(8):
                    quad = [0, 1, 2, 3] if ss % 2 == 0 else [4, 5, 6, 7]
                    for bi in quad:
                        guard_bank(bi)
                    for kk in range(4):
                        need(t, rdy_ctr, rdy_map[2 * kk + 1])
                        slot, wc, wv = ring_load(nm, ss, kk, s_ == 0)
                        need(t, wc, wv)
                        for k8 in range(8):
                            for og in range(4):
                                ii = t.matmul(p[quad[og]][:, :],
                                              wring[:, slot, k8, og * 128:(og + 1) * 128],
                                              src_sb[:, kk * 8 + k8, :],
                                              start=(kk == 0 and k8 == 0),
                                              stop=(kk == 3 and k8 == 7))
                        slot_last[slot] = cT.inc(ii)
                    stop = slot_last[slot]
                    need(sc, cT, stop)
                    for og in range(4):
                        oc = ss * 4 + og
                        ev = S(lambda dst_sb=dst_sb, oc=oc, og=og, quad=quad, s_col=s_col:
                               sc.activation(dst_sb[:, oc, :], p[quad[og]][:, :], AF.Gelu,
                                             scale=s_col[:, oc:oc + 1]))
                        bank_guard[quad[og]] = (cS, ev)
                    dst_rec[ss] = cS.n

            if s_ + 1 < NS:
                feats_next = emit_dot0(xn_ready_sweep)

            xt_val = {}

            def xt_load(dc):
                slot = dc % 4
                need(sy, cV, xt_guard.get(slot, 0))
                sy.dma_start(xt_r[:, slot, :],
                             xT[dc * 128:(dc + 1) * 128, tok0:tok0 + TS])\
                    .then_inc(d_xt[slot].sem, 16)
                d_xt[slot].n += 16
                xt_val[dc] = d_xt[slot].n

            for dc in range(4):
                xt_load(dc)

            sg_done = [0] * 8
            for ss in range(2):
                quad = [0, 1, 2, 3] if ss == 0 else [4, 5, 2, 3]
                for bi in quad:
                    guard_bank(bi)
                for kk in range(4):
                    need(t, cS, g2_by_strip[2 * kk + 1])
                    slot, wc, wv = ring_load("w3", ss, kk, s_ == 0)
                    need(t, wc, wv)
                    for k8 in range(8):
                        for og in range(4):
                            ii = t.matmul(p[quad[og]][:, :],
                                          wring[:, slot, k8, og * 128:(og + 1) * 128],
                                          h_sb[:, kk * 8 + k8, :],
                                          start=(kk == 0 and k8 == 0),
                                          stop=(kk == 3 and k8 == 7))
                    slot_last[slot] = cT.inc(ii)
                ffn4_done_T = slot_last[slot]
                need(sc, cT, ffn4_done_T)
                for og in range(4):
                    dc = ss * 4 + og
                    tg = t2_guard.get(og)
                    if tg:
                        need(sc, cV, tg)
                    ev = S(lambda og=og, dc=dc, quad=quad: sc.activation(
                        t2[:, og, :], p[quad[og]][:, :], AF.Silu,
                        scale=w3s_c[:, dc:dc + 1]))
                    bank_guard[quad[og]] = (cS, ev)
                for og in range(4):
                    dc = ss * 4 + og
                    need(v, cS, bank_guard[quad[og]][1] if bank_guard[quad[og]][0] is cS else 0)
                    mv = V(lambda og=og, dc=dc: v.tensor_mul(
                        s_sb[:, dc, :], t2[:, og, :], s_sb[:, dc, :]))
                    t2_guard[og] = mv
                    sg_done[dc] = mv

            for dc in range(8):
                slot = dc % 4
                need(v, cV, sg_done[dc])
                need(v, d_xt[slot], xt_val[dc])
                av = V(lambda dc=dc, slot=slot: v.tensor_add(
                    s_sb[:, dc, :], xt_r[:, slot, :], s_sb[:, dc, :]))
                xt_guard[slot] = av
                if dc + 4 < 8:
                    xt_load(dc + 4)
                need(sy, cV, av)
                sy.dma_start(y[dc * 128:(dc + 1) * 128, tok0:tok0 + TS],
                             s_sb[:, dc, :]).then_inc(d_y[dc].sem, 16)
                d_y[dc].n += 16
                y_val[dc] = d_y[dc].n

            if s_ + 1 < NS:
                smm_next = emit_smm(feats_next)

        for slot in range(8):
            g.wait_ge(d_y[slot].sem, d_y[slot].n)
    return nc


def _pack(w):
    O, I = w.shape
    wt = np.ascontiguousarray(w.T)
    arr = wt.reshape(I // 128, 128, O // 512, 512)
    return np.ascontiguousarray(arr.transpose(2, 1, 0, 3))


_NC_CACHE = {}


def kernel(L0_attn_x, L0_attn_input_weights, L0_attn_xis, L0_attn_proj_matrix,
           L0_ffn_w1_w, L0_ffn_w1_s, L0_ffn_w2_w, L0_ffn_w2_s,
           L0_ffn_h1_w, L0_ffn_h1_s, L0_ffn_h2_w, L0_ffn_h2_s,
           L0_ffn_w3_w, L0_ffn_w3_s):
    x = np.ascontiguousarray(np.asarray(L0_attn_x, dtype=np.float32)).reshape(-1, D)
    shared = {
        "projT": np.ascontiguousarray(np.asarray(L0_attn_proj_matrix, np.float32).T),
        "iwT": np.ascontiguousarray(np.asarray(L0_attn_input_weights, np.float32).T),
        "xis": np.ascontiguousarray(np.asarray(L0_attn_xis, np.float32)),
        "w1p": _pack(np.asarray(L0_ffn_w1_w, np.float32)),
        "w2p": _pack(np.asarray(L0_ffn_w2_w, np.float32)),
        "h1p": _pack(np.asarray(L0_ffn_h1_w, np.float32)),
        "h2p": _pack(np.asarray(L0_ffn_h2_w, np.float32)),
        "w3p": _pack(np.asarray(L0_ffn_w3_w, np.float32)),
        "w1s": np.ascontiguousarray(np.asarray(L0_ffn_w1_s, np.float32)),
        "w2s": np.ascontiguousarray(np.asarray(L0_ffn_w2_s, np.float32)),
        "h1s": np.ascontiguousarray(np.asarray(L0_ffn_h1_s, np.float32)),
        "h2s": np.ascontiguousarray(np.asarray(L0_ffn_h2_s, np.float32)),
        "w3s": np.ascontiguousarray(np.asarray(L0_ffn_w3_s, np.float32)),
    }
    if "nc" not in _NC_CACHE:
        _NC_CACHE["nc"] = build_nc()
    nc = _NC_CACHE["nc"]
    in_maps = []
    for c in range(8):
        xc = np.ascontiguousarray(x[c * T:(c + 1) * T])
        m = {"xT": np.ascontiguousarray(xc.T)}
        m.update(shared)
        in_maps.append(m)
    res = run_bass_kernel_spmd(nc, in_maps, core_ids=list(range(8)))
    out = np.concatenate([np.ascontiguousarray(res.results[c]["y"].T) for c in range(8)], axis=0)
    return out.reshape(4, 4096, D)


if __name__ == "__main__":
    nc = build_nc()
    print("built ok, instructions:", nc.n_instructions())


# version 28
# speedup vs baseline: 1.0216x; 1.0027x over previous
import contextlib
import numpy as np
import ml_dtypes
import concourse.bass as bass
import concourse.mybir as mybir
from concourse.bass_utils import run_bass_kernel_spmd

F32 = mybir.dt.float32
BF16 = mybir.dt.bfloat16
AF = mybir.ActivationFunctionType
ALU = mybir.AluOpType
AX = mybir.AxisListType
bf16 = ml_dtypes.bfloat16

T = 2048
TS = 512
NS = T // TS
NT = TS // 128
D = 1024
H = 4096
R = 1024
LN32 = float(np.log(np.sqrt(1024.0)))
EPS = 1e-5


class Ctr:
    def __init__(self, nc, stack, name, eng):
        self.sem = stack.enter_context(nc.semaphore(name))
        self.n = 0
        self.eng = eng

    def inc(self, instr, k=1):
        instr.then_inc(self.sem, k)
        self.n += k
        return self.n


def build_nc():
    nc = bass.Bass("TRN2", target_bir_lowering=False)
    g = nc.gpsimd
    sy = nc.sync
    t = nc.tensor
    v = nc.vector
    sc = nc.scalar

    xT = nc.dram_tensor("xT", [D, T], F32, kind="ExternalInput").ap()
    projT = nc.dram_tensor("projT", [D, R], F32, kind="ExternalInput").ap()
    iwT = nc.dram_tensor("iwT", [D, R], F32, kind="ExternalInput").ap()
    xis = nc.dram_tensor("xis", [R], F32, kind="ExternalInput").ap()
    w1p = nc.dram_tensor("w1p", [8, 128, 8, 512], F32, kind="ExternalInput").ap()
    w2p = nc.dram_tensor("w2p", [8, 128, 8, 512], F32, kind="ExternalInput").ap()
    h1p = nc.dram_tensor("h1p", [8, 128, 32, 512], F32, kind="ExternalInput").ap()
    h2p = nc.dram_tensor("h2p", [8, 128, 32, 512], F32, kind="ExternalInput").ap()
    w3p = nc.dram_tensor("w3p", [2, 128, 32, 512], F32, kind="ExternalInput").ap()
    w1s = nc.dram_tensor("w1s", [H], F32, kind="ExternalInput").ap()
    w2s = nc.dram_tensor("w2s", [H], F32, kind="ExternalInput").ap()
    h1s = nc.dram_tensor("h1s", [H], F32, kind="ExternalInput").ap()
    h2s = nc.dram_tensor("h2s", [H], F32, kind="ExternalInput").ap()
    w3s = nc.dram_tensor("w3s", [D], F32, kind="ExternalInput").ap()
    y = nc.dram_tensor("y", [D, T], F32, kind="ExternalOutput").ap()

    identf_h = nc.inline_tensor(np.eye(128).astype(np.float32), "ident_f")

    with contextlib.ExitStack() as stack:
        E = stack.enter_context

        def sb(name, shape, dtype):
            return E(nc.sbuf_tensor(name, shape, dtype))

        projT_sb = sb("projT_sb", [128, 9, R], BF16)
        Wt_sb = sb("Wt_sb", [128, 8, R], BF16)
        xt_sweep = sb("xt_sweep", [128, 8, TS], BF16)
        xt_r = sb("xt_r", [128, 4, 512], F32)
        rows = sb("rows", [128, 4, TS], F32)
        xn_fm = sb("xn_fm", [128, 9, TS], BF16)
        feats = sb("feats", [128, 8, TS], BF16)
        s_sb = sb("s_sb", [128, 8, TS], F32)
        h_sb = sb("h_sb", [128, 32, TS], BF16)
        g1_sb = sb("g1_sb", [128, 32, TS], BF16)
        wring = sb("wring", [128, 5, 8, TS], BF16)
        t1 = sb("t1", [128, 4, TS], BF16)
        t2 = sb("t2", [128, 4, TS], BF16)
        identf = sb("identf", [128, 128], F32)
        vec_sb = sb("vec_sb", [32, 6, 128], F32)
        ab_col = sb("ab_col", [128, 8], F32)
        bias_f = sb("bias_f", [128, 8], F32)
        bias_w = sb("bias_w", [128, 8], F32)
        w1s_c = sb("w1s_c", [128, 32], F32)
        w2s_c = sb("w2s_c", [128, 32], F32)
        h1s_c = sb("h1s_c", [128, 32], F32)
        h2s_c = sb("h2s_c", [128, 32], F32)
        w3s_c = sb("w3s_c", [128, 8], F32)
        stat = sb("stat", [128, 16], F32)
        ones_nh = sb("ones_nh", [128, 1], BF16)
        ones_b = sb("ones_b", [128, 1], BF16)
        ones_c = sb("ones_c", [128, 1], F32)
        ones_row = sb("ones_row", [1, 128], F32)
        c_eps = sb("c_eps", [128, 1], F32)
        c_nhalf = sb("c_nhalf", [128, 1], F32)

        iw_view = g1_sb.ap().rearrange("p a b -> p (a b)")
        iw_c = [iw_view[:, j * R:(j + 1) * R] for j in range(9)]

        p = [E(nc.psum_tensor(f"pb{i}", [128, 512], F32)) for i in range(8)]

        cT = Ctr(nc, stack, "cT", t)
        cV = Ctr(nc, stack, "cV", v)
        cS = Ctr(nc, stack, "cS", sc)
        d_xts = Ctr(nc, stack, "d_xts", g)
        d_xt = [Ctr(nc, stack, f"d_xt{i}", sy) for i in range(4)]
        d_y = [Ctr(nc, stack, f"d_y{i}", sy) for i in range(8)]
        d_wr = [Ctr(nc, stack, f"d_wr{i}", g) for i in range(5)]
        d_ident = Ctr(nc, stack, "d_ident", sy)
        d_misc = Ctr(nc, stack, "d_misc", sy)
        d_vecj = [Ctr(nc, stack, f"d_vecj{i}", sy) for i in range(6)]
        cast_sem = {}

        waited = {}

        def need(eng, ctr, val, allow_self=False):
            if ctr is None or val <= 0:
                return
            if ctr.eng is eng and not allow_self:
                return
            key = (id(eng), id(ctr.sem))
            if waited.get(key, 0) >= val:
                return
            eng.wait_ge(ctr.sem, val)
            waited[key] = val

        bank_guard = {}

        def guard_bank(i):
            gd = bank_guard.get(i)
            if gd is not None:
                need(t, gd[0], gd[1])

        def V(instr_fn):
            need(v, cV, cV.n, allow_self=True)
            return cV.inc(instr_fn())

        def S(instr_fn):
            need(sc, cS, cS.n, allow_self=True)
            return cS.inc(instr_fn())

        cast_sem["projT"] = Ctr(nc, stack, "cs_projT", g)
        g.dma_start(projT_sb[:, 0:8, :], projT.rearrange("(j p) r -> p j r", p=128))\
            .then_inc(cast_sem["projT"].sem, 16)
        cast_sem["projT"].n = 16
        cast_sem["iwT"] = Ctr(nc, stack, "cs_iwT", g)
        g.dma_start(iw_view[:, 0:8 * R].rearrange("p (j r) -> p j r", j=8),
                    iwT.rearrange("(j p) r -> p j r", p=128)).then_inc(cast_sem["iwT"].sem, 16)
        cast_sem["iwT"].n = 16

        xts_state = {"guard": 0}

        def emit_xts_loads(s_):
            need(g, cV, xts_state["guard"])
            for j in range(8):
                g.dma_start(xt_sweep[:, j, :],
                            xT[j * 128:(j + 1) * 128, s_ * TS:(s_ + 1) * TS])\
                    .then_inc(d_xts.sem, 16)
                d_xts.n += 16

        emit_xts_loads(0)

        srcs = {"w1": w1p, "w2": w2p, "h1": h1p, "h2": h2p, "w3": w3p}

        sy.dma_start(identf[:, :], identf_h.ap()).then_inc(d_ident.sem, 16)
        d_ident.n += 16

        ms_p8 = V(lambda: v.memset(projT_sb[:, 8, :], 0.0))
        V(lambda: v.memset(iw_c[8], 0.0))
        V(lambda: v.memset(xn_fm[:, 8, :], 0.0))
        V(lambda: v.memset(ones_nh[:, :], -0.5))
        V(lambda: v.memset(c_eps[:, :], EPS))
        V(lambda: v.memset(c_nhalf[:, :], -0.5))
        V(lambda: v.memset(ones_b[:, :], 1.0))
        V(lambda: v.memset(ones_c[:, :], 1.0))
        V(lambda: v.memset(ones_row[:, :], 1.0))
        cast_sem["xisrow"] = Ctr(nc, stack, "cs_xisrow", g)
        g.wait_ge(cV.sem, ms_p8)
        g.dma_start(projT_sb[0:1, 8, :], xis[None, :]).then_inc(cast_sem["xisrow"].sem, 16)
        cast_sem["xisrow"].n = 16

        vec_jobs = [
            (xis, 8, ab_col, 8), (w1s, 32, w1s_c, 32), (w2s, 32, w2s_c, 32),
            (h1s, 32, h1s_c, 32), (h2s, 32, h2s_c, 32), (w3s, 8, w3s_c, 8),
        ]
        for vi, (vsrc, nparts, dst, ncols) in enumerate(vec_jobs):
            sy.dma_start(vec_sb[0:nparts, vi, :], vsrc.rearrange("(c p) -> c p", p=128))\
                .then_inc(d_vecj[vi].sem, 16)
            d_vecj[vi].n += 16
        for vi, (vsrc, nparts, dst, ncols) in enumerate(vec_jobs):
            need(t, d_vecj[vi], 16)
            need(t, d_ident, 16)
            bk = 4 + (vi % 4)
            guard_bank(bk)
            vt = cT.inc(t.transpose(p[bk][0:128, 0:ncols], vec_sb[0:nparts, vi, :],
                                    identf[0:nparts, 0:nparts]))
            need(v, cT, vt)
            ev = V(lambda dst=dst, ncols=ncols, bk=bk: v.tensor_copy(dst[:, 0:ncols],
                                                                    p[bk][:, 0:ncols]))
            bank_guard[bk] = (cV, ev)

        V(lambda: v.tensor_scalar_add(bias_w[:, :], ab_col[:, :], -LN32))
        V(lambda: v.tensor_mul(stat[:, 0:8], ab_col[:, :], ab_col[:, :]))
        V(lambda: v.scalar_tensor_tensor(bias_f[:, :], stat[:, 0:8], -512.0,
                                         bias_w[:, :], op0=ALU.mult, op1=ALU.add))
        cols_ready = cV.n

        ring_seq = 0
        slot_last = [0, 0, 0, 0, 0]

        def ring_load(nm, strip, kk8, first):
            nonlocal ring_seq
            slot = ring_seq % 5
            fsrc = srcs[nm][strip] if nm in ("w1", "w2") \
                else srcs[nm][strip][:, kk8 * 8:(kk8 + 1) * 8, :]
            need(g, cT, slot_last[slot])
            g.dma_start(wring[:, slot, :, :], fsrc).then_inc(d_wr[slot].sem, 16)
            d_wr[slot].n += 16
            ring_seq += 1
            return slot, d_wr[slot], d_wr[slot].n

        xsq = feats
        ln_state = {}

        def ln2_squares(s_, smm_last):
            need(v, cT, smm_last)
            need(v, d_xts, d_xts.n)
            for j in range(8):
                V(lambda j=j: v.tensor_mul(xsq[:, j, :], xt_sweep[:, j, :],
                                           xt_sweep[:, j, :]))
            ln_state["sq"] = cV.n

        def ln2_stats(s_):
            need(t, cV, ln_state["sq"])
            guard_bank(6)
            guard_bank(7)
            for j in range(8):
                t.matmul(p[6][0:1, :], ones_b[:, 0:1], xt_sweep[:, j, :],
                         start=(j == 0), stop=(j == 7))
                ii = t.matmul(p[7][0:1, :], ones_b[:, 0:1], xsq[:, j, :],
                              start=(j == 0), stop=(j == 7))
            st = cT.inc(ii)
            need(v, cT, st)
            V(lambda: v.tensor_copy(rows[0:1, 0, :], p[6][0:1, :]))
            ev = V(lambda: v.tensor_copy(rows[0:1, 1, :], p[7][0:1, :]))
            bank_guard[6] = (cV, ev)
            bank_guard[7] = (cV, ev)
            V(lambda: v.tensor_scalar_mul(rows[0:1, 0, :], rows[0:1, 0, :], 1.0 / D))
            V(lambda: v.tensor_scalar_mul(rows[0:1, 1, :], rows[0:1, 1, :], 1.0 / D))
            V(lambda: v.tensor_mul(rows[0:1, 2, :], rows[0:1, 0, :], rows[0:1, 0, :]))
            sv = V(lambda: v.tensor_sub(rows[0:1, 1, :], rows[0:1, 1, :], rows[0:1, 2, :]))
            need(sc, cV, sv)
            S(lambda: sc.activation(rows[0:1, 2, :], rows[0:1, 1, :], AF.Ln,
                                    bias=c_eps[0:1, 0:1]))
            rs = S(lambda: sc.activation(rows[0:1, 2, :], rows[0:1, 2, :], AF.Exp,
                                         scale=c_nhalf[0:1, 0:1]))
            need(v, cS, rs)
            V(lambda: v.tensor_mul(rows[0:1, 3, :], rows[0:1, 0, :], rows[0:1, 2, :]))
            rd = V(lambda: v.scalar_tensor_tensor(rows[0:1, 1, :], rows[0:1, 2, :],
                                                  512.0 * EPS, rows[0:1, 2, :],
                                                  op0=ALU.mult, op1=ALU.mult))
            ln_state["rows"] = rd

        def ln2_bcast_affine(s_, ffn1_last):
            need(t, cV, ln_state["rows"])
            guard_bank(4)
            guard_bank(5)
            bc1 = cT.inc(t.matmul(p[4][:, :], ones_row[0:1, :], rows[0:1, 2, :],
                                  start=True, stop=True))
            bc2 = cT.inc(t.matmul(p[5][:, :], ones_row[0:1, :], rows[0:1, 3, :],
                                  start=True, stop=True))
            need(v, cT, bc2)
            need(v, cT, ffn1_last)
            for j in range(8):
                V(lambda j=j: v.tensor_mul(xn_fm[:, j, :], xt_sweep[:, j, :], p[4][:, :]))
                V(lambda j=j: v.tensor_sub(xn_fm[:, j, :], xn_fm[:, j, :], p[5][:, :]))
            xr = V(lambda: v.tensor_copy(xn_fm[0:1, 8, :], rows[0:1, 1, :]))
            bank_guard[4] = (cV, xr)
            bank_guard[5] = (cV, xr)
            xts_state["guard"] = xr
            return xr

        ffn4_done_T = 0
        t1_guard = {}
        t2_guard = {}
        y_val = {}
        xt_guard = {}

        need(v, cast_sem["iwT"], 16)
        sq_done = 0
        for j in range(8):
            sq_done = V(lambda j=j: v.tensor_mul(
                h_sb[:, 2 * j:2 * j + 2, :].rearrange("p a b -> p (a b)"),
                iw_c[j], iw_c[j]))

        ln2_squares(0, 0)

        wstate = {}

        def emit_wphase():
            need(t, cV, sq_done)
            need(t, cast_sem["iwT"], 16)
            guard_bank(4)
            guard_bank(5)
            for j in range(8):
                sqv = h_sb[:, 2 * j:2 * j + 2, :].rearrange("p a b -> p (a b)")
                t.matmul(p[4][0:1, :], ones_nh[:, 0:1], sqv[:, 0:512],
                         start=(j == 0), stop=(j == 7))
                ii = t.matmul(p[5][0:1, :], ones_nh[:, 0:1], sqv[:, 512:1024],
                              start=(j == 0), stop=(j == 7))
            so_done = cT.inc(ii)
            wstate["so_mm"] = so_done
            need(sc, cT, so_done)
            e1 = S(lambda: sc.copy(iw_c[8][0:1, 0:512], p[4][0:1, :]))
            e2 = S(lambda: sc.copy(iw_c[8][0:1, 512:1024], p[5][0:1, :]))
            bank_guard[4] = (cS, e1)
            bank_guard[5] = (cS, e2)
            need(t, cast_sem["projT"], 16)
            need(t, cast_sem["xisrow"], 16)
            need(t, cS, e2)
            need(t, cV, cols_ready)
            ev = 0
            for rc in range(8):
                for oh in range(2):
                    bk = 4 + oh
                    guard_bank(bk)
                    for j in range(9):
                        ii = t.matmul(p[bk][:, :], projT_sb[:, j, rc * 128:(rc + 1) * 128],
                                      iw_c[j][:, oh * 512:(oh + 1) * 512],
                                      start=(j == 0), stop=(j == 8))
                    stop = cT.inc(ii)
                    need(sc, cT, stop)
                    ev = S(lambda rc=rc, oh=oh, bk=bk: sc.activation(
                        Wt_sb[:, rc, oh * 512:(oh + 1) * 512], p[bk][:, :], AF.Exp,
                        bias=bias_w[:, rc:rc + 1], scale=ab_col[:, rc:rc + 1]))
                    bank_guard[bk] = (cS, ev)
            wstate["wt_done"] = ev

        ln2_stats(0)
        emit_wphase()
        xn_ready_sweep = ln2_bcast_affine(0, 0)

        def emit_dot0(xn_rdy):
            need(t, cast_sem["projT"], 16)
            need(t, cast_sem["xisrow"], 16)
            need(t, cV, xn_rdy)
            need(t, cV, cols_ready)
            for rc in range(8):
                bk = (6, 7, 4, 5)[rc % 4]
                guard_bank(bk)
                for j in range(9):
                    ii = t.matmul(p[bk][:, :], projT_sb[:, j, rc * 128:(rc + 1) * 128],
                                  xn_fm[:, j, :], start=(j == 0), stop=(j == 8))
                stop = cT.inc(ii)
                need(sc, cT, stop)
                ev = S(lambda rc=rc, bk=bk: sc.activation(
                    feats[:, rc, :], p[bk][:, :], AF.Exp,
                    bias=bias_f[:, rc:rc + 1], scale=ab_col[:, rc:rc + 1]))
                bank_guard[bk] = (cS, ev)
            return cS.n

        def emit_smm(feats_done):
            need(t, cS, feats_done)
            need(t, cS, wstate["wt_done"])
            for oc in range(8):
                bk = oc
                guard_bank(bk)
                for rc in range(8):
                    ii = t.matmul(p[bk][:, :], Wt_sb[:, rc, oc * 128:(oc + 1) * 128],
                                  feats[:, rc, :], start=(rc == 0), stop=(rc == 7))
                stop = cT.inc(ii)
                need(v, cT, stop)
                need(v, d_y[oc], y_val.get(oc, 0))
                ev = V(lambda oc=oc, bk=bk: v.tensor_copy(s_sb[:, oc, :], p[bk][:, :]))
                bank_guard[bk] = (cV, ev)
            return cT.n

        smm_next = 0
        feats_next = 0
        for s_ in range(NS):
            tok0 = s_ * TS

            h_by_strip = {}

            def ffn1_strip(ss):
                slot1, wc1, wv1 = ring_load("w1", ss, 0, s_ == 0)
                slot2, wc2, wv2 = ring_load("w2", ss, 0, s_ == 0)
                for hh in range(2):
                    ba1 = 0 if hh == 0 else 4
                    ba2 = 2 if hh == 0 else 6
                    need(t, wc1, wv1)
                    for og in range(2):
                        guard_bank(ba1 + og)
                    for k in range(8):
                        for og in range(2):
                            ii = t.matmul(p[ba1 + og][:, :],
                                          wring[:, slot1, k, (hh * 2 + og) * 128:(hh * 2 + og + 1) * 128],
                                          xn_fm[:, k, :], start=(k == 0), stop=(k == 7))
                    a1_stop = cT.inc(ii)
                    if hh == 1:
                        slot_last[slot1] = max(slot_last[slot1], a1_stop)
                    need(sc, cT, a1_stop)
                    last_silu = 0
                    for og in range(2):
                        oc = ss * 4 + hh * 2 + og
                        sl = hh * 2 + og
                        tg = t1_guard.get(sl)
                        if tg:
                            need(sc, cV, tg)
                        ev = S(lambda sl=sl, og=og, oc=oc, ba1=ba1: sc.activation(
                            t1[:, sl, :], p[ba1 + og][:, :], AF.Silu, scale=w1s_c[:, oc:oc + 1]))
                        bank_guard[ba1 + og] = (cS, ev)
                        last_silu = ev
                    need(t, wc2, wv2)
                    for og in range(2):
                        guard_bank(ba2 + og)
                    for k in range(8):
                        for og in range(2):
                            ii = t.matmul(p[ba2 + og][:, :],
                                          wring[:, slot2, k, (hh * 2 + og) * 128:(hh * 2 + og + 1) * 128],
                                          xn_fm[:, k, :], start=(k == 0), stop=(k == 7))
                    a2_stop = cT.inc(ii)
                    if hh == 1:
                        slot_last[slot2] = max(slot_last[slot2], a2_stop)
                    need(v, cT, a2_stop)
                    need(v, cS, last_silu)
                    need(v, cT, ffn4_done_T)
                    if s_ == 0:
                        need(v, cT, wstate["so_mm"])
                    for og in range(2):
                        oc = ss * 4 + hh * 2 + og
                        sl = hh * 2 + og
                        ev = V(lambda oc=oc, og=og, sl=sl, ba2=ba2: v.scalar_tensor_tensor(
                            h_sb[:, oc, :], p[ba2 + og][:, :], w2s_c[:, oc:oc + 1],
                            t1[:, sl, :], op0=ALU.mult, op1=ALU.mult))
                        bank_guard[ba2 + og] = (cV, ev)
                        t1_guard[sl] = ev
                h_by_strip[ss] = cV.n

            if s_ == 0:
                feats_done = emit_dot0(xn_ready_sweep)
                ffn1_strip(0)
                ffn1_strip(1)
                smm_last = emit_smm(feats_done)
                first_strip = 2
            else:
                smm_last = smm_next
                first_strip = 0

            for ss in range(first_strip, 8):
                ffn1_strip(ss)
                if ss == 2 and s_ + 1 < NS:
                    emit_xts_loads(s_ + 1)
                if ss == 3 and s_ + 1 < NS:
                    ln2_squares(s_ + 1, smm_last)
                if ss == 5 and s_ + 1 < NS:
                    ln2_stats(s_ + 1)
            ffn1_last = cT.n

            if s_ + 1 < NS:
                xn_ready_sweep = ln2_bcast_affine(s_ + 1, ffn1_last)

            g1_by_strip = {}
            g2_by_strip = {}
            for nm, src_sb, dst_sb, s_col, src_rdy, dst_rec in (
                    ("h1", h_sb, g1_sb, h1s_c, (cV, h_by_strip), g1_by_strip),
                    ("h2", g1_sb, h_sb, h2s_c, (cS, g1_by_strip), g2_by_strip)):
                rdy_ctr, rdy_map = src_rdy
                for ss in range(8):
                    quad = [0, 1, 2, 3] if ss % 2 == 0 else [4, 5, 6, 7]
                    for bi in quad:
                        guard_bank(bi)
                    for kk in range(4):
                        need(t, rdy_ctr, rdy_map[2 * kk + 1])
                        slot, wc, wv = ring_load(nm, ss, kk, s_ == 0)
                        need(t, wc, wv)
                        for k8 in range(8):
                            for og in range(4):
                                ii = t.matmul(p[quad[og]][:, :],
                                              wring[:, slot, k8, og * 128:(og + 1) * 128],
                                              src_sb[:, kk * 8 + k8, :],
                                              start=(kk == 0 and k8 == 0),
                                              stop=(kk == 3 and k8 == 7))
                        slot_last[slot] = cT.inc(ii)
                    stop = slot_last[slot]
                    need(sc, cT, stop)
                    for og in range(4):
                        oc = ss * 4 + og
                        ev = S(lambda dst_sb=dst_sb, oc=oc, og=og, quad=quad, s_col=s_col:
                               sc.activation(dst_sb[:, oc, :], p[quad[og]][:, :], AF.Gelu,
                                             scale=s_col[:, oc:oc + 1]))
                        bank_guard[quad[og]] = (cS, ev)
                    dst_rec[ss] = cS.n

            if s_ + 1 < NS:
                feats_next = emit_dot0(xn_ready_sweep)

            xt_val = {}

            def xt_load(dc):
                slot = dc % 4
                need(sy, cV, xt_guard.get(slot, 0))
                sy.dma_start(xt_r[:, slot, :],
                             xT[dc * 128:(dc + 1) * 128, tok0:tok0 + TS])\
                    .then_inc(d_xt[slot].sem, 16)
                d_xt[slot].n += 16
                xt_val[dc] = d_xt[slot].n

            for dc in range(4):
                xt_load(dc)

            sg_done = [0] * 8
            for ss in range(2):
                quad = [0, 1, 2, 3] if ss == 0 else [4, 5, 2, 3]
                for bi in quad:
                    guard_bank(bi)
                for kk in range(4):
                    need(t, cS, g2_by_strip[2 * kk + 1])
                    slot, wc, wv = ring_load("w3", ss, kk, s_ == 0)
                    need(t, wc, wv)
                    for k8 in range(8):
                        for og in range(4):
                            ii = t.matmul(p[quad[og]][:, :],
                                          wring[:, slot, k8, og * 128:(og + 1) * 128],
                                          h_sb[:, kk * 8 + k8, :],
                                          start=(kk == 0 and k8 == 0),
                                          stop=(kk == 3 and k8 == 7))
                    slot_last[slot] = cT.inc(ii)
                ffn4_done_T = slot_last[slot]
                need(sc, cT, ffn4_done_T)
                for og in range(4):
                    dc = ss * 4 + og
                    tg = t2_guard.get(og)
                    if tg:
                        need(sc, cV, tg)
                    ev = S(lambda og=og, dc=dc, quad=quad: sc.activation(
                        t2[:, og, :], p[quad[og]][:, :], AF.Silu,
                        scale=w3s_c[:, dc:dc + 1]))
                    bank_guard[quad[og]] = (cS, ev)
                for og in range(4):
                    dc = ss * 4 + og
                    need(v, cS, bank_guard[quad[og]][1] if bank_guard[quad[og]][0] is cS else 0)
                    mv = V(lambda og=og, dc=dc: v.tensor_mul(
                        s_sb[:, dc, :], t2[:, og, :], s_sb[:, dc, :]))
                    t2_guard[og] = mv
                    sg_done[dc] = mv
                for og in range(4):
                    dc = ss * 4 + og
                    slot = dc % 4
                    need(v, d_xt[slot], xt_val[dc])
                    av = V(lambda dc=dc, slot=slot: v.tensor_add(
                        s_sb[:, dc, :], xt_r[:, slot, :], s_sb[:, dc, :]))
                    xt_guard[slot] = av
                    need(sy, cV, av)
                    sy.dma_start(y[dc * 128:(dc + 1) * 128, tok0:tok0 + TS],
                                 s_sb[:, dc, :]).then_inc(d_y[dc].sem, 16)
                    d_y[dc].n += 16
                    y_val[dc] = d_y[dc].n
                if ss == 0:
                    for dc in range(4, 8):
                        xt_load(dc)

            if s_ + 1 < NS:
                smm_next = emit_smm(feats_next)

        for slot in range(8):
            g.wait_ge(d_y[slot].sem, d_y[slot].n)
    return nc


def _pack(w):
    O, I = w.shape
    wt = np.ascontiguousarray(w.T)
    arr = wt.reshape(I // 128, 128, O // 512, 512)
    return np.ascontiguousarray(arr.transpose(2, 1, 0, 3))


_NC_CACHE = {}


def kernel(L0_attn_x, L0_attn_input_weights, L0_attn_xis, L0_attn_proj_matrix,
           L0_ffn_w1_w, L0_ffn_w1_s, L0_ffn_w2_w, L0_ffn_w2_s,
           L0_ffn_h1_w, L0_ffn_h1_s, L0_ffn_h2_w, L0_ffn_h2_s,
           L0_ffn_w3_w, L0_ffn_w3_s):
    x = np.ascontiguousarray(np.asarray(L0_attn_x, dtype=np.float32)).reshape(-1, D)
    shared = {
        "projT": np.ascontiguousarray(np.asarray(L0_attn_proj_matrix, np.float32).T),
        "iwT": np.ascontiguousarray(np.asarray(L0_attn_input_weights, np.float32).T),
        "xis": np.ascontiguousarray(np.asarray(L0_attn_xis, np.float32)),
        "w1p": _pack(np.asarray(L0_ffn_w1_w, np.float32)),
        "w2p": _pack(np.asarray(L0_ffn_w2_w, np.float32)),
        "h1p": _pack(np.asarray(L0_ffn_h1_w, np.float32)),
        "h2p": _pack(np.asarray(L0_ffn_h2_w, np.float32)),
        "w3p": _pack(np.asarray(L0_ffn_w3_w, np.float32)),
        "w1s": np.ascontiguousarray(np.asarray(L0_ffn_w1_s, np.float32)),
        "w2s": np.ascontiguousarray(np.asarray(L0_ffn_w2_s, np.float32)),
        "h1s": np.ascontiguousarray(np.asarray(L0_ffn_h1_s, np.float32)),
        "h2s": np.ascontiguousarray(np.asarray(L0_ffn_h2_s, np.float32)),
        "w3s": np.ascontiguousarray(np.asarray(L0_ffn_w3_s, np.float32)),
    }
    if "nc" not in _NC_CACHE:
        _NC_CACHE["nc"] = build_nc()
    nc = _NC_CACHE["nc"]
    in_maps = []
    for c in range(8):
        xc = np.ascontiguousarray(x[c * T:(c + 1) * T])
        m = {"xT": np.ascontiguousarray(xc.T)}
        m.update(shared)
        in_maps.append(m)
    res = run_bass_kernel_spmd(nc, in_maps, core_ids=list(range(8)))
    out = np.concatenate([np.ascontiguousarray(res.results[c]["y"].T) for c in range(8)], axis=0)
    return out.reshape(4, 4096, D)


if __name__ == "__main__":
    nc = build_nc()
    print("built ok, instructions:", nc.n_instructions())


# version 29
# speedup vs baseline: 1.0432x; 1.0212x over previous
import contextlib
import numpy as np
import ml_dtypes
import concourse.bass as bass
import concourse.mybir as mybir
from concourse.bass_utils import run_bass_kernel_spmd

F32 = mybir.dt.float32
BF16 = mybir.dt.bfloat16
AF = mybir.ActivationFunctionType
ALU = mybir.AluOpType
AX = mybir.AxisListType
bf16 = ml_dtypes.bfloat16

T = 2048
TS = 512
NS = T // TS
NT = TS // 128
D = 1024
H = 4096
R = 1024
LN32 = float(np.log(np.sqrt(1024.0)))
EPS = 1e-5


class Ctr:
    def __init__(self, nc, stack, name, eng):
        self.sem = stack.enter_context(nc.semaphore(name))
        self.n = 0
        self.eng = eng

    def inc(self, instr, k=1):
        instr.then_inc(self.sem, k)
        self.n += k
        return self.n


def build_nc():
    nc = bass.Bass("TRN2", target_bir_lowering=False)
    g = nc.gpsimd
    sy = nc.sync
    t = nc.tensor
    v = nc.vector
    sc = nc.scalar

    xT = nc.dram_tensor("xT", [D, T], F32, kind="ExternalInput").ap()
    projT = nc.dram_tensor("projT", [D, R], F32, kind="ExternalInput").ap()
    iwT = nc.dram_tensor("iwT", [D, R], F32, kind="ExternalInput").ap()
    xis = nc.dram_tensor("xis", [R], F32, kind="ExternalInput").ap()
    w1p = nc.dram_tensor("w1p", [8, 128, 8, 512], F32, kind="ExternalInput").ap()
    w2p = nc.dram_tensor("w2p", [8, 128, 8, 512], F32, kind="ExternalInput").ap()
    h1p = nc.dram_tensor("h1p", [8, 128, 32, 512], F32, kind="ExternalInput").ap()
    h2p = nc.dram_tensor("h2p", [8, 128, 32, 512], F32, kind="ExternalInput").ap()
    w3p = nc.dram_tensor("w3p", [2, 128, 32, 512], F32, kind="ExternalInput").ap()
    w1s = nc.dram_tensor("w1s", [H], F32, kind="ExternalInput").ap()
    w2s = nc.dram_tensor("w2s", [H], F32, kind="ExternalInput").ap()
    h1s = nc.dram_tensor("h1s", [H], F32, kind="ExternalInput").ap()
    h2s = nc.dram_tensor("h2s", [H], F32, kind="ExternalInput").ap()
    w3s = nc.dram_tensor("w3s", [D], F32, kind="ExternalInput").ap()
    y = nc.dram_tensor("y", [D, T], F32, kind="ExternalOutput").ap()

    identf_h = nc.inline_tensor(np.eye(128).astype(np.float32), "ident_f")

    with contextlib.ExitStack() as stack:
        E = stack.enter_context

        def sb(name, shape, dtype):
            return E(nc.sbuf_tensor(name, shape, dtype))

        projT_sb = sb("projT_sb", [128, 9, R], BF16)
        Wt_sb = sb("Wt_sb", [128, 8, R], BF16)
        xt_sweep = sb("xt_sweep", [128, 8, TS], BF16)
        xt_r = sb("xt_r", [128, 4, 512], F32)
        rows = sb("rows", [128, 4, TS], F32)
        xn_fm = sb("xn_fm", [128, 9, TS], BF16)
        feats = sb("feats", [128, 8, TS], BF16)
        s_sb = sb("s_sb", [128, 8, TS], F32)
        h_sb = sb("h_sb", [128, 32, TS], BF16)
        g1_sb = sb("g1_sb", [128, 32, TS], BF16)
        wring = sb("wring", [128, 5, 8, TS], BF16)
        t1 = sb("t1", [128, 4, TS], BF16)
        t2 = sb("t2", [128, 4, TS], BF16)
        identf = sb("identf", [128, 128], F32)
        vec_sb = sb("vec_sb", [32, 6, 128], F32)
        ab_col = sb("ab_col", [128, 8], F32)
        bias_f = sb("bias_f", [128, 8], F32)
        bias_w = sb("bias_w", [128, 8], F32)
        w1s_c = sb("w1s_c", [128, 32], F32)
        w2s_c = sb("w2s_c", [128, 32], F32)
        h1s_c = sb("h1s_c", [128, 32], F32)
        h2s_c = sb("h2s_c", [128, 32], F32)
        w3s_c = sb("w3s_c", [128, 8], F32)
        stat = sb("stat", [128, 16], F32)
        ones_nh = sb("ones_nh", [128, 1], BF16)
        ones_b = sb("ones_b", [128, 1], BF16)
        ones_c = sb("ones_c", [128, 1], F32)
        ones_row = sb("ones_row", [1, 128], F32)
        c_eps = sb("c_eps", [128, 1], F32)
        c_nhalf = sb("c_nhalf", [128, 1], F32)

        iw_view = g1_sb.ap().rearrange("p a b -> p (a b)")
        iw_c = [iw_view[:, j * R:(j + 1) * R] for j in range(9)]

        p = [E(nc.psum_tensor(f"pb{i}", [128, 512], F32)) for i in range(8)]

        cT = Ctr(nc, stack, "cT", t)
        cV = Ctr(nc, stack, "cV", v)
        cS = Ctr(nc, stack, "cS", sc)
        d_xts = Ctr(nc, stack, "d_xts", g)
        d_xt = [Ctr(nc, stack, f"d_xt{i}", sy) for i in range(4)]
        d_y = [Ctr(nc, stack, f"d_y{i}", sy) for i in range(8)]
        d_wr = [Ctr(nc, stack, f"d_wr{i}", g) for i in range(5)]
        d_ident = Ctr(nc, stack, "d_ident", sy)
        d_misc = Ctr(nc, stack, "d_misc", sy)
        d_vecj = [Ctr(nc, stack, f"d_vecj{i}", sy) for i in range(6)]
        cast_sem = {}

        waited = {}

        def need(eng, ctr, val, allow_self=False):
            if ctr is None or val <= 0:
                return
            if ctr.eng is eng and not allow_self:
                return
            key = (id(eng), id(ctr.sem))
            if waited.get(key, 0) >= val:
                return
            eng.wait_ge(ctr.sem, val)
            waited[key] = val

        bank_guard = {}

        def guard_bank(i):
            gd = bank_guard.get(i)
            if gd is not None:
                need(t, gd[0], gd[1])

        def V(instr_fn):
            need(v, cV, cV.n, allow_self=True)
            return cV.inc(instr_fn())

        def S(instr_fn):
            need(sc, cS, cS.n, allow_self=True)
            return cS.inc(instr_fn())

        cast_sem["iwT"] = Ctr(nc, stack, "cs_iwT", g)
        g.dma_start(iw_view[:, 0:8 * R].rearrange("p (j r) -> p j r", j=8),
                    iwT.rearrange("(j p) r -> p j r", p=128)).then_inc(cast_sem["iwT"].sem, 16)
        cast_sem["iwT"].n = 16

        xts_state = {"guard": 0}

        def emit_xts_loads(s_):
            need(g, cV, xts_state["guard"])
            for j in range(8):
                g.dma_start(xt_sweep[:, j, :],
                            xT[j * 128:(j + 1) * 128, s_ * TS:(s_ + 1) * TS])\
                    .then_inc(d_xts.sem, 16)
                d_xts.n += 16

        emit_xts_loads(0)

        srcs = {"w1": w1p, "w2": w2p, "h1": h1p, "h2": h2p, "w3": w3p}

        sy.dma_start(identf[:, :], identf_h.ap()).then_inc(d_ident.sem, 16)
        d_ident.n += 16

        ms_p8 = V(lambda: v.memset(projT_sb[:, 8, :], 0.0))
        V(lambda: v.memset(iw_c[8], 0.0))
        V(lambda: v.memset(xn_fm[:, 8, :], 0.0))
        V(lambda: v.memset(ones_nh[:, :], -0.5))
        V(lambda: v.memset(c_eps[:, :], EPS))
        V(lambda: v.memset(c_nhalf[:, :], -0.5))
        V(lambda: v.memset(ones_b[:, :], 1.0))
        V(lambda: v.memset(ones_c[:, :], 1.0))
        V(lambda: v.memset(ones_row[:, :], 1.0))
        cast_sem["xisrow"] = Ctr(nc, stack, "cs_xisrow", g)
        g.wait_ge(cV.sem, ms_p8)
        g.dma_start(projT_sb[0:1, 8, :], xis[None, :]).then_inc(cast_sem["xisrow"].sem, 16)
        cast_sem["xisrow"].n = 16
        cast_sem["projT"] = Ctr(nc, stack, "cs_projT", g)
        g.dma_start(projT_sb[:, 0:8, :], projT.rearrange("(j p) r -> p j r", p=128))\
            .then_inc(cast_sem["projT"].sem, 16)
        cast_sem["projT"].n = 16

        vec_jobs = [
            (xis, 8, ab_col, 8), (w1s, 32, w1s_c, 32), (w2s, 32, w2s_c, 32),
            (h1s, 32, h1s_c, 32), (h2s, 32, h2s_c, 32), (w3s, 8, w3s_c, 8),
        ]
        for vi, (vsrc, nparts, dst, ncols) in enumerate(vec_jobs):
            sy.dma_start(vec_sb[0:nparts, vi, :], vsrc.rearrange("(c p) -> c p", p=128))\
                .then_inc(d_vecj[vi].sem, 16)
            d_vecj[vi].n += 16
        for vi, (vsrc, nparts, dst, ncols) in enumerate(vec_jobs):
            need(t, d_vecj[vi], 16)
            need(t, d_ident, 16)
            bk = 4 + (vi % 4)
            guard_bank(bk)
            vt = cT.inc(t.transpose(p[bk][0:128, 0:ncols], vec_sb[0:nparts, vi, :],
                                    identf[0:nparts, 0:nparts]))
            need(v, cT, vt)
            ev = V(lambda dst=dst, ncols=ncols, bk=bk: v.tensor_copy(dst[:, 0:ncols],
                                                                    p[bk][:, 0:ncols]))
            bank_guard[bk] = (cV, ev)

        V(lambda: v.tensor_scalar_add(bias_w[:, :], ab_col[:, :], -LN32))
        V(lambda: v.tensor_mul(stat[:, 0:8], ab_col[:, :], ab_col[:, :]))
        V(lambda: v.scalar_tensor_tensor(bias_f[:, :], stat[:, 0:8], -512.0,
                                         bias_w[:, :], op0=ALU.mult, op1=ALU.add))
        cols_ready = cV.n

        ring_seq = 0
        slot_last = [0, 0, 0, 0, 0]

        def ring_load(nm, strip, kk8, first):
            nonlocal ring_seq
            slot = ring_seq % 5
            fsrc = srcs[nm][strip] if nm in ("w1", "w2") \
                else srcs[nm][strip][:, kk8 * 8:(kk8 + 1) * 8, :]
            need(g, cT, slot_last[slot])
            g.dma_start(wring[:, slot, :, :], fsrc).then_inc(d_wr[slot].sem, 16)
            d_wr[slot].n += 16
            ring_seq += 1
            return slot, d_wr[slot], d_wr[slot].n

        xsq = feats
        ln_state = {}

        def ln2_squares(s_, smm_last):
            need(v, cT, smm_last)
            need(v, d_xts, d_xts.n)
            for j in range(8):
                V(lambda j=j: v.tensor_mul(xsq[:, j, :], xt_sweep[:, j, :],
                                           xt_sweep[:, j, :]))
            ln_state["sq"] = cV.n

        def ln2_stats(s_):
            need(t, cV, ln_state["sq"])
            guard_bank(6)
            guard_bank(7)
            for j in range(8):
                t.matmul(p[6][0:1, :], ones_b[:, 0:1], xt_sweep[:, j, :],
                         start=(j == 0), stop=(j == 7))
                ii = t.matmul(p[7][0:1, :], ones_b[:, 0:1], xsq[:, j, :],
                              start=(j == 0), stop=(j == 7))
            st = cT.inc(ii)
            need(v, cT, st)
            V(lambda: v.tensor_copy(rows[0:1, 0, :], p[6][0:1, :]))
            ev = V(lambda: v.tensor_copy(rows[0:1, 1, :], p[7][0:1, :]))
            bank_guard[6] = (cV, ev)
            bank_guard[7] = (cV, ev)
            V(lambda: v.tensor_scalar_mul(rows[0:1, 0, :], rows[0:1, 0, :], 1.0 / D))
            V(lambda: v.tensor_scalar_mul(rows[0:1, 1, :], rows[0:1, 1, :], 1.0 / D))
            V(lambda: v.tensor_mul(rows[0:1, 2, :], rows[0:1, 0, :], rows[0:1, 0, :]))
            sv = V(lambda: v.tensor_sub(rows[0:1, 1, :], rows[0:1, 1, :], rows[0:1, 2, :]))
            need(sc, cV, sv)
            S(lambda: sc.activation(rows[0:1, 2, :], rows[0:1, 1, :], AF.Ln,
                                    bias=c_eps[0:1, 0:1]))
            rs = S(lambda: sc.activation(rows[0:1, 2, :], rows[0:1, 2, :], AF.Exp,
                                         scale=c_nhalf[0:1, 0:1]))
            need(v, cS, rs)
            V(lambda: v.tensor_mul(rows[0:1, 3, :], rows[0:1, 0, :], rows[0:1, 2, :]))
            rd = V(lambda: v.scalar_tensor_tensor(rows[0:1, 1, :], rows[0:1, 2, :],
                                                  512.0 * EPS, rows[0:1, 2, :],
                                                  op0=ALU.mult, op1=ALU.mult))
            ln_state["rows"] = rd

        def ln2_bcast_affine(s_, ffn1_last):
            need(t, cV, ln_state["rows"])
            guard_bank(4)
            guard_bank(5)
            bc1 = cT.inc(t.matmul(p[4][:, :], ones_row[0:1, :], rows[0:1, 2, :],
                                  start=True, stop=True))
            bc2 = cT.inc(t.matmul(p[5][:, :], ones_row[0:1, :], rows[0:1, 3, :],
                                  start=True, stop=True))
            need(v, cT, bc2)
            need(v, cT, ffn1_last)
            for j in range(8):
                V(lambda j=j: v.tensor_mul(xn_fm[:, j, :], xt_sweep[:, j, :], p[4][:, :]))
                V(lambda j=j: v.tensor_sub(xn_fm[:, j, :], xn_fm[:, j, :], p[5][:, :]))
            xr = V(lambda: v.tensor_copy(xn_fm[0:1, 8, :], rows[0:1, 1, :]))
            bank_guard[4] = (cV, xr)
            bank_guard[5] = (cV, xr)
            xts_state["guard"] = xr
            return xr

        ffn4_done_T = 0
        t1_guard = {}
        t2_guard = {}
        y_val = {}
        xt_guard = {}

        need(v, cast_sem["iwT"], 16)
        sq_done = 0
        for j in range(8):
            sq_done = V(lambda j=j: v.tensor_mul(
                h_sb[:, 2 * j:2 * j + 2, :].rearrange("p a b -> p (a b)"),
                iw_c[j], iw_c[j]))

        wstate = {}

        def emit_so():
            need(t, cV, sq_done)
            need(t, cast_sem["iwT"], 16)
            guard_bank(2)
            guard_bank(3)
            for j in range(8):
                sqv = h_sb[:, 2 * j:2 * j + 2, :].rearrange("p a b -> p (a b)")
                t.matmul(p[2][0:1, :], ones_nh[:, 0:1], sqv[:, 0:512],
                         start=(j == 0), stop=(j == 7))
                ii = t.matmul(p[3][0:1, :], ones_nh[:, 0:1], sqv[:, 512:1024],
                              start=(j == 0), stop=(j == 7))
            so_done = cT.inc(ii)
            wstate["so_mm"] = so_done
            need(sc, cT, so_done)
            e1 = S(lambda: sc.copy(iw_c[8][0:1, 0:512], p[2][0:1, :]))
            e2 = S(lambda: sc.copy(iw_c[8][0:1, 512:1024], p[3][0:1, :]))
            bank_guard[2] = (cS, e1)
            bank_guard[3] = (cS, e2)
            wstate["e2"] = e2

        def emit_wmms():
            need(t, cast_sem["projT"], 16)
            need(t, cast_sem["xisrow"], 16)
            need(t, cS, wstate["e2"])
            need(t, cV, cols_ready)
            ev = 0
            for rc in range(8):
                for oh in range(2):
                    bk = 2 + oh
                    guard_bank(bk)
                    for j in range(9):
                        ii = t.matmul(p[bk][:, :], projT_sb[:, j, rc * 128:(rc + 1) * 128],
                                      iw_c[j][:, oh * 512:(oh + 1) * 512],
                                      start=(j == 0), stop=(j == 8))
                    stop = cT.inc(ii)
                    need(sc, cT, stop)
                    ev = S(lambda rc=rc, oh=oh, bk=bk: sc.activation(
                        Wt_sb[:, rc, oh * 512:(oh + 1) * 512], p[bk][:, :], AF.Exp,
                        bias=bias_w[:, rc:rc + 1], scale=ab_col[:, rc:rc + 1]))
                    bank_guard[bk] = (cS, ev)
            wstate["wt_done"] = ev

        emit_so()
        ln2_squares(0, 0)
        ln2_stats(0)
        xn_ready_sweep = ln2_bcast_affine(0, 0)
        emit_wmms()

        def emit_dot0(xn_rdy):
            need(t, cast_sem["projT"], 16)
            need(t, cast_sem["xisrow"], 16)
            need(t, cV, xn_rdy)
            need(t, cV, cols_ready)
            for rc in range(8):
                bk = (6, 7, 4, 5)[rc % 4]
                guard_bank(bk)
                for j in range(9):
                    ii = t.matmul(p[bk][:, :], projT_sb[:, j, rc * 128:(rc + 1) * 128],
                                  xn_fm[:, j, :], start=(j == 0), stop=(j == 8))
                stop = cT.inc(ii)
                need(sc, cT, stop)
                ev = S(lambda rc=rc, bk=bk: sc.activation(
                    feats[:, rc, :], p[bk][:, :], AF.Exp,
                    bias=bias_f[:, rc:rc + 1], scale=ab_col[:, rc:rc + 1]))
                bank_guard[bk] = (cS, ev)
            return cS.n

        def emit_smm(feats_done):
            need(t, cS, feats_done)
            need(t, cS, wstate["wt_done"])
            for oc in range(8):
                bk = oc
                guard_bank(bk)
                for rc in range(8):
                    ii = t.matmul(p[bk][:, :], Wt_sb[:, rc, oc * 128:(oc + 1) * 128],
                                  feats[:, rc, :], start=(rc == 0), stop=(rc == 7))
                stop = cT.inc(ii)
                need(v, cT, stop)
                need(v, d_y[oc], y_val.get(oc, 0))
                ev = V(lambda oc=oc, bk=bk: v.tensor_copy(s_sb[:, oc, :], p[bk][:, :]))
                bank_guard[bk] = (cV, ev)
            return cT.n

        smm_next = 0
        feats_next = 0
        for s_ in range(NS):
            tok0 = s_ * TS

            h_by_strip = {}

            def ffn1_strip(ss):
                slot1, wc1, wv1 = ring_load("w1", ss, 0, s_ == 0)
                slot2, wc2, wv2 = ring_load("w2", ss, 0, s_ == 0)
                for hh in range(2):
                    ba1 = 0 if hh == 0 else 4
                    ba2 = 2 if hh == 0 else 6
                    need(t, wc1, wv1)
                    for og in range(2):
                        guard_bank(ba1 + og)
                    for k in range(8):
                        for og in range(2):
                            ii = t.matmul(p[ba1 + og][:, :],
                                          wring[:, slot1, k, (hh * 2 + og) * 128:(hh * 2 + og + 1) * 128],
                                          xn_fm[:, k, :], start=(k == 0), stop=(k == 7))
                    a1_stop = cT.inc(ii)
                    if hh == 1:
                        slot_last[slot1] = max(slot_last[slot1], a1_stop)
                    need(sc, cT, a1_stop)
                    last_silu = 0
                    for og in range(2):
                        oc = ss * 4 + hh * 2 + og
                        sl = hh * 2 + og
                        tg = t1_guard.get(sl)
                        if tg:
                            need(sc, cV, tg)
                        ev = S(lambda sl=sl, og=og, oc=oc, ba1=ba1: sc.activation(
                            t1[:, sl, :], p[ba1 + og][:, :], AF.Silu, scale=w1s_c[:, oc:oc + 1]))
                        bank_guard[ba1 + og] = (cS, ev)
                        last_silu = ev
                    need(t, wc2, wv2)
                    for og in range(2):
                        guard_bank(ba2 + og)
                    for k in range(8):
                        for og in range(2):
                            ii = t.matmul(p[ba2 + og][:, :],
                                          wring[:, slot2, k, (hh * 2 + og) * 128:(hh * 2 + og + 1) * 128],
                                          xn_fm[:, k, :], start=(k == 0), stop=(k == 7))
                    a2_stop = cT.inc(ii)
                    if hh == 1:
                        slot_last[slot2] = max(slot_last[slot2], a2_stop)
                    need(v, cT, a2_stop)
                    need(v, cS, last_silu)
                    need(v, cT, ffn4_done_T)
                    if s_ == 0:
                        need(v, cT, wstate["so_mm"])
                    for og in range(2):
                        oc = ss * 4 + hh * 2 + og
                        sl = hh * 2 + og
                        ev = V(lambda oc=oc, og=og, sl=sl, ba2=ba2: v.scalar_tensor_tensor(
                            h_sb[:, oc, :], p[ba2 + og][:, :], w2s_c[:, oc:oc + 1],
                            t1[:, sl, :], op0=ALU.mult, op1=ALU.mult))
                        bank_guard[ba2 + og] = (cV, ev)
                        t1_guard[sl] = ev
                h_by_strip[ss] = cV.n

            if s_ == 0:
                feats_done = emit_dot0(xn_ready_sweep)
                ffn1_strip(0)
                ffn1_strip(1)
                smm_last = emit_smm(feats_done)
                first_strip = 2
            else:
                smm_last = smm_next
                first_strip = 0

            for ss in range(first_strip, 8):
                ffn1_strip(ss)
                if ss == 2 and s_ + 1 < NS:
                    emit_xts_loads(s_ + 1)
                if ss == 3 and s_ + 1 < NS:
                    ln2_squares(s_ + 1, smm_last)
                if ss == 5 and s_ + 1 < NS:
                    ln2_stats(s_ + 1)
            ffn1_last = cT.n

            if s_ + 1 < NS:
                xn_ready_sweep = ln2_bcast_affine(s_ + 1, ffn1_last)

            g1_by_strip = {}
            g2_by_strip = {}
            for nm, src_sb, dst_sb, s_col, src_rdy, dst_rec in (
                    ("h1", h_sb, g1_sb, h1s_c, (cV, h_by_strip), g1_by_strip),
                    ("h2", g1_sb, h_sb, h2s_c, (cS, g1_by_strip), g2_by_strip)):
                rdy_ctr, rdy_map = src_rdy
                for ss in range(8):
                    quad = [0, 1, 2, 3] if ss % 2 == 0 else [4, 5, 6, 7]
                    for bi in quad:
                        guard_bank(bi)
                    for kk in range(4):
                        need(t, rdy_ctr, rdy_map[2 * kk + 1])
                        slot, wc, wv = ring_load(nm, ss, kk, s_ == 0)
                        need(t, wc, wv)
                        for k8 in range(8):
                            for og in range(4):
                                ii = t.matmul(p[quad[og]][:, :],
                                              wring[:, slot, k8, og * 128:(og + 1) * 128],
                                              src_sb[:, kk * 8 + k8, :],
                                              start=(kk == 0 and k8 == 0),
                                              stop=(kk == 3 and k8 == 7))
                        slot_last[slot] = cT.inc(ii)
                    stop = slot_last[slot]
                    need(sc, cT, stop)
                    for og in range(4):
                        oc = ss * 4 + og
                        ev = S(lambda dst_sb=dst_sb, oc=oc, og=og, quad=quad, s_col=s_col:
                               sc.activation(dst_sb[:, oc, :], p[quad[og]][:, :], AF.Gelu,
                                             scale=s_col[:, oc:oc + 1]))
                        bank_guard[quad[og]] = (cS, ev)
                    dst_rec[ss] = cS.n

            if s_ + 1 < NS:
                feats_next = emit_dot0(xn_ready_sweep)

            xt_val = {}

            def xt_load(dc):
                slot = dc % 4
                need(sy, cV, xt_guard.get(slot, 0))
                sy.dma_start(xt_r[:, slot, :],
                             xT[dc * 128:(dc + 1) * 128, tok0:tok0 + TS])\
                    .then_inc(d_xt[slot].sem, 16)
                d_xt[slot].n += 16
                xt_val[dc] = d_xt[slot].n

            for dc in range(4):
                xt_load(dc)

            sg_done = [0] * 8
            for ss in range(2):
                quad = [0, 1, 2, 3] if ss == 0 else [4, 5, 2, 3]
                for bi in quad:
                    guard_bank(bi)
                for kk in range(4):
                    need(t, cS, g2_by_strip[2 * kk + 1])
                    slot, wc, wv = ring_load("w3", ss, kk, s_ == 0)
                    need(t, wc, wv)
                    for k8 in range(8):
                        for og in range(4):
                            ii = t.matmul(p[quad[og]][:, :],
                                          wring[:, slot, k8, og * 128:(og + 1) * 128],
                                          h_sb[:, kk * 8 + k8, :],
                                          start=(kk == 0 and k8 == 0),
                                          stop=(kk == 3 and k8 == 7))
                    slot_last[slot] = cT.inc(ii)
                ffn4_done_T = slot_last[slot]
                need(sc, cT, ffn4_done_T)
                for og in range(4):
                    dc = ss * 4 + og
                    tg = t2_guard.get(og)
                    if tg:
                        need(sc, cV, tg)
                    ev = S(lambda og=og, dc=dc, quad=quad: sc.activation(
                        t2[:, og, :], p[quad[og]][:, :], AF.Silu,
                        scale=w3s_c[:, dc:dc + 1]))
                    bank_guard[quad[og]] = (cS, ev)
                for og in range(4):
                    dc = ss * 4 + og
                    need(v, cS, bank_guard[quad[og]][1] if bank_guard[quad[og]][0] is cS else 0)
                    mv = V(lambda og=og, dc=dc: v.tensor_mul(
                        s_sb[:, dc, :], t2[:, og, :], s_sb[:, dc, :]))
                    t2_guard[og] = mv
                    sg_done[dc] = mv
                for og in range(4):
                    dc = ss * 4 + og
                    slot = dc % 4
                    need(v, d_xt[slot], xt_val[dc])
                    av = V(lambda dc=dc, slot=slot: v.tensor_add(
                        s_sb[:, dc, :], xt_r[:, slot, :], s_sb[:, dc, :]))
                    xt_guard[slot] = av
                    need(sy, cV, av)
                    sy.dma_start(y[dc * 128:(dc + 1) * 128, tok0:tok0 + TS],
                                 s_sb[:, dc, :]).then_inc(d_y[dc].sem, 16)
                    d_y[dc].n += 16
                    y_val[dc] = d_y[dc].n
                if ss == 0:
                    for dc in range(4, 8):
                        xt_load(dc)

            if s_ + 1 < NS:
                smm_next = emit_smm(feats_next)

        for slot in range(8):
            g.wait_ge(d_y[slot].sem, d_y[slot].n)
    return nc


def _pack(w):
    O, I = w.shape
    wt = np.ascontiguousarray(w.T)
    arr = wt.reshape(I // 128, 128, O // 512, 512)
    return np.ascontiguousarray(arr.transpose(2, 1, 0, 3))


_NC_CACHE = {}


def kernel(L0_attn_x, L0_attn_input_weights, L0_attn_xis, L0_attn_proj_matrix,
           L0_ffn_w1_w, L0_ffn_w1_s, L0_ffn_w2_w, L0_ffn_w2_s,
           L0_ffn_h1_w, L0_ffn_h1_s, L0_ffn_h2_w, L0_ffn_h2_s,
           L0_ffn_w3_w, L0_ffn_w3_s):
    x = np.ascontiguousarray(np.asarray(L0_attn_x, dtype=np.float32)).reshape(-1, D)
    shared = {
        "projT": np.ascontiguousarray(np.asarray(L0_attn_proj_matrix, np.float32).T),
        "iwT": np.ascontiguousarray(np.asarray(L0_attn_input_weights, np.float32).T),
        "xis": np.ascontiguousarray(np.asarray(L0_attn_xis, np.float32)),
        "w1p": _pack(np.asarray(L0_ffn_w1_w, np.float32)),
        "w2p": _pack(np.asarray(L0_ffn_w2_w, np.float32)),
        "h1p": _pack(np.asarray(L0_ffn_h1_w, np.float32)),
        "h2p": _pack(np.asarray(L0_ffn_h2_w, np.float32)),
        "w3p": _pack(np.asarray(L0_ffn_w3_w, np.float32)),
        "w1s": np.ascontiguousarray(np.asarray(L0_ffn_w1_s, np.float32)),
        "w2s": np.ascontiguousarray(np.asarray(L0_ffn_w2_s, np.float32)),
        "h1s": np.ascontiguousarray(np.asarray(L0_ffn_h1_s, np.float32)),
        "h2s": np.ascontiguousarray(np.asarray(L0_ffn_h2_s, np.float32)),
        "w3s": np.ascontiguousarray(np.asarray(L0_ffn_w3_s, np.float32)),
    }
    if "nc" not in _NC_CACHE:
        _NC_CACHE["nc"] = build_nc()
    nc = _NC_CACHE["nc"]
    in_maps = []
    for c in range(8):
        xc = np.ascontiguousarray(x[c * T:(c + 1) * T])
        m = {"xT": np.ascontiguousarray(xc.T)}
        m.update(shared)
        in_maps.append(m)
    res = run_bass_kernel_spmd(nc, in_maps, core_ids=list(range(8)))
    out = np.concatenate([np.ascontiguousarray(res.results[c]["y"].T) for c in range(8)], axis=0)
    return out.reshape(4, 4096, D)


if __name__ == "__main__":
    nc = build_nc()
    print("built ok, instructions:", nc.n_instructions())


# version 30
# speedup vs baseline: 1.0434x; 1.0002x over previous
import contextlib
import numpy as np
import ml_dtypes
import concourse.bass as bass
import concourse.mybir as mybir
from concourse.bass_utils import run_bass_kernel_spmd

F32 = mybir.dt.float32
BF16 = mybir.dt.bfloat16
AF = mybir.ActivationFunctionType
ALU = mybir.AluOpType
AX = mybir.AxisListType
bf16 = ml_dtypes.bfloat16

T = 2048
TS = 512
NS = T // TS
NT = TS // 128
D = 1024
H = 4096
R = 1024
LN32 = float(np.log(np.sqrt(1024.0)))
EPS = 1e-5


class Ctr:
    def __init__(self, nc, stack, name, eng):
        self.sem = stack.enter_context(nc.semaphore(name))
        self.n = 0
        self.eng = eng

    def inc(self, instr, k=1):
        instr.then_inc(self.sem, k)
        self.n += k
        return self.n


def build_nc():
    nc = bass.Bass("TRN2", target_bir_lowering=False)
    g = nc.gpsimd
    sy = nc.sync
    t = nc.tensor
    v = nc.vector
    sc = nc.scalar

    xT = nc.dram_tensor("xT", [D, T], F32, kind="ExternalInput").ap()
    projT = nc.dram_tensor("projT", [D, R], F32, kind="ExternalInput").ap()
    iwT = nc.dram_tensor("iwT", [D, R], F32, kind="ExternalInput").ap()
    xis = nc.dram_tensor("xis", [R], F32, kind="ExternalInput").ap()
    w1p = nc.dram_tensor("w1p", [8, 128, 8, 512], F32, kind="ExternalInput").ap()
    w2p = nc.dram_tensor("w2p", [8, 128, 8, 512], F32, kind="ExternalInput").ap()
    h1p = nc.dram_tensor("h1p", [8, 128, 32, 512], F32, kind="ExternalInput").ap()
    h2p = nc.dram_tensor("h2p", [8, 128, 32, 512], F32, kind="ExternalInput").ap()
    w3p = nc.dram_tensor("w3p", [2, 128, 32, 512], F32, kind="ExternalInput").ap()
    w1s = nc.dram_tensor("w1s", [H], F32, kind="ExternalInput").ap()
    w2s = nc.dram_tensor("w2s", [H], F32, kind="ExternalInput").ap()
    h1s = nc.dram_tensor("h1s", [H], F32, kind="ExternalInput").ap()
    h2s = nc.dram_tensor("h2s", [H], F32, kind="ExternalInput").ap()
    w3s = nc.dram_tensor("w3s", [D], F32, kind="ExternalInput").ap()
    y = nc.dram_tensor("y", [D, T], F32, kind="ExternalOutput").ap()

    identf_h = nc.inline_tensor(np.eye(128).astype(np.float32), "ident_f")

    with contextlib.ExitStack() as stack:
        E = stack.enter_context

        def sb(name, shape, dtype):
            return E(nc.sbuf_tensor(name, shape, dtype))

        projT_sb = sb("projT_sb", [128, 9, R], BF16)
        Wt_sb = sb("Wt_sb", [128, 8, R], BF16)
        xt_sweep = sb("xt_sweep", [128, 8, TS], BF16)
        xt_r = sb("xt_r", [128, 4, 512], F32)
        rows = sb("rows", [128, 4, TS], F32)
        xn_fm = sb("xn_fm", [128, 9, TS], BF16)
        feats = sb("feats", [128, 8, TS], BF16)
        s_sb = sb("s_sb", [128, 8, TS], F32)
        h_sb = sb("h_sb", [128, 32, TS], BF16)
        g1_sb = sb("g1_sb", [128, 32, TS], BF16)
        wring = sb("wring", [128, 5, 8, TS], BF16)
        t1 = sb("t1", [128, 4, TS], BF16)
        t2 = sb("t2", [128, 4, TS], BF16)
        identf = sb("identf", [128, 128], F32)
        vec_sb = sb("vec_sb", [32, 6, 128], F32)
        ab_col = sb("ab_col", [128, 8], F32)
        bias_f = sb("bias_f", [128, 8], F32)
        bias_w = sb("bias_w", [128, 8], F32)
        w1s_c = sb("w1s_c", [128, 32], F32)
        w2s_c = sb("w2s_c", [128, 32], F32)
        h1s_c = sb("h1s_c", [128, 32], F32)
        h2s_c = sb("h2s_c", [128, 32], F32)
        w3s_c = sb("w3s_c", [128, 8], F32)
        stat = sb("stat", [128, 16], F32)
        ones_nh = sb("ones_nh", [128, 1], BF16)
        ones_b = sb("ones_b", [128, 1], BF16)
        ones_c = sb("ones_c", [128, 1], F32)
        ones_row = sb("ones_row", [1, 128], F32)
        c_eps = sb("c_eps", [128, 1], F32)
        c_nhalf = sb("c_nhalf", [128, 1], F32)

        iw_view = g1_sb.ap().rearrange("p a b -> p (a b)")
        iw_c = [iw_view[:, j * R:(j + 1) * R] for j in range(9)]

        p = [E(nc.psum_tensor(f"pb{i}", [128, 512], F32)) for i in range(8)]

        cT = Ctr(nc, stack, "cT", t)
        cV = Ctr(nc, stack, "cV", v)
        cS = Ctr(nc, stack, "cS", sc)
        d_xts = Ctr(nc, stack, "d_xts", g)
        d_xt = [Ctr(nc, stack, f"d_xt{i}", sy) for i in range(4)]
        d_y = [Ctr(nc, stack, f"d_y{i}", sy) for i in range(8)]
        d_wr = [Ctr(nc, stack, f"d_wr{i}", g) for i in range(5)]
        d_ident = Ctr(nc, stack, "d_ident", sy)
        d_misc = Ctr(nc, stack, "d_misc", sy)
        d_vecj = [Ctr(nc, stack, f"d_vecj{i}", sy) for i in range(6)]
        cast_sem = {}

        waited = {}

        def need(eng, ctr, val, allow_self=False):
            if ctr is None or val <= 0:
                return
            if ctr.eng is eng and not allow_self:
                return
            key = (id(eng), id(ctr.sem))
            if waited.get(key, 0) >= val:
                return
            eng.wait_ge(ctr.sem, val)
            waited[key] = val

        bank_guard = {}

        def guard_bank(i):
            gd = bank_guard.get(i)
            if gd is not None:
                need(t, gd[0], gd[1])

        def V(instr_fn):
            need(v, cV, cV.n, allow_self=True)
            return cV.inc(instr_fn())

        def S(instr_fn):
            need(sc, cS, cS.n, allow_self=True)
            return cS.inc(instr_fn())

        cast_sem["iwT"] = Ctr(nc, stack, "cs_iwT", g)
        g.dma_start(iw_view[:, 0:8 * R].rearrange("p (j r) -> p j r", j=8),
                    iwT.rearrange("(j p) r -> p j r", p=128)).then_inc(cast_sem["iwT"].sem, 16)
        cast_sem["iwT"].n = 16

        xts_state = {"guard": 0}

        def emit_xts_loads(s_):
            need(g, cV, xts_state["guard"])
            for j in range(8):
                g.dma_start(xt_sweep[:, j, :],
                            xT[j * 128:(j + 1) * 128, s_ * TS:(s_ + 1) * TS])\
                    .then_inc(d_xts.sem, 16)
                d_xts.n += 16

        emit_xts_loads(0)

        srcs = {"w1": w1p, "w2": w2p, "h1": h1p, "h2": h2p, "w3": w3p}

        sy.dma_start(identf[:, :], identf_h.ap()).then_inc(d_ident.sem, 16)
        d_ident.n += 16

        ms_p8 = V(lambda: v.memset(projT_sb[:, 8, :], 0.0))
        V(lambda: v.memset(iw_c[8], 0.0))
        V(lambda: v.memset(xn_fm[:, 8, :], 0.0))
        V(lambda: v.memset(ones_nh[:, :], -0.5))
        V(lambda: v.memset(c_eps[:, :], EPS))
        V(lambda: v.memset(c_nhalf[:, :], -0.5))
        V(lambda: v.memset(ones_b[:, :], 1.0))
        V(lambda: v.memset(ones_c[:, :], 1.0))
        V(lambda: v.memset(ones_row[:, :], 1.0))
        cast_sem["xisrow"] = Ctr(nc, stack, "cs_xisrow", g)
        g.wait_ge(cV.sem, ms_p8)
        g.dma_start(projT_sb[0:1, 8, :], xis[None, :]).then_inc(cast_sem["xisrow"].sem, 16)
        cast_sem["xisrow"].n = 16
        cast_sem["projT"] = Ctr(nc, stack, "cs_projT", g)
        g.dma_start(projT_sb[:, 0:8, :], projT.rearrange("(j p) r -> p j r", p=128))\
            .then_inc(cast_sem["projT"].sem, 16)
        cast_sem["projT"].n = 16

        vec_jobs = [
            (xis, 8, ab_col, 8), (w1s, 32, w1s_c, 32), (w2s, 32, w2s_c, 32),
            (h1s, 32, h1s_c, 32), (h2s, 32, h2s_c, 32), (w3s, 8, w3s_c, 8),
        ]
        for vi, (vsrc, nparts, dst, ncols) in enumerate(vec_jobs):
            sy.dma_start(vec_sb[0:nparts, vi, :], vsrc.rearrange("(c p) -> c p", p=128))\
                .then_inc(d_vecj[vi].sem, 16)
            d_vecj[vi].n += 16
        for vi, (vsrc, nparts, dst, ncols) in enumerate(vec_jobs):
            need(t, d_vecj[vi], 16)
            need(t, d_ident, 16)
            bk = 4 + (vi % 4)
            guard_bank(bk)
            vt = cT.inc(t.transpose(p[bk][0:128, 0:ncols], vec_sb[0:nparts, vi, :],
                                    identf[0:nparts, 0:nparts]))
            need(v, cT, vt)
            ev = V(lambda dst=dst, ncols=ncols, bk=bk: v.tensor_copy(dst[:, 0:ncols],
                                                                    p[bk][:, 0:ncols]))
            bank_guard[bk] = (cV, ev)

        V(lambda: v.tensor_scalar_add(bias_w[:, :], ab_col[:, :], -LN32))
        V(lambda: v.tensor_mul(stat[:, 0:8], ab_col[:, :], ab_col[:, :]))
        V(lambda: v.scalar_tensor_tensor(bias_f[:, :], stat[:, 0:8], -512.0,
                                         bias_w[:, :], op0=ALU.mult, op1=ALU.add))
        cols_ready = cV.n

        ring_seq = 0
        slot_last = [0, 0, 0, 0, 0]

        def ring_load(nm, strip, kk8, first):
            nonlocal ring_seq
            slot = ring_seq % 5
            fsrc = srcs[nm][strip] if nm in ("w1", "w2") \
                else srcs[nm][strip][:, kk8 * 8:(kk8 + 1) * 8, :]
            need(g, cT, slot_last[slot])
            g.dma_start(wring[:, slot, :, :], fsrc).then_inc(d_wr[slot].sem, 16)
            d_wr[slot].n += 16
            ring_seq += 1
            return slot, d_wr[slot], d_wr[slot].n

        xsq = feats
        ln_state = {}

        def ln2_squares(s_, smm_last):
            need(v, cT, smm_last)
            need(v, d_xts, d_xts.n)
            for j in range(8):
                V(lambda j=j: v.tensor_mul(xsq[:, j, :], xt_sweep[:, j, :],
                                           xt_sweep[:, j, :]))
            ln_state["sq"] = cV.n

        def ln2_stats(s_):
            need(t, cV, ln_state["sq"])
            guard_bank(6)
            guard_bank(7)
            for j in range(8):
                t.matmul(p[6][0:1, :], ones_b[:, 0:1], xt_sweep[:, j, :],
                         start=(j == 0), stop=(j == 7))
                ii = t.matmul(p[7][0:1, :], ones_b[:, 0:1], xsq[:, j, :],
                              start=(j == 0), stop=(j == 7))
            st = cT.inc(ii)
            need(v, cT, st)
            V(lambda: v.tensor_copy(rows[0:1, 0, :], p[6][0:1, :]))
            ev = V(lambda: v.tensor_copy(rows[0:1, 1, :], p[7][0:1, :]))
            bank_guard[6] = (cV, ev)
            bank_guard[7] = (cV, ev)
            V(lambda: v.tensor_scalar_mul(rows[0:1, 0, :], rows[0:1, 0, :], 1.0 / D))
            V(lambda: v.tensor_scalar_mul(rows[0:1, 1, :], rows[0:1, 1, :], 1.0 / D))
            V(lambda: v.tensor_mul(rows[0:1, 2, :], rows[0:1, 0, :], rows[0:1, 0, :]))
            sv = V(lambda: v.tensor_sub(rows[0:1, 1, :], rows[0:1, 1, :], rows[0:1, 2, :]))
            need(sc, cV, sv)
            S(lambda: sc.activation(rows[0:1, 2, :], rows[0:1, 1, :], AF.Ln,
                                    bias=c_eps[0:1, 0:1]))
            rs = S(lambda: sc.activation(rows[0:1, 2, :], rows[0:1, 2, :], AF.Exp,
                                         scale=c_nhalf[0:1, 0:1]))
            need(v, cS, rs)
            V(lambda: v.tensor_mul(rows[0:1, 3, :], rows[0:1, 0, :], rows[0:1, 2, :]))
            rd = V(lambda: v.scalar_tensor_tensor(rows[0:1, 1, :], rows[0:1, 2, :],
                                                  512.0 * EPS, rows[0:1, 2, :],
                                                  op0=ALU.mult, op1=ALU.mult))
            ln_state["rows"] = rd

        def ln2_bcast_affine(s_, ffn1_last):
            need(t, cV, ln_state["rows"])
            guard_bank(4)
            guard_bank(5)
            bc1 = cT.inc(t.matmul(p[4][:, :], ones_row[0:1, :], rows[0:1, 2, :],
                                  start=True, stop=True))
            bc2 = cT.inc(t.matmul(p[5][:, :], ones_row[0:1, :], rows[0:1, 3, :],
                                  start=True, stop=True))
            need(v, cT, bc2)
            need(v, cT, ffn1_last)
            for j in range(8):
                V(lambda j=j: v.tensor_mul(xn_fm[:, j, :], xt_sweep[:, j, :], p[4][:, :]))
                V(lambda j=j: v.tensor_sub(xn_fm[:, j, :], xn_fm[:, j, :], p[5][:, :]))
            xr = V(lambda: v.tensor_copy(xn_fm[0:1, 8, :], rows[0:1, 1, :]))
            bank_guard[4] = (cV, xr)
            bank_guard[5] = (cV, xr)
            xts_state["guard"] = xr
            return xr

        ffn4_done_T = 0
        t1_guard = {}
        t2_guard = {}
        y_val = {}
        xt_guard = {}

        need(v, cast_sem["iwT"], 16)
        sq_done = 0
        for j in range(8):
            sq_done = V(lambda j=j: v.tensor_mul(
                h_sb[:, 2 * j:2 * j + 2, :].rearrange("p a b -> p (a b)"),
                iw_c[j], iw_c[j]))

        wstate = {}

        def emit_so():
            need(t, cV, sq_done)
            need(t, cast_sem["iwT"], 16)
            guard_bank(2)
            guard_bank(3)
            for j in range(8):
                sqv = h_sb[:, 2 * j:2 * j + 2, :].rearrange("p a b -> p (a b)")
                t.matmul(p[2][0:1, :], ones_nh[:, 0:1], sqv[:, 0:512],
                         start=(j == 0), stop=(j == 7))
                ii = t.matmul(p[3][0:1, :], ones_nh[:, 0:1], sqv[:, 512:1024],
                              start=(j == 0), stop=(j == 7))
            so_done = cT.inc(ii)
            wstate["so_mm"] = so_done
            need(sc, cT, so_done)
            e1 = S(lambda: sc.copy(iw_c[8][0:1, 0:512], p[2][0:1, :]))
            e2 = S(lambda: sc.copy(iw_c[8][0:1, 512:1024], p[3][0:1, :]))
            bank_guard[2] = (cS, e1)
            bank_guard[3] = (cS, e2)
            wstate["e2"] = e2

        def emit_wmms():
            need(t, cast_sem["projT"], 16)
            need(t, cast_sem["xisrow"], 16)
            need(t, cS, wstate["e2"])
            need(t, cV, cols_ready)
            ev = 0
            for rc in range(8):
                for oh in range(2):
                    bk = 2 + oh
                    guard_bank(bk)
                    for j in range(9):
                        ii = t.matmul(p[bk][:, :], projT_sb[:, j, rc * 128:(rc + 1) * 128],
                                      iw_c[j][:, oh * 512:(oh + 1) * 512],
                                      start=(j == 0), stop=(j == 8))
                    stop = cT.inc(ii)
                    need(sc, cT, stop)
                    ev = S(lambda rc=rc, oh=oh, bk=bk: sc.activation(
                        Wt_sb[:, rc, oh * 512:(oh + 1) * 512], p[bk][:, :], AF.Exp,
                        bias=bias_w[:, rc:rc + 1], scale=ab_col[:, rc:rc + 1]))
                    bank_guard[bk] = (cS, ev)
            wstate["wt_done"] = ev

        emit_so()
        ln2_squares(0, 0)
        ln2_stats(0)
        xn_ready_sweep = ln2_bcast_affine(0, 0)
        emit_wmms()

        def emit_dot0(xn_rdy):
            need(t, cast_sem["projT"], 16)
            need(t, cast_sem["xisrow"], 16)
            need(t, cV, xn_rdy)
            need(t, cV, cols_ready)
            for rc in range(8):
                bk = (6, 7, 4, 5)[rc % 4]
                guard_bank(bk)
                for j in range(9):
                    ii = t.matmul(p[bk][:, :], projT_sb[:, j, rc * 128:(rc + 1) * 128],
                                  xn_fm[:, j, :], start=(j == 0), stop=(j == 8))
                stop = cT.inc(ii)
                need(sc, cT, stop)
                ev = S(lambda rc=rc, bk=bk: sc.activation(
                    feats[:, rc, :], p[bk][:, :], AF.Exp,
                    bias=bias_f[:, rc:rc + 1], scale=ab_col[:, rc:rc + 1]))
                bank_guard[bk] = (cS, ev)
            return cS.n

        def emit_smm(feats_done):
            need(t, cS, feats_done)
            need(t, cS, wstate["wt_done"])
            for oc in range(8):
                bk = oc
                guard_bank(bk)
                for rc in range(8):
                    ii = t.matmul(p[bk][:, :], Wt_sb[:, rc, oc * 128:(oc + 1) * 128],
                                  feats[:, rc, :], start=(rc == 0), stop=(rc == 7))
                stop = cT.inc(ii)
                need(v, cT, stop)
                need(v, d_y[oc], y_val.get(oc, 0))
                ev = V(lambda oc=oc, bk=bk: v.tensor_copy(s_sb[:, oc, :], p[bk][:, :]))
                bank_guard[bk] = (cV, ev)
            return cT.n

        smm_next = 0
        feats_next = 0
        for s_ in range(NS):
            tok0 = s_ * TS

            h_by_strip = {}

            def ffn1_strip(ss):
                slot1, wc1, wv1 = ring_load("w1", ss, 0, s_ == 0)
                slot2, wc2, wv2 = ring_load("w2", ss, 0, s_ == 0)
                for hh in range(2):
                    ba1 = 0 if hh == 0 else 4
                    ba2 = 2 if hh == 0 else 6
                    need(t, wc1, wv1)
                    for og in range(2):
                        guard_bank(ba1 + og)
                    for k in range(8):
                        for og in range(2):
                            ii = t.matmul(p[ba1 + og][:, :],
                                          wring[:, slot1, k, (hh * 2 + og) * 128:(hh * 2 + og + 1) * 128],
                                          xn_fm[:, k, :], start=(k == 0), stop=(k == 7))
                    a1_stop = cT.inc(ii)
                    if hh == 1:
                        slot_last[slot1] = max(slot_last[slot1], a1_stop)
                    need(sc, cT, a1_stop)
                    last_silu = 0
                    for og in range(2):
                        oc = ss * 4 + hh * 2 + og
                        sl = hh * 2 + og
                        tg = t1_guard.get(sl)
                        if tg:
                            need(sc, cV, tg)
                        ev = S(lambda sl=sl, og=og, oc=oc, ba1=ba1: sc.activation(
                            t1[:, sl, :], p[ba1 + og][:, :], AF.Silu, scale=w1s_c[:, oc:oc + 1]))
                        bank_guard[ba1 + og] = (cS, ev)
                        last_silu = ev
                    need(t, wc2, wv2)
                    for og in range(2):
                        guard_bank(ba2 + og)
                    for k in range(8):
                        for og in range(2):
                            ii = t.matmul(p[ba2 + og][:, :],
                                          wring[:, slot2, k, (hh * 2 + og) * 128:(hh * 2 + og + 1) * 128],
                                          xn_fm[:, k, :], start=(k == 0), stop=(k == 7))
                    a2_stop = cT.inc(ii)
                    if hh == 1:
                        slot_last[slot2] = max(slot_last[slot2], a2_stop)
                    need(v, cT, a2_stop)
                    need(v, cS, last_silu)
                    need(v, cT, ffn4_done_T)
                    if s_ == 0:
                        need(v, cT, wstate["so_mm"])
                    for og in range(2):
                        oc = ss * 4 + hh * 2 + og
                        sl = hh * 2 + og
                        ev = V(lambda oc=oc, og=og, sl=sl, ba2=ba2: v.scalar_tensor_tensor(
                            h_sb[:, oc, :], p[ba2 + og][:, :], w2s_c[:, oc:oc + 1],
                            t1[:, sl, :], op0=ALU.mult, op1=ALU.mult))
                        bank_guard[ba2 + og] = (cV, ev)
                        t1_guard[sl] = ev
                h_by_strip[ss] = cV.n

            if s_ == 0:
                feats_done = emit_dot0(xn_ready_sweep)
                ffn1_strip(0)
                ffn1_strip(1)
                smm_last = emit_smm(feats_done)
                first_strip = 2
            else:
                smm_last = smm_next
                first_strip = 0

            for ss in range(first_strip, 8):
                ffn1_strip(ss)
                if ss == 2 and s_ + 1 < NS:
                    emit_xts_loads(s_ + 1)
                if ss == 3 and s_ + 1 < NS:
                    ln2_squares(s_ + 1, smm_last)
                if ss == 5 and s_ + 1 < NS:
                    ln2_stats(s_ + 1)
            ffn1_last = cT.n

            if s_ + 1 < NS:
                xn_ready_sweep = ln2_bcast_affine(s_ + 1, ffn1_last)

            g1_by_strip = {}
            g2_by_strip = {}
            for nm, src_sb, dst_sb, s_col, src_rdy, dst_rec in (
                    ("h1", h_sb, g1_sb, h1s_c, (cV, h_by_strip), g1_by_strip),
                    ("h2", g1_sb, h_sb, h2s_c, (cS, g1_by_strip), g2_by_strip)):
                rdy_ctr, rdy_map = src_rdy
                for ss in range(8):
                    quad = [0, 1, 2, 3] if ss % 2 == 0 else [4, 5, 6, 7]
                    for bi in quad:
                        guard_bank(bi)
                    for kk in range(4):
                        need(t, rdy_ctr, rdy_map[2 * kk + 1])
                        slot, wc, wv = ring_load(nm, ss, kk, s_ == 0)
                        need(t, wc, wv)
                        for k8 in range(8):
                            for og in range(4):
                                ii = t.matmul(p[quad[og]][:, :],
                                              wring[:, slot, k8, og * 128:(og + 1) * 128],
                                              src_sb[:, kk * 8 + k8, :],
                                              start=(kk == 0 and k8 == 0),
                                              stop=(kk == 3 and k8 == 7))
                        slot_last[slot] = cT.inc(ii)
                    stop = slot_last[slot]
                    need(sc, cT, stop)
                    og_order = (2, 3, 0, 1) if (nm == "h2" and ss == 7) else range(4)
                    for og in og_order:
                        oc = ss * 4 + og
                        ev = S(lambda dst_sb=dst_sb, oc=oc, og=og, quad=quad, s_col=s_col:
                               sc.activation(dst_sb[:, oc, :], p[quad[og]][:, :], AF.Gelu,
                                             scale=s_col[:, oc:oc + 1]))
                        bank_guard[quad[og]] = (cS, ev)
                    dst_rec[ss] = cS.n

            if s_ + 1 < NS:
                feats_next = emit_dot0(xn_ready_sweep)

            xt_val = {}

            def xt_load(dc):
                slot = dc % 4
                need(sy, cV, xt_guard.get(slot, 0))
                sy.dma_start(xt_r[:, slot, :],
                             xT[dc * 128:(dc + 1) * 128, tok0:tok0 + TS])\
                    .then_inc(d_xt[slot].sem, 16)
                d_xt[slot].n += 16
                xt_val[dc] = d_xt[slot].n

            for dc in range(4):
                xt_load(dc)

            sg_done = [0] * 8
            for ss in range(2):
                quad = [0, 1, 2, 3] if ss == 0 else [4, 5, 2, 3]
                for bi in quad:
                    guard_bank(bi)
                for kk in range(4):
                    need(t, cS, g2_by_strip[2 * kk + 1])
                    slot, wc, wv = ring_load("w3", ss, kk, s_ == 0)
                    need(t, wc, wv)
                    for k8 in range(8):
                        for og in range(4):
                            ii = t.matmul(p[quad[og]][:, :],
                                          wring[:, slot, k8, og * 128:(og + 1) * 128],
                                          h_sb[:, kk * 8 + k8, :],
                                          start=(kk == 0 and k8 == 0),
                                          stop=(kk == 3 and k8 == 7))
                    slot_last[slot] = cT.inc(ii)
                ffn4_done_T = slot_last[slot]
                need(sc, cT, ffn4_done_T)
                for og in range(4):
                    dc = ss * 4 + og
                    tg = t2_guard.get(og)
                    if tg:
                        need(sc, cV, tg)
                    ev = S(lambda og=og, dc=dc, quad=quad: sc.activation(
                        t2[:, og, :], p[quad[og]][:, :], AF.Silu,
                        scale=w3s_c[:, dc:dc + 1]))
                    bank_guard[quad[og]] = (cS, ev)
                for og in range(4):
                    dc = ss * 4 + og
                    need(v, cS, bank_guard[quad[og]][1] if bank_guard[quad[og]][0] is cS else 0)
                    mv = V(lambda og=og, dc=dc: v.tensor_mul(
                        s_sb[:, dc, :], t2[:, og, :], s_sb[:, dc, :]))
                    t2_guard[og] = mv
                    sg_done[dc] = mv
                for og in range(4):
                    dc = ss * 4 + og
                    slot = dc % 4
                    need(v, d_xt[slot], xt_val[dc])
                    av = V(lambda dc=dc, slot=slot: v.tensor_add(
                        s_sb[:, dc, :], xt_r[:, slot, :], s_sb[:, dc, :]))
                    xt_guard[slot] = av
                    need(sy, cV, av)
                    sy.dma_start(y[dc * 128:(dc + 1) * 128, tok0:tok0 + TS],
                                 s_sb[:, dc, :]).then_inc(d_y[dc].sem, 16)
                    d_y[dc].n += 16
                    y_val[dc] = d_y[dc].n
                if ss == 0:
                    for dc in range(4, 8):
                        xt_load(dc)

            if s_ + 1 < NS:
                smm_next = emit_smm(feats_next)

        for slot in range(8):
            g.wait_ge(d_y[slot].sem, d_y[slot].n)
    return nc


def _pack(w):
    O, I = w.shape
    wt = np.ascontiguousarray(w.T)
    arr = wt.reshape(I // 128, 128, O // 512, 512)
    return np.ascontiguousarray(arr.transpose(2, 1, 0, 3))


_NC_CACHE = {}


def kernel(L0_attn_x, L0_attn_input_weights, L0_attn_xis, L0_attn_proj_matrix,
           L0_ffn_w1_w, L0_ffn_w1_s, L0_ffn_w2_w, L0_ffn_w2_s,
           L0_ffn_h1_w, L0_ffn_h1_s, L0_ffn_h2_w, L0_ffn_h2_s,
           L0_ffn_w3_w, L0_ffn_w3_s):
    x = np.ascontiguousarray(np.asarray(L0_attn_x, dtype=np.float32)).reshape(-1, D)
    shared = {
        "projT": np.ascontiguousarray(np.asarray(L0_attn_proj_matrix, np.float32).T),
        "iwT": np.ascontiguousarray(np.asarray(L0_attn_input_weights, np.float32).T),
        "xis": np.ascontiguousarray(np.asarray(L0_attn_xis, np.float32)),
        "w1p": _pack(np.asarray(L0_ffn_w1_w, np.float32)),
        "w2p": _pack(np.asarray(L0_ffn_w2_w, np.float32)),
        "h1p": _pack(np.asarray(L0_ffn_h1_w, np.float32)),
        "h2p": _pack(np.asarray(L0_ffn_h2_w, np.float32)),
        "w3p": _pack(np.asarray(L0_ffn_w3_w, np.float32)),
        "w1s": np.ascontiguousarray(np.asarray(L0_ffn_w1_s, np.float32)),
        "w2s": np.ascontiguousarray(np.asarray(L0_ffn_w2_s, np.float32)),
        "h1s": np.ascontiguousarray(np.asarray(L0_ffn_h1_s, np.float32)),
        "h2s": np.ascontiguousarray(np.asarray(L0_ffn_h2_s, np.float32)),
        "w3s": np.ascontiguousarray(np.asarray(L0_ffn_w3_s, np.float32)),
    }
    if "nc" not in _NC_CACHE:
        _NC_CACHE["nc"] = build_nc()
    nc = _NC_CACHE["nc"]
    in_maps = []
    for c in range(8):
        xc = np.ascontiguousarray(x[c * T:(c + 1) * T])
        m = {"xT": np.ascontiguousarray(xc.T)}
        m.update(shared)
        in_maps.append(m)
    res = run_bass_kernel_spmd(nc, in_maps, core_ids=list(range(8)))
    out = np.concatenate([np.ascontiguousarray(res.results[c]["y"].T) for c in range(8)], axis=0)
    return out.reshape(4, 4096, D)


if __name__ == "__main__":
    nc = build_nc()
    print("built ok, instructions:", nc.n_instructions())


# version 31 (final)
# speedup vs baseline: 1.0463x; 1.0028x over previous
import contextlib
import numpy as np
import ml_dtypes
import concourse.bass as bass
import concourse.mybir as mybir
from concourse.bass_utils import run_bass_kernel_spmd

F32 = mybir.dt.float32
BF16 = mybir.dt.bfloat16
AF = mybir.ActivationFunctionType
ALU = mybir.AluOpType
AX = mybir.AxisListType
bf16 = ml_dtypes.bfloat16

T = 2048
TS = 512
NS = T // TS
NT = TS // 128
D = 1024
H = 4096
R = 1024
LN32 = float(np.log(np.sqrt(1024.0)))
EPS = 1e-5


class Ctr:
    def __init__(self, nc, stack, name, eng):
        self.sem = stack.enter_context(nc.semaphore(name))
        self.n = 0
        self.eng = eng

    def inc(self, instr, k=1):
        instr.then_inc(self.sem, k)
        self.n += k
        return self.n


def build_nc():
    nc = bass.Bass("TRN2", target_bir_lowering=False)
    g = nc.gpsimd
    sy = nc.sync
    t = nc.tensor
    v = nc.vector
    sc = nc.scalar

    xT = nc.dram_tensor("xT", [D, T], F32, kind="ExternalInput").ap()
    projT = nc.dram_tensor("projT", [D, R], F32, kind="ExternalInput").ap()
    iwT = nc.dram_tensor("iwT", [D, R], F32, kind="ExternalInput").ap()
    xis = nc.dram_tensor("xis", [R], F32, kind="ExternalInput").ap()
    w1p = nc.dram_tensor("w1p", [8, 128, 8, 512], F32, kind="ExternalInput").ap()
    w2p = nc.dram_tensor("w2p", [8, 128, 8, 512], F32, kind="ExternalInput").ap()
    h1p = nc.dram_tensor("h1p", [8, 128, 32, 512], F32, kind="ExternalInput").ap()
    h2p = nc.dram_tensor("h2p", [8, 128, 32, 512], F32, kind="ExternalInput").ap()
    w3p = nc.dram_tensor("w3p", [2, 128, 32, 512], F32, kind="ExternalInput").ap()
    w1s = nc.dram_tensor("w1s", [H], F32, kind="ExternalInput").ap()
    w2s = nc.dram_tensor("w2s", [H], F32, kind="ExternalInput").ap()
    h1s = nc.dram_tensor("h1s", [H], F32, kind="ExternalInput").ap()
    h2s = nc.dram_tensor("h2s", [H], F32, kind="ExternalInput").ap()
    w3s = nc.dram_tensor("w3s", [D], F32, kind="ExternalInput").ap()
    y = nc.dram_tensor("y", [D, T], F32, kind="ExternalOutput").ap()

    identf_h = nc.inline_tensor(np.eye(128).astype(np.float32), "ident_f")

    with contextlib.ExitStack() as stack:
        E = stack.enter_context

        def sb(name, shape, dtype):
            return E(nc.sbuf_tensor(name, shape, dtype))

        projT_sb = sb("projT_sb", [128, 9, R], BF16)
        Wt_sb = sb("Wt_sb", [128, 8, R], BF16)
        xt_sweep = sb("xt_sweep", [128, 8, TS], BF16)
        xt_r = sb("xt_r", [128, 4, 512], F32)
        rows = sb("rows", [128, 4, TS], F32)
        xn_fm = sb("xn_fm", [128, 9, TS], BF16)
        feats = sb("feats", [128, 8, TS], BF16)
        s_sb = sb("s_sb", [128, 8, TS], F32)
        h_sb = sb("h_sb", [128, 32, TS], BF16)
        g1_sb = sb("g1_sb", [128, 32, TS], BF16)
        wring = sb("wring", [128, 5, 8, TS], BF16)
        t1 = sb("t1", [128, 4, TS], BF16)
        t2 = sb("t2", [128, 4, TS], BF16)
        identf = sb("identf", [128, 128], F32)
        vec_sb = sb("vec_sb", [32, 6, 128], F32)
        ab_col = sb("ab_col", [128, 8], F32)
        bias_f = sb("bias_f", [128, 8], F32)
        bias_w = sb("bias_w", [128, 8], F32)
        w1s_c = sb("w1s_c", [128, 32], F32)
        w2s_c = sb("w2s_c", [128, 32], F32)
        h1s_c = sb("h1s_c", [128, 32], F32)
        h2s_c = sb("h2s_c", [128, 32], F32)
        w3s_c = sb("w3s_c", [128, 8], F32)
        stat = sb("stat", [128, 16], F32)
        ones_nh = sb("ones_nh", [128, 1], BF16)
        ones_b = sb("ones_b", [128, 1], BF16)
        ones_c = sb("ones_c", [128, 1], F32)
        ones_row = sb("ones_row", [1, 128], F32)
        c_eps = sb("c_eps", [128, 1], F32)
        c_nhalf = sb("c_nhalf", [128, 1], F32)

        iw_view = g1_sb.ap().rearrange("p a b -> p (a b)")
        iw_c = [iw_view[:, j * R:(j + 1) * R] for j in range(9)]

        p = [E(nc.psum_tensor(f"pb{i}", [128, 512], F32)) for i in range(8)]

        cT = Ctr(nc, stack, "cT", t)
        cV = Ctr(nc, stack, "cV", v)
        cS = Ctr(nc, stack, "cS", sc)
        d_xts = Ctr(nc, stack, "d_xts", g)
        d_xt = [Ctr(nc, stack, f"d_xt{i}", sy) for i in range(4)]
        d_y = [Ctr(nc, stack, f"d_y{i}", sy) for i in range(8)]
        d_wr = [Ctr(nc, stack, f"d_wr{i}", g) for i in range(5)]
        d_ident = Ctr(nc, stack, "d_ident", sy)
        d_misc = Ctr(nc, stack, "d_misc", sy)
        d_vecj = [Ctr(nc, stack, f"d_vecj{i}", sy) for i in range(6)]
        cast_sem = {}

        waited = {}

        def need(eng, ctr, val, allow_self=False):
            if ctr is None or val <= 0:
                return
            if ctr.eng is eng and not allow_self:
                return
            key = (id(eng), id(ctr.sem))
            if waited.get(key, 0) >= val:
                return
            eng.wait_ge(ctr.sem, val)
            waited[key] = val

        bank_guard = {}

        def guard_bank(i):
            gd = bank_guard.get(i)
            if gd is not None:
                need(t, gd[0], gd[1])

        def V(instr_fn):
            need(v, cV, cV.n, allow_self=True)
            return cV.inc(instr_fn())

        def S(instr_fn):
            need(sc, cS, cS.n, allow_self=True)
            return cS.inc(instr_fn())

        cast_sem["iwT"] = Ctr(nc, stack, "cs_iwT", g)
        g.dma_start(iw_view[:, 0:8 * R].rearrange("p (j r) -> p j r", j=8),
                    iwT.rearrange("(j p) r -> p j r", p=128)).then_inc(cast_sem["iwT"].sem, 16)
        cast_sem["iwT"].n = 16

        xts_state = {"guard": 0}

        def emit_xts_loads(s_):
            need(g, cV, xts_state["guard"])
            for j in range(8):
                g.dma_start(xt_sweep[:, j, :],
                            xT[j * 128:(j + 1) * 128, s_ * TS:(s_ + 1) * TS])\
                    .then_inc(d_xts.sem, 16)
                d_xts.n += 16

        emit_xts_loads(0)

        srcs = {"w1": w1p, "w2": w2p, "h1": h1p, "h2": h2p, "w3": w3p}

        sy.dma_start(identf[:, :], identf_h.ap()).then_inc(d_ident.sem, 16)
        d_ident.n += 16

        ms_p8 = V(lambda: v.memset(projT_sb[:, 8, :], 0.0))
        V(lambda: v.memset(iw_c[8], 0.0))
        V(lambda: v.memset(xn_fm[:, 8, :], 0.0))
        V(lambda: v.memset(ones_nh[:, :], -0.5))
        V(lambda: v.memset(c_eps[:, :], EPS))
        V(lambda: v.memset(c_nhalf[:, :], -0.5))
        V(lambda: v.memset(ones_b[:, :], 1.0))
        V(lambda: v.memset(ones_c[:, :], 1.0))
        V(lambda: v.memset(ones_row[:, :], 1.0))
        cast_sem["xisrow"] = Ctr(nc, stack, "cs_xisrow", g)
        g.wait_ge(cV.sem, ms_p8)
        g.dma_start(projT_sb[0:1, 8, :], xis[None, :]).then_inc(cast_sem["xisrow"].sem, 16)
        cast_sem["xisrow"].n = 16
        cast_sem["projT"] = Ctr(nc, stack, "cs_projT", g)
        g.dma_start(projT_sb[:, 0:8, :], projT.rearrange("(j p) r -> p j r", p=128))\
            .then_inc(cast_sem["projT"].sem, 16)
        cast_sem["projT"].n = 16

        vec_jobs = [
            (xis, 8, ab_col, 8), (w1s, 32, w1s_c, 32), (w2s, 32, w2s_c, 32),
            (h1s, 32, h1s_c, 32), (h2s, 32, h2s_c, 32), (w3s, 8, w3s_c, 8),
        ]
        for vi, (vsrc, nparts, dst, ncols) in enumerate(vec_jobs):
            sy.dma_start(vec_sb[0:nparts, vi, :], vsrc.rearrange("(c p) -> c p", p=128))\
                .then_inc(d_vecj[vi].sem, 16)
            d_vecj[vi].n += 16
        for vi, (vsrc, nparts, dst, ncols) in enumerate(vec_jobs):
            need(t, d_vecj[vi], 16)
            need(t, d_ident, 16)
            bk = 4 + (vi % 4)
            guard_bank(bk)
            vt = cT.inc(t.transpose(p[bk][0:128, 0:ncols], vec_sb[0:nparts, vi, :],
                                    identf[0:nparts, 0:nparts]))
            need(v, cT, vt)
            ev = V(lambda dst=dst, ncols=ncols, bk=bk: v.tensor_copy(dst[:, 0:ncols],
                                                                    p[bk][:, 0:ncols]))
            bank_guard[bk] = (cV, ev)

        V(lambda: v.tensor_scalar_add(bias_w[:, :], ab_col[:, :], -LN32))
        V(lambda: v.tensor_mul(stat[:, 0:8], ab_col[:, :], ab_col[:, :]))
        V(lambda: v.scalar_tensor_tensor(bias_f[:, :], stat[:, 0:8], -512.0,
                                         bias_w[:, :], op0=ALU.mult, op1=ALU.add))
        cols_ready = cV.n

        ring_seq = 0
        slot_last = [0, 0, 0, 0, 0]

        def ring_load(nm, strip, kk8, first):
            nonlocal ring_seq
            slot = ring_seq % 5
            fsrc = srcs[nm][strip] if nm in ("w1", "w2") \
                else srcs[nm][strip][:, kk8 * 8:(kk8 + 1) * 8, :]
            need(g, cT, slot_last[slot])
            g.dma_start(wring[:, slot, :, :], fsrc).then_inc(d_wr[slot].sem, 16)
            d_wr[slot].n += 16
            ring_seq += 1
            return slot, d_wr[slot], d_wr[slot].n

        xsq = feats
        ln_state = {}

        def ln2_squares(s_, smm_last):
            need(v, cT, smm_last)
            need(v, d_xts, d_xts.n)
            for j in range(8):
                V(lambda j=j: v.tensor_mul(xsq[:, j, :], xt_sweep[:, j, :],
                                           xt_sweep[:, j, :]))
            ln_state["sq"] = cV.n

        def ln2_stats(s_):
            need(t, cV, ln_state["sq"])
            guard_bank(6)
            guard_bank(7)
            for j in range(8):
                t.matmul(p[6][0:1, :], ones_b[:, 0:1], xt_sweep[:, j, :],
                         start=(j == 0), stop=(j == 7))
                ii = t.matmul(p[7][0:1, :], ones_b[:, 0:1], xsq[:, j, :],
                              start=(j == 0), stop=(j == 7))
            st = cT.inc(ii)
            need(v, cT, st)
            V(lambda: v.tensor_copy(rows[0:1, 0, :], p[6][0:1, :]))
            ev = V(lambda: v.tensor_copy(rows[0:1, 1, :], p[7][0:1, :]))
            bank_guard[6] = (cV, ev)
            bank_guard[7] = (cV, ev)
            V(lambda: v.tensor_scalar_mul(rows[0:1, 0, :], rows[0:1, 0, :], 1.0 / D))
            V(lambda: v.tensor_scalar_mul(rows[0:1, 1, :], rows[0:1, 1, :], 1.0 / D))
            V(lambda: v.tensor_mul(rows[0:1, 2, :], rows[0:1, 0, :], rows[0:1, 0, :]))
            sv = V(lambda: v.tensor_sub(rows[0:1, 1, :], rows[0:1, 1, :], rows[0:1, 2, :]))
            need(sc, cV, sv)
            S(lambda: sc.activation(rows[0:1, 2, :], rows[0:1, 1, :], AF.Ln,
                                    bias=c_eps[0:1, 0:1]))
            rs = S(lambda: sc.activation(rows[0:1, 2, :], rows[0:1, 2, :], AF.Exp,
                                         scale=c_nhalf[0:1, 0:1]))
            need(v, cS, rs)
            V(lambda: v.tensor_mul(rows[0:1, 3, :], rows[0:1, 0, :], rows[0:1, 2, :]))
            rd = V(lambda: v.scalar_tensor_tensor(rows[0:1, 1, :], rows[0:1, 2, :],
                                                  512.0 * EPS, rows[0:1, 2, :],
                                                  op0=ALU.mult, op1=ALU.mult))
            ln_state["rows"] = rd

        def ln2_bcast_affine(s_, ffn1_last):
            need(t, cV, ln_state["rows"])
            guard_bank(4)
            guard_bank(5)
            bc1 = cT.inc(t.matmul(p[4][:, :], ones_row[0:1, :], rows[0:1, 2, :],
                                  start=True, stop=True))
            bc2 = cT.inc(t.matmul(p[5][:, :], ones_row[0:1, :], rows[0:1, 3, :],
                                  start=True, stop=True))
            need(v, cT, bc2)
            need(v, cT, ffn1_last)
            for j in range(8):
                V(lambda j=j: v.tensor_mul(xn_fm[:, j, :], xt_sweep[:, j, :], p[4][:, :]))
                V(lambda j=j: v.tensor_sub(xn_fm[:, j, :], xn_fm[:, j, :], p[5][:, :]))
            xr = V(lambda: v.tensor_copy(xn_fm[0:1, 8, :], rows[0:1, 1, :]))
            bank_guard[4] = (cV, xr)
            bank_guard[5] = (cV, xr)
            xts_state["guard"] = xr
            return xr

        ffn4_done_T = 0
        t1_guard = {}
        t2_guard = {}
        y_val = {}
        xt_guard = {}

        need(v, cast_sem["iwT"], 16)
        sq_done = 0
        for j in range(8):
            sq_done = V(lambda j=j: v.tensor_mul(
                h_sb[:, 2 * j:2 * j + 2, :].rearrange("p a b -> p (a b)"),
                iw_c[j], iw_c[j]))

        wstate = {}

        def emit_so():
            need(t, cV, sq_done)
            need(t, cast_sem["iwT"], 16)
            guard_bank(2)
            guard_bank(3)
            for j in range(8):
                sqv = h_sb[:, 2 * j:2 * j + 2, :].rearrange("p a b -> p (a b)")
                t.matmul(p[2][0:1, :], ones_nh[:, 0:1], sqv[:, 0:512],
                         start=(j == 0), stop=(j == 7))
                ii = t.matmul(p[3][0:1, :], ones_nh[:, 0:1], sqv[:, 512:1024],
                              start=(j == 0), stop=(j == 7))
            so_done = cT.inc(ii)
            wstate["so_mm"] = so_done
            need(sc, cT, so_done)
            e1 = S(lambda: sc.copy(iw_c[8][0:1, 0:512], p[2][0:1, :]))
            e2 = S(lambda: sc.copy(iw_c[8][0:1, 512:1024], p[3][0:1, :]))
            bank_guard[2] = (cS, e1)
            bank_guard[3] = (cS, e2)
            wstate["e2"] = e2

        def emit_wmms():
            need(t, cast_sem["projT"], 16)
            need(t, cast_sem["xisrow"], 16)
            need(t, cS, wstate["e2"])
            need(t, cV, cols_ready)
            ev = 0
            for rc in range(8):
                for oh in range(2):
                    bk = 2 + oh
                    guard_bank(bk)
                    for j in range(9):
                        ii = t.matmul(p[bk][:, :], projT_sb[:, j, rc * 128:(rc + 1) * 128],
                                      iw_c[j][:, oh * 512:(oh + 1) * 512],
                                      start=(j == 0), stop=(j == 8))
                    stop = cT.inc(ii)
                    need(sc, cT, stop)
                    ev = S(lambda rc=rc, oh=oh, bk=bk: sc.activation(
                        Wt_sb[:, rc, oh * 512:(oh + 1) * 512], p[bk][:, :], AF.Exp,
                        bias=bias_w[:, rc:rc + 1], scale=ab_col[:, rc:rc + 1]))
                    bank_guard[bk] = (cS, ev)
            wstate["wt_done"] = ev

        emit_so()
        ln2_squares(0, 0)
        ln2_stats(0)
        xn_ready_sweep = ln2_bcast_affine(0, 0)
        emit_wmms()

        def emit_dot0(xn_rdy):
            need(t, cast_sem["projT"], 16)
            need(t, cast_sem["xisrow"], 16)
            need(t, cV, xn_rdy)
            need(t, cV, cols_ready)
            for rc in range(8):
                bk = (6, 7, 4, 5)[rc % 4]
                guard_bank(bk)
                for j in range(9):
                    ii = t.matmul(p[bk][:, :], projT_sb[:, j, rc * 128:(rc + 1) * 128],
                                  xn_fm[:, j, :], start=(j == 0), stop=(j == 8))
                stop = cT.inc(ii)
                need(sc, cT, stop)
                ev = S(lambda rc=rc, bk=bk: sc.activation(
                    feats[:, rc, :], p[bk][:, :], AF.Exp,
                    bias=bias_f[:, rc:rc + 1], scale=ab_col[:, rc:rc + 1]))
                bank_guard[bk] = (cS, ev)
            return cS.n

        def emit_smm(feats_done):
            need(t, cS, feats_done)
            need(t, cS, wstate["wt_done"])
            for oc in range(8):
                bk = oc
                guard_bank(bk)
                for rc in range(8):
                    ii = t.matmul(p[bk][:, :], Wt_sb[:, rc, oc * 128:(oc + 1) * 128],
                                  feats[:, rc, :], start=(rc == 0), stop=(rc == 7))
                stop = cT.inc(ii)
                need(v, cT, stop)
                need(v, d_y[oc], y_val.get(oc, 0))
                ev = V(lambda oc=oc, bk=bk: v.tensor_copy(s_sb[:, oc, :], p[bk][:, :]))
                bank_guard[bk] = (cV, ev)
            return cT.n

        smm_next = 0
        feats_next = 0
        for s_ in range(NS):
            tok0 = s_ * TS

            h_by_strip = {}

            def ffn1_strip(ss):
                slot1, wc1, wv1 = ring_load("w1", ss, 0, s_ == 0)
                slot2, wc2, wv2 = ring_load("w2", ss, 0, s_ == 0)
                for hh in range(2):
                    ba1 = 0 if hh == 0 else 4
                    ba2 = 2 if hh == 0 else 6
                    need(t, wc1, wv1)
                    for og in range(2):
                        guard_bank(ba1 + og)
                    for k in range(8):
                        for og in range(2):
                            ii = t.matmul(p[ba1 + og][:, :],
                                          wring[:, slot1, k, (hh * 2 + og) * 128:(hh * 2 + og + 1) * 128],
                                          xn_fm[:, k, :], start=(k == 0), stop=(k == 7))
                    a1_stop = cT.inc(ii)
                    if hh == 1:
                        slot_last[slot1] = max(slot_last[slot1], a1_stop)
                    need(sc, cT, a1_stop)
                    last_silu = 0
                    for og in range(2):
                        oc = ss * 4 + hh * 2 + og
                        sl = hh * 2 + og
                        tg = t1_guard.get(sl)
                        if tg:
                            need(sc, cV, tg)
                        ev = S(lambda sl=sl, og=og, oc=oc, ba1=ba1: sc.activation(
                            t1[:, sl, :], p[ba1 + og][:, :], AF.Silu, scale=w1s_c[:, oc:oc + 1]))
                        bank_guard[ba1 + og] = (cS, ev)
                        last_silu = ev
                    need(t, wc2, wv2)
                    for og in range(2):
                        guard_bank(ba2 + og)
                    for k in range(8):
                        for og in range(2):
                            ii = t.matmul(p[ba2 + og][:, :],
                                          wring[:, slot2, k, (hh * 2 + og) * 128:(hh * 2 + og + 1) * 128],
                                          xn_fm[:, k, :], start=(k == 0), stop=(k == 7))
                    a2_stop = cT.inc(ii)
                    if hh == 1:
                        slot_last[slot2] = max(slot_last[slot2], a2_stop)
                    need(v, cT, a2_stop)
                    need(v, cS, last_silu)
                    need(v, cT, ffn4_done_T)
                    if s_ == 0:
                        need(v, cT, wstate["so_mm"])
                    for og in range(2):
                        oc = ss * 4 + hh * 2 + og
                        sl = hh * 2 + og
                        ev = V(lambda oc=oc, og=og, sl=sl, ba2=ba2: v.scalar_tensor_tensor(
                            h_sb[:, oc, :], p[ba2 + og][:, :], w2s_c[:, oc:oc + 1],
                            t1[:, sl, :], op0=ALU.mult, op1=ALU.mult))
                        bank_guard[ba2 + og] = (cV, ev)
                        t1_guard[sl] = ev
                h_by_strip[ss] = cV.n

            if s_ == 0:
                feats_done = emit_dot0(xn_ready_sweep)
                ffn1_strip(0)
                ffn1_strip(1)
                smm_last = emit_smm(feats_done)
                first_strip = 2
            else:
                smm_last = smm_next
                first_strip = 0

            for ss in range(first_strip, 8):
                ffn1_strip(ss)
                if ss == 2 and s_ + 1 < NS:
                    emit_xts_loads(s_ + 1)
                if ss == 3 and s_ + 1 < NS:
                    ln2_squares(s_ + 1, smm_last)
                if ss == 5 and s_ + 1 < NS:
                    ln2_stats(s_ + 1)
            ffn1_last = cT.n

            if s_ + 1 < NS:
                xn_ready_sweep = ln2_bcast_affine(s_ + 1, ffn1_last)

            g1_by_strip = {}
            g2_by_strip = {}
            for nm, src_sb, dst_sb, s_col, src_rdy, dst_rec in (
                    ("h1", h_sb, g1_sb, h1s_c, (cV, h_by_strip), g1_by_strip),
                    ("h2", g1_sb, h_sb, h2s_c, (cS, g1_by_strip), g2_by_strip)):
                rdy_ctr, rdy_map = src_rdy
                for ss in range(8):
                    quad = [0, 1, 2, 3] if ss % 2 == 0 else [4, 5, 6, 7]
                    for bi in quad:
                        guard_bank(bi)
                    for kk in range(4):
                        need(t, rdy_ctr, rdy_map[2 * kk + 1])
                        slot, wc, wv = ring_load(nm, ss, kk, s_ == 0)
                        need(t, wc, wv)
                        for k8 in range(8):
                            for og in range(4):
                                ii = t.matmul(p[quad[og]][:, :],
                                              wring[:, slot, k8, og * 128:(og + 1) * 128],
                                              src_sb[:, kk * 8 + k8, :],
                                              start=(kk == 0 and k8 == 0),
                                              stop=(kk == 3 and k8 == 7))
                        slot_last[slot] = cT.inc(ii)
                    stop = slot_last[slot]
                    need(sc, cT, stop)
                    og_order = (2, 3, 0, 1) if (nm == "h2" and ss == 7) else range(4)
                    for og in og_order:
                        oc = ss * 4 + og
                        ev = S(lambda dst_sb=dst_sb, oc=oc, og=og, quad=quad, s_col=s_col:
                               sc.activation(dst_sb[:, oc, :], p[quad[og]][:, :], AF.Gelu,
                                             scale=s_col[:, oc:oc + 1]))
                        bank_guard[quad[og]] = (cS, ev)
                    dst_rec[ss] = cS.n

            if s_ + 1 < NS:
                feats_next = emit_dot0(xn_ready_sweep)

            xt_val = {}

            def xt_load(dc):
                slot = dc % 4
                need(sy, cV, xt_guard.get(slot, 0))
                sy.dma_start(xt_r[:, slot, :],
                             xT[dc * 128:(dc + 1) * 128, tok0:tok0 + TS])\
                    .then_inc(d_xt[slot].sem, 16)
                d_xt[slot].n += 16
                xt_val[dc] = d_xt[slot].n

            for dc in range(4):
                xt_load(dc)

            sg_done = [0] * 8
            for ss in range(2):
                quad = [0, 1, 2, 3] if ss == 0 else [4, 5, 2, 3]
                for bi in quad:
                    guard_bank(bi)
                for kk in range(4):
                    need(t, cS, g2_by_strip[2 * kk + 1])
                    slot, wc, wv = ring_load("w3", ss, kk, s_ == 0)
                    need(t, wc, wv)
                    for k8 in range(8):
                        for og in range(4):
                            ii = t.matmul(p[quad[og]][:, :],
                                          wring[:, slot, k8, og * 128:(og + 1) * 128],
                                          h_sb[:, kk * 8 + k8, :],
                                          start=(kk == 0 and k8 == 0),
                                          stop=(kk == 3 and k8 == 7))
                    slot_last[slot] = cT.inc(ii)
                ffn4_done_T = slot_last[slot]
                need(sc, cT, ffn4_done_T)
                for og in range(4):
                    dc = ss * 4 + og
                    tg = t2_guard.get(og)
                    if tg:
                        need(sc, cV, tg)
                    ev = S(lambda og=og, dc=dc, quad=quad: sc.activation(
                        t2[:, og, :], p[quad[og]][:, :], AF.Silu,
                        scale=w3s_c[:, dc:dc + 1]))
                    bank_guard[quad[og]] = (cS, ev)
                for og in range(4):
                    dc = ss * 4 + og
                    need(v, cS, bank_guard[quad[og]][1] if bank_guard[quad[og]][0] is cS else 0)
                    mv = V(lambda og=og, dc=dc: v.tensor_mul(
                        s_sb[:, dc, :], t2[:, og, :], s_sb[:, dc, :]))
                    t2_guard[og] = mv
                    sg_done[dc] = mv
                for og in range(4):
                    dc = ss * 4 + og
                    slot = dc % 4
                    need(v, d_xt[slot], xt_val[dc])
                    av = V(lambda dc=dc, slot=slot: v.tensor_add(
                        s_sb[:, dc, :], xt_r[:, slot, :], s_sb[:, dc, :]))
                    xt_guard[slot] = av
                    need(sc, cV, av)
                    sc.dma_start(y[dc * 128:(dc + 1) * 128, tok0:tok0 + TS],
                                 s_sb[:, dc, :]).then_inc(d_y[dc].sem, 16)
                    d_y[dc].n += 16
                    y_val[dc] = d_y[dc].n
                if ss == 0:
                    for dc in range(4, 8):
                        xt_load(dc)

            if s_ + 1 < NS:
                smm_next = emit_smm(feats_next)

        for slot in range(8):
            g.wait_ge(d_y[slot].sem, d_y[slot].n)
    return nc


def _pack(w):
    O, I = w.shape
    wt = np.ascontiguousarray(w.T)
    arr = wt.reshape(I // 128, 128, O // 512, 512)
    return np.ascontiguousarray(arr.transpose(2, 1, 0, 3))


_NC_CACHE = {}


def kernel(L0_attn_x, L0_attn_input_weights, L0_attn_xis, L0_attn_proj_matrix,
           L0_ffn_w1_w, L0_ffn_w1_s, L0_ffn_w2_w, L0_ffn_w2_s,
           L0_ffn_h1_w, L0_ffn_h1_s, L0_ffn_h2_w, L0_ffn_h2_s,
           L0_ffn_w3_w, L0_ffn_w3_s):
    x = np.ascontiguousarray(np.asarray(L0_attn_x, dtype=np.float32)).reshape(-1, D)
    shared = {
        "projT": np.ascontiguousarray(np.asarray(L0_attn_proj_matrix, np.float32).T),
        "iwT": np.ascontiguousarray(np.asarray(L0_attn_input_weights, np.float32).T),
        "xis": np.ascontiguousarray(np.asarray(L0_attn_xis, np.float32)),
        "w1p": _pack(np.asarray(L0_ffn_w1_w, np.float32)),
        "w2p": _pack(np.asarray(L0_ffn_w2_w, np.float32)),
        "h1p": _pack(np.asarray(L0_ffn_h1_w, np.float32)),
        "h2p": _pack(np.asarray(L0_ffn_h2_w, np.float32)),
        "w3p": _pack(np.asarray(L0_ffn_w3_w, np.float32)),
        "w1s": np.ascontiguousarray(np.asarray(L0_ffn_w1_s, np.float32)),
        "w2s": np.ascontiguousarray(np.asarray(L0_ffn_w2_s, np.float32)),
        "h1s": np.ascontiguousarray(np.asarray(L0_ffn_h1_s, np.float32)),
        "h2s": np.ascontiguousarray(np.asarray(L0_ffn_h2_s, np.float32)),
        "w3s": np.ascontiguousarray(np.asarray(L0_ffn_w3_s, np.float32)),
    }
    if "nc" not in _NC_CACHE:
        _NC_CACHE["nc"] = build_nc()
    nc = _NC_CACHE["nc"]
    in_maps = []
    for c in range(8):
        xc = np.ascontiguousarray(x[c * T:(c + 1) * T])
        m = {"xT": np.ascontiguousarray(xc.T)}
        m.update(shared)
        in_maps.append(m)
    res = run_bass_kernel_spmd(nc, in_maps, core_ids=list(range(8)))
    out = np.concatenate([np.ascontiguousarray(res.results[c]["y"].T) for c in range(8)], axis=0)
    return out.reshape(4, 4096, D)


if __name__ == "__main__":
    nc = build_nc()
    print("built ok, instructions:", nc.n_instructions())
